# version 12
# baseline (speedup 1.0000x reference)
import numpy as np
import ml_dtypes
import concourse.bass as bass
import concourse.mybir as mybir
from concourse.bass_utils import run_bass_kernel_spmd

F32 = mybir.dt.float32
BF16 = mybir.dt.bfloat16
AF = mybir.ActivationFunctionType
ALU = mybir.AluOpType
AX = mybir.AxisListType
BF = ml_dtypes.bfloat16
FP8 = mybir.dt.float8e4

NCORES = 8
D = 1024
H = 16
DH = 64
TCORE = 2048
NCH = TCORE // 128
MAGIC = 12582912.0
LN_EPS = 1e-5

BLOCKS = ["l0a", "l0f", "l1a", "l1f"]
L0A_DEBUG = 0

ENGS = ("pe", "ve", "se", "gp", "sp")



class Op:
    __slots__ = ("eng", "fn", "deps", "sem", "n", "val", "needed", "label")

    def __init__(self, eng, fn, deps, sem, n, label):
        self.eng = eng
        self.fn = fn
        self.deps = deps
        self.sem = sem
        self.n = n
        self.val = None
        self.needed = False
        self.label = label


class Sched:
    def __init__(self, nc):
        self.nc = nc
        self.ops = []
        self.batch_sems = {"dms"}
        self.floor = {}

    def op(self, eng, fn, deps=(), sem=None, n=None, label=""):
        deps = [d for d in deps if d is not None]
        f = self.floor.get(eng)
        if f is not None:
            deps.append(f)
        if n is None:
            n = 16 if (sem is not None and sem[0] == "d") else 1
        o = Op(eng, fn, deps, sem if sem else eng, n, label)
        self.ops.append(o)
        return o

    def barrier(self):
        last = {}
        for o in self.ops:
            last[o.sem] = o
        toks = [o for o in last.values()]
        self.floor = {}
        for e in ENGS:
            self.floor[e] = self.op(e, _nop(), deps=toks, label="bar")

    def emit(self, block):
        for o in self.ops:
            if o.sem[0] == "d" or o.sem in self.batch_sems:
                o.needed = True
            for d in o.deps:
                d.needed = True
        cnt = {}
        for o in self.ops:
            if o.needed:
                cnt[o.sem] = cnt.get(o.sem, 0) + o.n
                o.val = cnt[o.sem]
        for name in self.batch_sems & set(cnt):
            total = cnt[name]
            for o in self.ops:
                if o.sem == name:
                    o.val = total
        sems = {name: self.nc.alloc_semaphore(f"s_{name}") for name in cnt}
        ops_by_eng = {e: [o for o in self.ops if o.eng == e] for e in ENGS}
        engmap = {"pe": block.tensor, "ve": block.vector, "se": block.scalar,
                  "gp": block.gpsimd, "sp": block.sync}

        def make_body(eng_name, eng_ops):
            def body(engine):
                waited = {}
                for o in eng_ops:
                    need = {}
                    for d in o.deps:
                        if need.get(d.sem, 0) < d.val:
                            need[d.sem] = d.val
                    for sm, v in need.items():
                        if waited.get(sm, 0) < v:
                            engine.wait_ge(sems[sm], v)
                            waited[sm] = v
                    inst = o.fn(engine)
                    if o.needed:
                        if inst is None:
                            raise RuntimeError(f"op {o.label} no inst")
                        inst.then_inc(sems[o.sem], o.n)
            return body

        for e in ENGS:
            if ops_by_eng[e]:
                engmap[e](make_body(e, ops_by_eng[e]))


class TBuf:

    def __init__(self, t):
        self.t = t
        self.deps = []

    def w(self):
        d = self.deps
        self.deps = []
        return d

    def add(self, *toks):
        for t in toks:
            if t is not None:
                self.deps.append(t)



def _dma(dst, src, slow=False):
    if slow:
        return lambda e: e.dma_start(out=dst, in_=src,
                                     allow_slow_non_contiguous=True)
    return lambda e: e.dma_start(out=dst, in_=src)


def _copy(dst, src):
    def f(e):
        if hasattr(e, "tensor_copy"):
            return e.tensor_copy(dst, src)
        return e.copy(dst, src)
    return f


def _act(dst, src, func, bias=0.0, scale=1.0, accum=None):
    def f(e):
        kw = {}
        if accum is not None:
            kw["accum_out"] = accum
        return e.activation(dst, src, func, bias=bias, scale=scale, **kw)
    return f


def _ts(dst, src, s1, s2, op0, op1=ALU.bypass):
    return lambda e: e.tensor_scalar(dst, src, s1, s2, op0, op1)


def _tt(dst, a, b, op):
    return lambda e: e.tensor_tensor(dst, a, b, op)


def _stt(dst, in0, scalar, in1, op0, op1):
    return lambda e: e.scalar_tensor_tensor(dst, in0, scalar, in1, op0, op1)


def _mm(out, lhsT, rhs, start, stop, tp=None):
    return lambda e: e.matmul(out, lhsT, rhs, start=start, stop=stop,
                              tile_position=tp)


def _tr(out, in_, ident):
    return lambda e: e.transpose(out, in_, ident)


def _red(dst, src, op, absv=False):
    return lambda e: e.tensor_reduce(dst, src, AX.X, op,
                                     apply_absolute_value=absv)


def _recip(dst, src):
    return lambda e: e.reciprocal(dst, src)


def _memset(dst, v):
    return lambda e: e.memset(dst, v)


def _scan_add(dst, src, zeros):
    return lambda e: e.tensor_tensor_scan(dst, src, zeros, 0.0, ALU.add,
                                          ALU.bypass)


def _cc_allgather(in_ap, out_ap):
    def f(e):
        return e.collective_compute(
            "AllGather", ALU.bypass,
            replica_groups=[list(range(NCORES))],
            ins=[in_ap], outs=[out_ap])
    return f


def _nop():
    return lambda e: e.nop()



class KB:
    def __init__(self, nc):
        self.nc = nc
        self.s = Sched(nc)
        self.sb = {}
        self.dram = {}
        self.wtok = {}
        self.scope = None
        self.small_sem = "dms"

    def sbuf(self, name, shape, dtype):
        if self.scope is not None:
            t = self.scope.enter_context(
                self.nc.sbuf_tensor(name, list(shape), dtype))
        else:
            t = self.nc.alloc_sbuf_tensor(name, list(shape), dtype)
        self.sb[name] = t
        return t

    def tbuf(self, name, shape, dtype):
        return TBuf(self.sbuf(name, shape, dtype))

    def dram_in(self, name, shape, dtype=F32):
        t = self.nc.dram_tensor(name, list(shape), dtype, kind="ExternalInput")
        self.dram[name] = t
        return t

    def dram_out(self, name, shape, dtype=F32):
        t = self.nc.dram_tensor(name, list(shape), dtype, kind="ExternalOutput")
        self.dram[name] = t
        return t

    def dram_tmp(self, name, shape, dtype=F32):
        t = self.nc.dram_tensor(name, list(shape), dtype, kind="Internal")
        self.dram[name] = t
        return t

    def inline(self, name, arr):
        return self.nc.inline_tensor(np.ascontiguousarray(arr), name=name)


class Common:
    def __init__(self, kb: KB):
        self.kb = kb
        nc = kb.nc
        self.ident_h = kb.inline("c_ident", np.eye(128, dtype=BF))
        self.ident = kb.sbuf("ident", [128, 128], BF16)
        self.ones8_h = kb.inline("c_ones8", np.ones((8, 1), dtype=BF))
        self.ones8 = kb.sbuf("ones8", [8, 1], BF16)
        self.onescol_h = kb.inline("c_ones128", np.ones((128, 1), dtype=BF))
        self.onescol = kb.sbuf("ones128", [128, 1], BF16)
        m = np.zeros((128, 64), dtype=BF)
        for p in range(128):
            m[p, p % 64 + 1:] = 1
        self.mwkv_s_h = kb.inline("c_mwkvs", m)
        self.mwkv_s = kb.sbuf("mwkvs", [128, 64], BF16)
        md = np.zeros((128, 64), dtype=BF)
        for p in range(128):
            md[p, p % 64] = 1
        self.mwkv_d_h = kb.inline("c_mwkvd", md)
        self.mwkv_d = kb.sbuf("mwkvd", [128, 64], BF16)
        ml = np.triu(np.ones((128, 128), dtype=BF), 0)
        self.ml1_h = kb.inline("c_ml1", ml)
        self.ml1 = kb.sbuf("ml1", [128, 128], BF16)
        msl = np.triu(np.ones((128, 128), dtype=BF), 1)
        self.msl2_h = kb.inline("c_msl2", np.concatenate([msl, msl], axis=1))
        self.msl2 = kb.sbuf("msl2", [128, 256], BF16)
        eye = np.eye(128, dtype=BF)
        self.idd2_h = kb.inline("c_idd2", np.concatenate([eye, eye], axis=1))
        self.idd2 = kb.sbuf("idd2", [128, 256], BF16)

        self.meta = kb.sbuf("meta", [128, 32], F32)
        self.l0f_qc = kb.tbuf("l0f_qc", [128, 16], F32)
        self.onesf_h = kb.inline("c_ones128f",
                                 np.ones((128, 128), dtype=np.float32))
        self.onesf = kb.sbuf("onesf", [128, 128], F32)

        self.x32 = [kb.tbuf(f"x32_{i}", [128, D], F32) for i in range(4)]
        self.xn = [kb.tbuf(f"xn_{i}", [128, D], BF16) for i in range(4)]
        self.xnf = [kb.tbuf(f"xnf_{k}", [128, 512], BF16) for k in range(8)]
        self.col = kb.tbuf("ln_cols", [128, 16], F32)
        self.col2 = kb.tbuf("ep_cols", [128, 16], F32)
        self.colg = kb.tbuf("g_cols", [1, 8], F32)
        sq0 = kb.tbuf("sqp0", [128, D], BF16)
        self.sqp = [sq0, sq0]
        self._sqi = 0
        self.zeros = kb.sbuf("z_zero", [128, 128], F32)
        ws0 = kb.tbuf("wstage0", [128, 512], F32)
        ws1 = kb.tbuf("wstage1", [128, 512], F32)
        self.wstage = [ws0, ws1]
        self.pb = [TBuf(nc.alloc_psum_tensor(f"pb{i}", [128, 512], F32))
                   for i in range(6)]
        self.pt = [TBuf(nc.alloc_psum_tensor(f"pt{i}", [128, 1024], BF16))
                   for i in range(2)]
        self._rr = 0
        self._rrt = 0

    def setup(self):
        s = self.kb.s
        t = None
        for src_h, dst in ((self.ident_h, self.ident), (self.ones8_h, self.ones8),
                           (self.onescol_h, self.onescol),
                           (self.mwkv_s_h, self.mwkv_s),
                           (self.mwkv_d_h, self.mwkv_d), (self.ml1_h, self.ml1),
                           (self.msl2_h, self.msl2), (self.idd2_h, self.idd2),
                           (self.onesf_h, self.onesf),
                           (self.kb.dram["meta_in"], self.meta)):
            t = s.op("sp", _dma(dst[:], src_h.ap()), sem="dms", label="const")
        self.tconst = t
        self.tz = s.op("ve", _memset(self.zeros[:], 0.0), label="zeros")
        self.epscol = self.kb.sbuf("epscol", [128, 1], F32)
        self.tepc = s.op("ve", _memset(self.epscol[:], LN_EPS), label="eps")

    def bank(self):
        b = self.pb[self._rr % len(self.pb)]
        self._rr += 1
        return b

    def tbank(self):
        b = self.pt[self._rrt % len(self.pt)]
        self._rrt += 1
        return b

    def sq(self):
        b = self.sqp[self._sqi % len(self.sqp)]
        self._sqi += 1
        return b

    def ln_load(self, src_dram, chunk, slot, src_deps=None):
        s = self.kb.s
        x32 = self.x32[slot]
        xn = self.xn[slot]
        col = self.col
        c0 = 4 * slot
        sqb = self.sq()
        tload = s.op("sp", _dma(x32.t[:],
                                src_dram.ap()[chunk * 128:(chunk + 1) * 128, :]),
                     deps=x32.w() + ([src_deps[chunk]] if src_deps else []),
                     sem=f"dxi{slot}", label=f"ld{chunk}")
        x32.add(tload)
        t1 = s.op("ve", _red(col.t[:, c0:c0 + 1], x32.t[:], ALU.add),
                  deps=[tload] + col.w())
        sqw = sqb.w()
        t2 = s.op("se", _act(sqb.t[:], x32.t[:], AF.Square,
                             accum=col.t[:, c0 + 1:c0 + 2]),
                  deps=[tload] + sqw, label="lnsq")
        sqb.add(t2)
        t3 = s.op("ve", _ts(col.t[:, c0:c0 + 1], col.t[:, c0:c0 + 1], 1.0 / D,
                            None, ALU.mult), deps=[t1])
        t4 = s.op("ve", _stt(col.t[:, c0 + 2:c0 + 3], col.t[:, c0:c0 + 1], 0.0,
                             col.t[:, c0:c0 + 1], ALU.bypass, ALU.mult),
                  deps=[t3])
        t5 = s.op("ve", _ts(col.t[:, c0 + 1:c0 + 2], col.t[:, c0 + 1:c0 + 2],
                            1.0 / D, None, ALU.mult), deps=[t2])
        t6 = s.op("ve", _tt(col.t[:, c0 + 2:c0 + 3], col.t[:, c0 + 1:c0 + 2],
                            col.t[:, c0 + 2:c0 + 3], ALU.subtract),
                  deps=[t4, t5])
        t7 = s.op("se", _act(col.t[:, c0 + 3:c0 + 4], col.t[:, c0 + 2:c0 + 3],
                             AF.Sqrt, bias=self.epscol[:, 0:1]),
                  deps=[t6, self.tepc])
        t8 = s.op("ve", _recip(col.t[:, c0 + 3:c0 + 4], col.t[:, c0 + 3:c0 + 4]),
                  deps=[t7])
        t9 = s.op("ve", _ts(xn.t[:], x32.t[:], col.t[:, c0:c0 + 1],
                            col.t[:, c0 + 3:c0 + 4], ALU.subtract, ALU.mult),
                  deps=[t8, t3] + xn.w(), label="lnapply")
        xn.add(t9)
        x32.add(t9)
        col.add(t9)
        return t9, tload

    def ln_load_f32(self, src_dram, chunk, slot, dstf, src_deps=None):
        s = self.kb.s
        x32 = self.x32[slot]
        col = self.col
        c0 = 4 * slot
        sqb = self.sq()
        tload = s.op("sp", _dma(x32.t[:],
                                src_dram.ap()[chunk * 128:(chunk + 1) * 128, :]),
                     deps=x32.w() + ([src_deps[chunk]] if src_deps else []),
                     sem=f"dxi{slot}", label=f"lq{chunk}")
        x32.add(tload)
        t1 = s.op("ve", _red(col.t[:, c0:c0 + 1], x32.t[:], ALU.add),
                  deps=[tload] + col.w())
        sqw = sqb.w()
        t2 = s.op("se", _act(sqb.t[:], x32.t[:], AF.Square,
                             accum=col.t[:, c0 + 1:c0 + 2]),
                  deps=[tload] + sqw, label="lnsq")
        sqb.add(t2)
        t3 = s.op("ve", _ts(col.t[:, c0:c0 + 1], col.t[:, c0:c0 + 1], 1.0 / D,
                            None, ALU.mult), deps=[t1])
        t4 = s.op("ve", _stt(col.t[:, c0 + 2:c0 + 3], col.t[:, c0:c0 + 1], 0.0,
                             col.t[:, c0:c0 + 1], ALU.bypass, ALU.mult),
                  deps=[t3])
        t5 = s.op("ve", _ts(col.t[:, c0 + 1:c0 + 2], col.t[:, c0 + 1:c0 + 2],
                            1.0 / D, None, ALU.mult), deps=[t2])
        t6 = s.op("ve", _tt(col.t[:, c0 + 2:c0 + 3], col.t[:, c0 + 1:c0 + 2],
                            col.t[:, c0 + 2:c0 + 3], ALU.subtract),
                  deps=[t4, t5])
        t7 = s.op("se", _act(col.t[:, c0 + 3:c0 + 4], col.t[:, c0 + 2:c0 + 3],
                             AF.Sqrt, bias=self.epscol[:, 0:1]),
                  deps=[t6, self.tepc])
        t8 = s.op("ve", _recip(col.t[:, c0 + 3:c0 + 4], col.t[:, c0 + 3:c0 + 4]),
                  deps=[t7])
        t9 = s.op("ve", _ts(dstf.t[:], x32.t[:], col.t[:, c0:c0 + 1],
                            col.t[:, c0 + 3:c0 + 4], ALU.subtract, ALU.mult),
                  deps=[t8, t3] + dstf.w(), label="lnappf")
        dstf.add(t9)
        x32.add(t9)
        col.add(t9)
        return t9, tload

    def xn_to_fm(self, slot, group_col, tok_xn):
        s = self.kb.s
        toks = []
        for k in range(8):
            tb = self.tbank()
            ps = tb.t
            tt_ = s.op("pe", _tr(ps[:, 0:128],
                                 self.xn[slot].t[:, k * 128:(k + 1) * 128],
                                 self.ident[:]),
                       deps=[tok_xn, self.tconst] + tb.w() + self.xnf[k].w(),
                       label=f"tr{k}")
            te = s.op("se", _copy(self.xnf[k].t[:, group_col:group_col + 128],
                                  ps[:, 0:128]), deps=[tt_], label=f"tre{k}")
            tb.add(te)
            self.xnf[k].add(te)
            self.xn[slot].add(tt_)
            toks.append(te)
        return toks

    def group_fm(self, src_dram, g, src_deps=None):
        fm_toks = []
        per_chunk = []
        for ci in range(4):
            chunk = g * 4 + ci
            t_xn, t_x = self.ln_load(src_dram, chunk, ci, src_deps=src_deps)
            fm_toks += self.xn_to_fm(ci, ci * 128, t_xn)
            per_chunk.append((t_x, t_xn, chunk))
        return fm_toks, per_chunk


def load_wt(kb, cm, dram_t, dst, rows, cols, slot=0, cast_eng="ve"):
    s = kb.s
    toks = list(dst.w())
    cc = min(cols, 512)
    i = slot
    first = {0: True, 1: True}
    blk = 0
    for k in range(rows // 128):
        for j in range(0, cols, cc):
            sl = i % 2
            stage = cm.wstage[sl]
            tl = s.op("sp" if sl == 0 else "gp",
                      _dma(stage.t[:, 0:cc],
                           dram_t.ap()[blk * 128:(blk + 1) * 128, 0:cc]),
                      deps=stage.w() + (toks if first[sl] else []),
                      sem=f"dwt{sl}", label="wld")
            first[sl] = False
            blk += 1
            stage.add(tl)
            tc = s.op(cast_eng, _copy(dst.t[:, k * cols + j:k * cols + j + cc],
                                      stage.t[:, 0:cc]), deps=[tl],
                      label="wcast")
            stage.add(tc)
            dst.add(tc)
            i += 1
    return dst.deps[-1]


def proj_tm(kb, cm, xbufs, xcol, wt, wtoks, xtoks, nchunks, out_cb,
            nout=D):
    s = kb.s
    toks = []
    for ci in range(nchunks):
        for gg in range(nout // 512):
            bank = cm.bank()
            ps = bank.t
            deps = bank.w() + list(xtoks) + list(wtoks)
            tmm = None
            for k in range(8):
                tmm = s.op("pe", _mm(ps[:, 0:512],
                                     xbufs[k].t[:, xcol + ci * 128:xcol + ci * 128 + 128],
                                     wt.t[:, k * nout + gg * 512:k * nout + gg * 512 + 512],
                                     start=(k == 0), stop=(k == 7)),
                           deps=deps if k == 0 else [], label=f"ptm{ci}.{gg}.{k}")
            tev = out_cb(ci, gg, ps[:, 0:512], [tmm])
            bank.add(tev)
            wt.add(tev)
            for b in xbufs:
                b.add(tev)
            toks.append(tev)
    return toks


def proj_fm(kb, cm, xbufs, xcol, glen, wt, wtoks, xtoks, out_cb, nout=D):
    s = kb.s
    toks = []
    for m in range(nout // 128):
        bank = cm.bank()
        ps = bank.t
        deps = bank.w() + list(xtoks) + list(wtoks)
        tmm = None
        for k in range(8):
            tmm = s.op("pe", _mm(ps[:, 0:glen],
                                 wt.t[:, k * nout + m * 128:k * nout + m * 128 + 128],
                                 xbufs[k].t[:, xcol:xcol + glen],
                                 start=(k == 0), stop=(k == 7)),
                       deps=deps if k == 0 else [], label=f"pfm{m}.{k}")
        tev = out_cb(m, ps[:, 0:glen], [tmm])
        bank.add(tev)
        wt.add(tev)
        for b in xbufs:
            b.add(tev)
        toks.append(tev)
    return toks


def epilogue_store(kb, cm, o_buf, o_toks, slot, dst_dram, chunk, xbuf=None):
    s = kb.s
    col = cm.col2
    x32 = cm.x32[slot] if xbuf is None else xbuf
    c0 = 4 * slot
    sqb = cm.sq()
    t1 = s.op("ve", _red(col.t[:, c0:c0 + 1], o_buf.t[:], ALU.add),
              deps=list(o_toks) + col.w())
    sqw = sqb.w()
    t2 = s.op("se", _act(sqb.t[:], o_buf.t[:], AF.Square,
                         accum=col.t[:, c0 + 1:c0 + 2]),
              deps=list(o_toks) + sqw, label="esq")
    sqb.add(t2)
    t3 = s.op("ve", _ts(col.t[:, c0:c0 + 1], col.t[:, c0:c0 + 1], 1.0 / D, None,
                        ALU.mult), deps=[t1])
    t4 = s.op("ve", _stt(col.t[:, c0 + 2:c0 + 3], col.t[:, c0:c0 + 1], 0.0,
                         col.t[:, c0:c0 + 1], ALU.bypass, ALU.mult), deps=[t3])
    t5 = s.op("ve", _ts(col.t[:, c0 + 1:c0 + 2], col.t[:, c0 + 1:c0 + 2],
                        1.0 / D, None, ALU.mult), deps=[t2])
    t6 = s.op("ve", _tt(col.t[:, c0 + 2:c0 + 3], col.t[:, c0 + 1:c0 + 2],
                        col.t[:, c0 + 2:c0 + 3], ALU.subtract), deps=[t4, t5])
    t7 = s.op("se", _act(col.t[:, c0 + 3:c0 + 4], col.t[:, c0 + 2:c0 + 3],
                         AF.Sqrt, bias=cm.epscol[:, 0:1]), deps=[t6, cm.tepc])
    t8 = s.op("ve", _recip(col.t[:, c0 + 3:c0 + 4], col.t[:, c0 + 3:c0 + 4]),
              deps=[t7])
    t9 = s.op("ve", _ts(o_buf.t[:], o_buf.t[:], col.t[:, c0:c0 + 1],
                        col.t[:, c0 + 3:c0 + 4], ALU.subtract, ALU.mult),
              deps=[t8, t3, t2], label="eln")
    o_buf.add(t9)
    t10 = s.op("ve", _tt(x32.t[:], x32.t[:], o_buf.t[:], ALU.add),
               deps=[t9] + (x32.deps[-1:] if xbuf is not None else []),
               label="resid")
    o_buf.add(t10)
    x32.add(t10)
    col.add(t9)
    tst = s.op("sp", _dma(dst_dram.ap()[chunk * 128:(chunk + 1) * 128, :],
                          x32.t[:]), deps=[t10], sem=f"dxo{slot}",
               label=f"st{chunk}")
    x32.add(tst)
    return tst, t10


def gather_prev_row_fm(kb, cm, src_dram, hx, idx, src_deps=None):
    s = kb.s
    row = kb.tbuf(f"g{idx}_row", [1, D], F32)
    rowb = kb.tbuf(f"g{idx}_rowb", [8, D], F32)
    rowb16 = kb.tbuf(f"g{idx}_rowb16", [8, D], BF16)
    rowo = kb.tbuf(f"g{idx}_rowo", [1, D], BF16)
    sqg = kb.tbuf(f"g{idx}_sq", [1, D], F32)
    cin = kb.dram_tmp(f"g{idx}_in", [1, D])
    cout = kb.dram_tmp(f"g{idx}_out", [8, D])
    tmpd = kb.dram_tmp(f"g{idx}_tmp", [1, D], BF16)
    col = cm.colg

    tl = s.op("sp", _dma(row.t[:], src_dram.ap()[TCORE - 1:TCORE, :]),
              deps=row.w() + ([src_deps[-1]] if src_deps else []),
              sem=f"dg{idx}", label="grow")
    row.add(tl)
    t1 = s.op("ve", _red(col.t[:, 0:1], row.t[:], ALU.add),
              deps=[tl] + col.w())
    t2 = s.op("se", _act(sqg.t[:], row.t[:], AF.Square, accum=col.t[:, 1:2]),
              deps=[tl], label="gsq")
    t3 = s.op("ve", _ts(col.t[:, 0:1], col.t[:, 0:1], 1.0 / D, None, ALU.mult),
              deps=[t1])
    t4 = s.op("ve", _stt(col.t[:, 2:3], col.t[:, 0:1], 0.0, col.t[:, 0:1],
                         ALU.bypass, ALU.mult), deps=[t3])
    t5 = s.op("ve", _ts(col.t[:, 1:2], col.t[:, 1:2], 1.0 / D, None, ALU.mult),
              deps=[t2])
    t6 = s.op("ve", _tt(col.t[:, 2:3], col.t[:, 1:2], col.t[:, 2:3],
                        ALU.subtract), deps=[t4, t5])
    t7 = s.op("se", _act(col.t[:, 3:4], col.t[:, 2:3], AF.Sqrt,
                         bias=cm.epscol[0:1, 0:1]), deps=[t6, cm.tepc])
    t8 = s.op("ve", _recip(col.t[:, 3:4], col.t[:, 3:4]), deps=[t7])
    t9 = s.op("ve", _ts(row.t[:], row.t[:], col.t[:, 0:1], col.t[:, 3:4],
                        ALU.subtract, ALU.mult), deps=[t8, t3, t2])
    row.add(t9)
    col.add(t9)
    ta = s.op("sp", _dma(cin.ap(), row.t[:]), deps=[t9], sem=f"dg{idx}")
    row.add(ta)
    tc = s.op("gp", _cc_allgather(cin.ap().opt(), cout.ap().opt()),
              deps=[ta], sem=f"cc{idx}", n=1, label=f"ag{idx}")
    tb = s.op("sp", _dma(rowb.t[:], cout.ap()), deps=[tc] + rowb.w(),
              sem=f"dg{idx}")
    rowb.add(tb)
    tsl = s.op("ve", _ts(rowb.t[:], rowb.t[:], cm.meta[0:8, 8:9], None,
                         ALU.mult), deps=[tb, cm.tconst], label="gsel")
    rowb.add(tsl)
    tcast = s.op("ve", _copy(rowb16.t[:], rowb.t[:]),
                 deps=[tsl] + rowb16.w(), label="gcast")
    rowb16.add(tcast)
    evs = []
    for half in range(2):
        bank = cm.bank()
        ps = bank.t
        tm_ = s.op("pe", _mm(ps[0:1, 0:512], cm.onescol[0:8, :],
                             rowb16.t[0:8, half * 512:(half + 1) * 512],
                             start=True, stop=True),
                   deps=[tcast, cm.tconst] + bank.w())
        tev = s.op("se", _copy(rowo.t[0:1, half * 512:(half + 1) * 512],
                               ps[0:1, 0:512]),
                   deps=[tm_] + (rowo.w() if half == 0 else []))
        bank.add(tev)
        rowo.add(tev)
        evs.append(tev)
    td = s.op("sp", _dma(tmpd.ap(), rowo.t[:]), deps=evs, sem=f"dg{idx}")
    rowo.add(td)
    te = s.op("sp", _dma(hx.t[:, 0:8],
                         tmpd.ap().rearrange("a (k p) -> (a p) k", p=128),
                         slow=True),
              deps=[td] + hx.w(), sem=f"dg{idx}", label="ghx")
    hx.add(te)
    return te



def prep_l1f(kb, cm):
    wr = kb.tbuf("l1f_wr", [128, 8 * D], BF16)
    wk = kb.tbuf("l1f_wk", [128, 8 * D], BF16)
    wv = kb.tbuf("l1f_wv", [128, 8 * D], BF16)
    kb.sbuf("l1f_mu_sb", [128, 16], F32)
    kb.wtok["l1f_wr"] = (wr, load_wt(kb, cm, kb.dram["l1f_wrT"], wr, D, D,
                                     slot=0))
    kb.wtok["l1f_wk"] = (wk, load_wt(kb, cm, kb.dram["l1f_wkT"], wk, D, D,
                                     slot=1))
    kb.wtok["l1f_wv"] = (wv, load_wt(kb, cm, kb.dram["l1f_wvT"], wv, D, D,
                                     slot=0))
    kb.wtok["l1f_mu"] = kb.s.op("sp", _dma(kb.sb["l1f_mu_sb"][:],
                                           kb.dram["l1f_mu"].ap()),
                                sem=kb.small_sem, label="mu")


def emit_l1f(kb, cm, src_dram, dst_dram, src_deps=None):
    s = kb.s
    wr, twr = kb.wtok["l1f_wr"]
    wk, twk = kb.wtok["l1f_wk"]
    wv, twv = kb.wtok["l1f_wv"]
    mu = kb.sb["l1f_mu_sb"]
    tmu = kb.wtok["l1f_mu"]
    xr = [kb.tbuf(f"l1f_xr{k}", [128, 512], BF16) for k in range(8)]
    xk = [kb.tbuf(f"l1f_xk{k}", [128, 512], BF16) for k in range(8)]
    k2 = [kb.tbuf(f"l1f_k2{k}", [128, 512], BF16) for k in range(8)]
    rsb = [kb.tbuf(f"l1f_r{i}", [128, D], BF16) for i in range(4)]
    dti = [kb.tbuf(f"l1f_d{i}", [128, 512], F32) for i in range(2)]
    oti = [kb.tbuf(f"l1f_o{i}", [128, D], F32) for i in range(4)]
    krl = [kb.tbuf(f"l1f_kr{i}", [128, 512], F32) for i in range(2)]
    hx = kb.tbuf("l1f_hx", [128, 8], BF16)
    pl = kb.tbuf("l1f_pl", [128, 8], BF16)

    thx = gather_prev_row_fm(kb, cm, src_dram, hx, idx=3, src_deps=src_deps)

    store_toks = []
    for g in range(4):
        fm_toks, per_chunk = cm.group_fm(src_dram, g, src_deps=src_deps)
        mix_r, mix_k = [], []
        for k in range(8):
            xf = cm.xnf[k]
            dt_ = dti[k % 2]
            if g == 0:
                pcol, pctok = hx.t[:, k:k + 1], thx
                hxread = hx
            else:
                pcol, pctok = pl.t[:, k:k + 1], pl.deps[-1] if pl.deps else None
                hxread = pl
            ta = s.op("ve", _tt(dt_.t[:, 1:512], xf.t[:, 1:512],
                                xf.t[:, 0:511], ALU.subtract),
                      deps=fm_toks + dt_.w(), label=f"mxd{k}")
            dt_.add(ta)
            tb = s.op("ve", _tt(dt_.t[:, 0:1], xf.t[:, 0:1], pcol,
                                ALU.subtract), deps=[pctok], label=f"mxd0{k}")
            dt_.add(tb)
            hxread.add(tb)
            t1 = s.op("ve", _stt(xr[k].t[:, 1:512], dt_.t[:, 1:512],
                                 mu[:, k:k + 1], xf.t[:, 0:511], ALU.mult,
                                 ALU.add), deps=[ta, tmu] + xr[k].w(),
                      label=f"mxr{k}")
            t1b = s.op("ve", _stt(xr[k].t[:, 0:1], dt_.t[:, 0:1],
                                  mu[:, k:k + 1], pcol, ALU.mult, ALU.add),
                       deps=[tb])
            t2 = s.op("ve", _stt(xk[k].t[:, 1:512], dt_.t[:, 1:512],
                                 mu[:, 8 + k:9 + k], xf.t[:, 0:511], ALU.mult,
                                 ALU.add), deps=[ta] + xk[k].w(),
                      label=f"mxk{k}")
            t2b = s.op("ve", _stt(xk[k].t[:, 0:1], dt_.t[:, 0:1],
                                  mu[:, 8 + k:9 + k], pcol, ALU.mult, ALU.add),
                       deps=[tb])
            xr[k].add(t1, t1b)
            xk[k].add(t2, t2b)
            hxread.add(t1b, t2b)
            dt_.add(t1, t1b, t2, t2b)
            xf.add(ta, tb, t1, t2)
            mix_r += [t1, t1b]
            mix_k += [t2, t2b]
        if g < 3:
            for k in range(8):
                tpl = s.op("ve", _copy(pl.t[:, k:k + 1],
                                       cm.xnf[k].t[:, 511:512]),
                           deps=pl.w() + [fm_toks[-1]], label=f"pl{k}")
                pl.add(tpl)
                cm.xnf[k].add(tpl)

        def r_cb(ci, gg, ps, deps):
            t = s.op("se", _act(rsb[ci].t[:, gg * 512:(gg + 1) * 512], ps,
                                AF.Sigmoid),
                     deps=deps + (rsb[ci].w() if gg == 0 else []), label="rsig")
            rsb[ci].add(t)
            return t
        r_toks = proj_tm(kb, cm, xr, 0, wr, [twr], mix_r, 4, r_cb)

        k2_toks = []

        def k_cb(m, ps, deps):
            kb_ = krl[m % 2]
            trl = s.op("se", _act(kb_.t[:, 0:512], ps, AF.Relu),
                       deps=deps + kb_.w(), label="krelu")
            kb_.add(trl)
            tq = s.op("ve", _tt(k2[m].t[:, 0:512], kb_.t[:, 0:512],
                                kb_.t[:, 0:512], ALU.mult),
                      deps=[trl] + k2[m].w(), label="ksq")
            kb_.add(tq)
            k2[m].add(tq)
            k2_toks.append(tq)
            return tq
        proj_fm(kb, cm, xk, 0, 512, wk, [twk], mix_k, k_cb)

        def v_cb(ci, gg, ps, deps):
            ob = oti[ci]
            t = s.op("ve", _tt(ob.t[:, gg * 512:(gg + 1) * 512],
                               rsb[ci].t[:, gg * 512:(gg + 1) * 512], ps,
                               ALU.mult),
                     deps=deps + [r_toks[ci * 2 + gg]] +
                          (ob.w() if gg == 0 else []), label="vmul")
            ob.add(t)
            rsb[ci].add(t)
            return t
        v_toks = proj_tm(kb, cm, k2, 0, wv, [twv], k2_toks, 4, v_cb)

        for ci in range(4):
            _x_tok, _xn_tok, chunk = per_chunk[ci]
            ob = oti[ci]
            tst, _t10 = epilogue_store(kb, cm, ob, [v_toks[ci * 2],
                                                    v_toks[ci * 2 + 1]],
                                       ci, dst_dram, chunk)
            store_toks.append(tst)
    return store_toks





def prep_l1a(kb, cm):
    wA = kb.tbuf("l1a_wA", [128, 8 * D], BF16)
    wB = kb.tbuf("l1a_wB", [128, 8 * D], BF16)
    kb.sbuf("l1a_bqk_sb", [128, 16], F32)
    kb.sbuf("l1a_bvo_sb", [128, 2 * D], F32)
    tk = load_wt(kb, cm, kb.dram["l1a_wkT"], wA, D, D, slot=0)
    tv = load_wt(kb, cm, kb.dram["l1a_wvT"], wB, D, D, slot=1)
    kb.wtok["l1a_w"] = (wA, tk, wB, tv)
    kb.wtok["l1a_bqk"] = kb.s.op("sp", _dma(kb.sb["l1a_bqk_sb"][:],
                                            kb.dram["l1a_bqk"].ap()),
                                 sem=kb.small_sem)
    kb.wtok["l1a_bvo"] = kb.s.op("sp", _dma(kb.sb["l1a_bvo_sb"][:],
                                            kb.dram["l1a_bvo"].ap()),
                                 sem=kb.small_sem)


def phi_evac(kb, cm, scr, dst_ap, ps_ap, bias_col, deps, war):
    s = kb.s
    t1 = s.op("ve", _ts(scr.t[:, 0:512], ps_ap, bias_col, None, ALU.add),
              deps=deps + scr.w(), label="phz")
    t2 = s.op("ve", _ts(scr.t[:, 512:1024], ps_ap, bias_col, 0.0, ALU.add,
                        ALU.min), deps=deps, label="phm")
    t3 = s.op("se", _act(scr.t[:, 512:1024], scr.t[:, 512:1024], AF.Exp),
              deps=[t2], label="phe")
    t4 = s.op("ve", _stt(dst_ap, scr.t[:, 0:512], 0.0, scr.t[:, 512:1024],
                         ALU.max, ALU.add), deps=[t1, t3] + war, label="phf")
    scr.add(t3, t4)
    return t4


def transpose_128(kb, cm, src_buf, src_col, dst_buf, dst_col, n, src_toks,
                  eng="se"):
    s = kb.s
    toks = []
    for i in range(n):
        tb = cm.tbank()
        ps = tb.t
        tt_ = s.op("pe", _tr(ps[:, 0:128],
                             src_buf.t[:, src_col + i * 128:src_col + (i + 1) * 128],
                             cm.ident[:]),
                   deps=list(src_toks) + [cm.tconst] + tb.w() +
                        (dst_buf.w() if i == 0 else []), label="tr")
        te = s.op(eng, _copy(dst_buf.t[:, dst_col + i * 128:dst_col + (i + 1) * 128],
                             ps[:, 0:128]), deps=[tt_], label="tre")
        tb.add(te)
        dst_buf.add(te)
        src_buf.add(tt_)
        toks.append(te)
    return toks


def emit_l1a(kb, cm, src_dram, dst_dram, src_deps=None):
    s = kb.s
    wA, twk, wB, twv = kb.wtok["l1a_w"]
    bqk, tbqk = kb.sb["l1a_bqk_sb"], kb.wtok["l1a_bqk"]
    bvo, tbvo = kb.sb["l1a_bvo_sb"], kb.wtok["l1a_bvo"]
    kfm = [kb.tbuf(f"l1a_k{m}", [128, TCORE], BF16) for m in range(8)]
    vtm = [kb.tbuf(f"l1a_v{c}", [128, D], BF16) for c in range(NCH)]
    ktm = [kb.tbuf(f"l1a_kt{i}", [128, D], BF16) for i in range(2)]
    qfm = [kb.tbuf(f"l1a_q{m}", [128, 512], BF16) for m in range(8)]
    S = kb.tbuf("l1a_S", [128, 8 * 65], F32)
    Sbf = kb.tbuf("l1a_Sbf", [128, 8 * 65], BF16)
    Sg = kb.tbuf("l1a_Sg", [128, 520], F32)
    scr = [kb.tbuf(f"l1a_scr{i}", [128, 1024], F32) for i in range(2)]
    scb = [kb.tbuf(f"l1a_sc{i}", [128, 128], BF16) for i in range(2)]
    otm = [kb.tbuf(f"l1a_ot{i}", [128, D], BF16) for i in range(2)]
    ofm = [kb.tbuf(f"l1a_of{i}", [128, 8 * 128], BF16) for i in range(2)]
    oti = [kb.tbuf(f"l1a_o{i}", [128, D], F32) for i in range(2)]
    rcol = kb.tbuf("l1a_rc", [128, 4], F32)
    ccin = kb.dram_tmp("l1a_ccin", [128, 520])
    ccout = kb.dram_tmp("l1a_ccout", [NCORES * 128, 520])

    tS0 = s.op("ve", _memset(S.t[:], 0.0), deps=S.w(), label="S0")
    S.add(tS0)

    def state_update(c, kt, kttoks, vdeps, extra):
        for p in range(8):
            bank = cm.bank()
            ps = bank.t
            bd = bank.w()
            m1 = s.op("pe", _mm(ps[0:64, 0:64],
                                kt.t[:, p * 128:p * 128 + 64],
                                vtm[c].t[:, p * 128:p * 128 + 64],
                                start=True, stop=True, tp=(0, 0)),
                      deps=list(kttoks) + list(vdeps) + bd + list(extra),
                      label="sA")
            m2 = s.op("pe", _mm(ps[64:128, 0:64],
                                kt.t[:, p * 128 + 64:p * 128 + 128],
                                vtm[c].t[:, p * 128 + 64:p * 128 + 128],
                                start=True, stop=True, tp=(0, 64)),
                      deps=[], label="sB")
            tu = s.op("ve", _tt(S.t[:, p * 65:p * 65 + 64],
                                S.t[:, p * 65:p * 65 + 64], ps[:, 0:64],
                                ALU.add), deps=[m1, m2] + list(extra) +
                      ([S.deps[-1]] if S.deps else []), label="Sup")
            bank.add(tu)
            S.add(tu)
            kt.add(m1, m2)
            vtm[c].add(m1, m2)
        for m in range(8):
            tz1 = s.op("ve", _red(rcol.t[:, 0:1],
                                  kfm[m].t[:, c * 128:(c + 1) * 128],
                                  ALU.add), deps=rcol.w() +
                       ([kfm[m].deps[-1]] if kfm[m].deps else []), label="zr")
            tz2 = s.op("ve", _tt(S.t[:, m * 65 + 64:m * 65 + 65],
                                 S.t[:, m * 65 + 64:m * 65 + 65],
                                 rcol.t[:, 0:1], ALU.add), deps=[tz1],
                       label="zup")
            rcol.add(tz2)
            S.add(tz2)
            kfm[m].add(tz1)

    for g in range(4):
        fm_toks, per_chunk = cm.group_fm(src_dram, g, src_deps=src_deps)
        kevs = []

        def k_cb(m, ps, deps):
            t = phi_evac(kb, cm, scr[m % 2],
                         kfm[m].t[:, g * 512:(g + 1) * 512], ps,
                         bqk[:, 8 + m:9 + m], deps + [tbqk], kfm[m].w())
            kfm[m].add(t)
            kevs.append(t)
            return t
        proj_fm(kb, cm, cm.xnf, 0, 512, wA, [twk], fm_toks, k_cb)

        vevs = {}

        def v_cb(ci, gg, ps, deps):
            vb = vtm[g * 4 + ci]
            t = s.op("ve", _tt(vb.t[:, gg * 512:(gg + 1) * 512], ps,
                               bvo[:, gg * 512:(gg + 1) * 512], ALU.add),
                     deps=deps + [tbvo] + (vb.w() if gg == 0 else []),
                     label="vbias")
            vb.add(t)
            vevs.setdefault(ci, []).append(t)
            return t
        proj_tm(kb, cm, cm.xnf, 0, wB, [twv], fm_toks, 4, v_cb)

        for ci in range(4):
            c = g * 4 + ci
            kt = ktm[ci % 2]
            kts = []
            for m in range(8):
                kts += transpose_128(kb, cm, kfm[m], c * 128, kt, m * 128, 1,
                                     [kevs[m]])
            state_update(c, kt, kts[-1:], vevs[ci], [])

    slast = S.deps[-1]
    tci = s.op("sp", _dma(ccin.ap(), S.t[:, 0:520]), deps=[slast], sem="dga")
    S.add(tci)
    tcc = s.op("gp", _cc_allgather(ccin.ap().opt(), ccout.ap().opt()),
               deps=[tci], sem="cca", n=1, label="agS")
    twq = load_wt(kb, cm, kb.dram["l1a_wqT"], wA, D, D, slot=0)
    two = load_wt(kb, cm, kb.dram["l1a_woT"], wB, D, D, slot=1)
    tprev = None
    for c in range(NCORES):
        tg = s.op("sp", _dma(Sg.t[:, 0:520],
                             ccout.ap()[c * 128:(c + 1) * 128, :]),
                  deps=[tcc] + Sg.w(), sem="dga", label="sgl")
        Sg.add(tg)
        if c == 0:
            tco = s.op("ve", _ts(S.t[:, 0:520], Sg.t[:, 0:520],
                                 cm.meta[:, 0:1], None, ALU.mult),
                       deps=[tg, cm.tconst] + S.w(), label="comb0")
        else:
            tco = s.op("ve", _stt(S.t[:, 0:520], Sg.t[:, 0:520],
                                  cm.meta[:, c:c + 1], S.t[:, 0:520],
                                  ALU.mult, ALU.add), deps=[tg, tprev],
                       label=f"comb{c}")
        Sg.add(tco)
        S.add(tco)
        tprev = tco
    tsb = s.op("ve", _copy(Sbf.t[:], S.t[:]), deps=[tprev] + Sbf.w(),
               label="Scast")
    Sbf.add(tsb)
    S.add(tsb)

    store_toks = []
    for g in range(4):
        fm_toks, per_chunk = cm.group_fm(src_dram, g, src_deps=src_deps)
        qevs = []

        def q_cb(m, ps, deps):
            t = phi_evac(kb, cm, scr[m % 2], qfm[m].t[:, 0:512], ps,
                         bqk[:, m:m + 1], deps + [tbqk], qfm[m].w())
            qfm[m].add(t)
            qevs.append(t)
            return t
        proj_fm(kb, cm, cm.xnf, 0, 512, wA, [twq], fm_toks, q_cb)

        for ci in range(4):
            c = g * 4 + ci
            ob = oti[ci % 2]
            otb = otm[ci % 2]
            ofb = ofm[ci % 2]
            out_evs = []
            ot_war = otb.w()
            for h in range(16):
                m, part = h // 2, (h % 2) * 64
                bsc = cm.bank()
                pssc = bsc.t
                tm1 = s.op("pe", _mm(pssc[:, 0:128],
                                     kfm[m].t[part:part + 64,
                                              c * 128:(c + 1) * 128],
                                     qfm[m].t[part:part + 64,
                                              ci * 128:(ci + 1) * 128],
                                     start=True, stop=True),
                           deps=[qevs[m]] + bsc.w() +
                                ([kfm[m].deps[-1]] if kfm[m].deps else []),
                           label="scm")
                kfm[m].add(tm1)
                qfm[m].add(tm1)
                sb_ = scb[h % 2]
                tev = s.op("ve", _stt(sb_.t[:], pssc[:, 0:128], 1.0,
                                      cm.ml1[:], ALU.bypass, ALU.mult),
                           deps=[tm1, cm.tconst] + sb_.w(), label="scev")
                bsc.add(tev)
                sb_.add(tev)
                bnd = cm.bank()
                psnd = bnd.t
                bndw = bnd.w()
                n1 = s.op("pe", _mm(psnd[:, 0:64], sb_.t[:],
                                    vtm[c].t[:, h * 64:(h + 1) * 64],
                                    start=True, stop=False),
                          deps=[tev] + bndw, label="num")
                n2 = s.op("pe", _mm(psnd[:, 64:65], sb_.t[:],
                                    cm.onescol[:], start=False, stop=False),
                          deps=[cm.tconst], label="den")
                n3 = s.op("pe", _mm(psnd[:, 0:65],
                                    qfm[m].t[part:part + 64,
                                             ci * 128:(ci + 1) * 128],
                                    Sbf.t[part:part + 64, m * 65:(m + 1) * 65],
                                    start=False, stop=True),
                          deps=[Sbf.deps[-1]] if Sbf.deps else [],
                          label="inter")
                Sbf.add(n3)
                sb_.add(n1, n2)
                vtm[c].add(n1)
                tr1 = s.op("ve", _recip(rcol.t[:, 1:2], psnd[:, 64:65]),
                           deps=[n3] + rcol.w(), label="rec")
                tr2 = s.op("ve", _ts(otb.t[:, h * 64:(h + 1) * 64],
                                     psnd[:, 0:64], rcol.t[:, 1:2], None,
                                     ALU.mult),
                           deps=[tr1] + (ot_war if h == 0 else []),
                           label="oev")
                bnd.add(tr2)
                rcol.add(tr2)
                otb.add(tr2)
                out_evs.append(tr2)
            kt = ktm[ci % 2]
            kts = []
            for m in range(8):
                kts += transpose_128(kb, cm, kfm[m], c * 128, kt, m * 128, 1,
                                     [kfm[m].deps[-1]] if kfm[m].deps else [])
            state_update(c, kt, kts[-1:], [vtm[c].deps[-1]] if vtm[c].deps
                         else [], out_evs[-1:])
            tsb2 = s.op("ve", _copy(Sbf.t[:], S.t[:]),
                        deps=[S.deps[-1]] + Sbf.w(), label="Scast2")
            Sbf.add(tsb2)
            S.add(tsb2)
            of_toks = transpose_128(kb, cm, otb, 0, ofb, 0, 8, out_evs[-1:])
            o_toks = []
            obw = ob.w()
            for gg in range(2):
                bank = cm.bank()
                ps = bank.t
                deps = bank.w() + of_toks + [two]
                tmm = None
                for kkm in range(8):
                    tmm = s.op("pe", _mm(ps[:, 0:512],
                                         ofb.t[:, kkm * 128:(kkm + 1) * 128],
                                         wB.t[:, kkm * D + gg * 512:
                                              kkm * D + gg * 512 + 512],
                                         start=(kkm == 0), stop=(kkm == 7)),
                               deps=deps if kkm == 0 else [], label="wo")
                wB.add(tmm)
                for bfx in [ofb]:
                    bfx.add(tmm)
                tev = s.op("ve", _tt(ob.t[:, gg * 512:(gg + 1) * 512],
                                     ps[:, 0:512],
                                     bvo[:, D + gg * 512:D + gg * 512 + 512],
                                     ALU.add),
                           deps=[tmm, tbvo] + (obw if gg == 0 else []),
                           label="obias")
                bank.add(tev)
                ob.add(tev)
                o_toks.append(tev)
            tst, _ = epilogue_store(kb, cm, ob, o_toks, ci, dst_dram, c)
            store_toks.append(tst)
    return store_toks





def _quant_pass(kb, cm, dram_t, dst, rows, cols, dstmap, s_col, sinv_col,
                tscale, slot, jouter=False):
    s = kb.s
    i = slot
    first = {0: True, 1: True}
    tokmap = {}
    nj = cols // 512
    if jouter:
        order = [(k, j) for j in range(0, cols, 512)
                 for k in range(rows // 128)]
    else:
        order = [(k, j) for k in range(rows // 128)
                 for j in range(0, cols, 512)]
    for k, j in order:
            blk = k * nj + j // 512
            sl = i % 2
            stage = cm.wstage[sl]
            tl = s.op("sp" if sl == 0 else "gp",
                      _dma(stage.t[:, 0:512],
                           dram_t.ap()[blk * 128:(blk + 1) * 128, 0:512]),
                      deps=stage.w(), sem=f"dwt{sl}", label="qld")
            stage.add(tl)
            t1 = s.op("ve", _ts(stage.t[:, 0:512], stage.t[:, 0:512], s_col,
                                MAGIC, ALU.mult, ALU.add),
                      deps=[tl, tscale], label="qr1")
            stage.add(t1)
            t2 = s.op("ve", _ts(stage.t[:, 0:512], stage.t[:, 0:512],
                                MAGIC, 1.0, ALU.subtract, ALU.min),
                      deps=[t1], label="qr2")
            stage.add(t2)
            t3 = s.op("ve", _ts(dst.t[:, dstmap(k, j):dstmap(k, j) + 512],
                                stage.t[:, 0:512], -1.0, None, ALU.max),
                      deps=[t2] + (dst.w() if first[sl] else []),
                      label="qr3")
            first[sl] = False
            stage.add(t3)
            dst.add(t3)
            tokmap[(k, j)] = t3
            i += 1
    return dst.deps[-1], tokmap


def make_l0f_pass1(kb, cm):
    s = kb.s
    qc = cm.l0f_qc
    tz = s.op("ve", _memset(qc.t[:], 0.0), deps=qc.w(), label="qcz")
    qc.add(tz)

    def gen():
        for nm, nblk, acc_col, tmpbase in (("l0f_w1T", 64, 0, 2),
                                           ("l0f_w2T", 64, 1, 7)):
            for blk in range(nblk):
                yield kb.dram[nm], blk, acc_col, tmpbase
    it = gen()
    st = {"i": 0, "done": False}
    last_tacc = {0: None, 1: None}
    pend = []

    def do_abs(item):
        sl, acc_col, tmpbase, tl = item
        tc_ = tmpbase + 8 * sl
        ta = s.op("se", _act(cm.sqp[sl].t[:, 0:512],
                             cm.wstage[sl].t[:, 0:512], AF.Abs,
                             accum=qc.t[:, tc_:tc_ + 1]),
                  deps=[tl, tz] + cm.sqp[sl].w() +
                       ([last_tacc[sl]] if last_tacc[sl] else []),
                  label="aab")
        cm.sqp[sl].add(ta)
        cm.wstage[sl].add(ta)
        tacc = s.op("ve", _tt(qc.t[:, acc_col:acc_col + 1],
                              qc.t[:, acc_col:acc_col + 1],
                              qc.t[:, tc_:tc_ + 1],
                              ALU.add), deps=[ta, tz], label="aacc")
        qc.add(tacc)
        last_tacc[sl] = tacc

    def pump(n):
        for _ in range(n):
            try:
                dram_t, blk, acc_col, tmpbase = next(it)
            except StopIteration:
                st["done"] = True
                return
            if len(pend) >= 2:
                do_abs(pend.pop(0))
            sl = st["i"] % 2
            st["i"] += 1
            stage = cm.wstage[sl]
            tl = s.op("gp",
                      _dma(stage.t[:, 0:512],
                           dram_t.ap()[blk * 128:(blk + 1) * 128, 0:512]),
                      deps=stage.w(), sem=f"dwt{sl}", label="ald")
            stage.add(tl)
            pend.append((sl, acc_col, tmpbase, tl))

    def drain():
        while pend:
            do_abs(pend.pop(0))

    def finish():
        while not st["done"]:
            pump(16)
        drain()
        return qc.deps[-1]
    return pump, drain, finish


def prep_l0f(kb, cm, pass1_finish=None):
    s = kb.s
    w1h = kb.tbuf("l0f_w1h", [128, 8 * 4096], FP8)
    w2h = kb.tbuf("l0f_w2h", [128, 32 * 1024], FP8)
    cwc = kb.sbuf("l0f_cw_sb", [128, 128], F32)
    onesf = cm.onesf
    tof = cm.tconst
    qc = cm.l0f_qc
    tcw = s.op("sp", _dma(cwc[:], kb.dram["l0f_cw"].ap()), sem=kb.small_sem)

    if pass1_finish is None:
        _, _, pass1_finish = make_l0f_pass1(kb, cm)
    tabs = pass1_finish()

    def scales(acc_col, mean_col, s_coli, nelem, dep):
        bank = cm.bank()
        ps = bank.t
        tm_ = s.op("pe", _mm(ps[:, 0:1], onesf[:],
                             qc.t[:, acc_col:acc_col + 1], start=True,
                             stop=True), deps=[dep, tof] + bank.w(),
                   label="qfold")
        qc.add(tm_)
        t1 = s.op("ve", _ts(qc.t[:, mean_col:mean_col + 1], ps[:, 0:1],
                            1.0 / nelem, 1e-5, ALU.mult, ALU.max),
                  deps=[tm_], label="qmean")
        bank.add(t1)
        qc.add(t1)
        t2 = s.op("ve", _recip(qc.t[:, s_coli:s_coli + 1],
                               qc.t[:, mean_col:mean_col + 1]), deps=[t1],
                  label="qs")
        qc.add(t2)
        return t2

    ts1 = scales(0, 4, 3, 1024 * 4096, tabs)
    ts2 = scales(1, 6, 5, 4096 * 1024, tabs)

    tw1, tw1map = _quant_pass(kb, cm, kb.dram["l0f_w1T"], w1h, 1024, 4096,
                              lambda k, j: k * 4096 + j, qc.t[:, 3:4], None,
                              ts1, 0, jouter=True)
    tw1c = [tw1map[(7, jj * 512)] for jj in range(8)]
    tw2, tw2map = _quant_pass(kb, cm, kb.dram["l0f_w2T"], w2h, 4096, 1024,
                              lambda k, j: k * 1024 + j, qc.t[:, 5:6], None,
                              ts2, 1)
    kb.wtok["l0f"] = (w1h, tw1, w2h, tw2, cwc, tcw, qc, onesf, tof,
                      tw1c, tw2map, ts1)


def gather_edge_rows(kb, cm, src_dram, idx, src_deps=None):
    s = kb.s
    rows = kb.tbuf(f"g{idx}_rows", [2, D], F32)
    rowg = kb.tbuf(f"g{idx}_rowg", [16, D], F32)
    rsel = kb.tbuf(f"g{idx}_rsel", [16, D], BF16)
    rq = rows
    rqb = kb.tbuf(f"g{idx}_rqb", [2, D], BF16)
    xhf = kb.tbuf(f"g{idx}_xhf", [128, 16], BF16)
    sqg = cm.sq()
    col = kb.tbuf(f"g{idx}_col", [2, 8], F32)
    cin = kb.dram_tmp(f"g{idx}_in", [2, D])
    cout = kb.dram_tmp(f"g{idx}_out", [16, D])
    tmpd = kb.dram_tmp(f"g{idx}_tmp", [2, D], BF16)

    t1 = s.op("se", _dma(rows.t[0:1, :], src_dram.ap()[0:1, :]),
              deps=rows.w() + ([src_deps[0]] if src_deps else []),
              sem=f"dg{idx}", label="ger0")
    t2 = s.op("se", _dma(rows.t[1:2, :], src_dram.ap()[TCORE - 1:TCORE, :]),
              deps=[src_deps[-1]] if src_deps else [], sem=f"dg{idx}2",
              label="ger1")
    rows.add(t1, t2)
    ta = s.op("ve", _red(col.t[:, 0:1], rows.t[:], ALU.add),
              deps=[t1, t2] + col.w())
    tb = s.op("se", _act(sqg.t[0:2, :], rows.t[:], AF.Square,
                         accum=col.t[:, 1:2]),
              deps=[t1, t2] + sqg.w(), label="gsq")
    sqg.add(tb)
    tc = s.op("ve", _ts(col.t[:, 0:1], col.t[:, 0:1], 1.0 / D, None, ALU.mult),
              deps=[ta])
    td = s.op("ve", _stt(col.t[:, 2:3], col.t[:, 0:1], 0.0, col.t[:, 0:1],
                         ALU.bypass, ALU.mult), deps=[tc])
    te = s.op("ve", _ts(col.t[:, 1:2], col.t[:, 1:2], 1.0 / D, None, ALU.mult),
              deps=[tb])
    tf = s.op("ve", _tt(col.t[:, 2:3], col.t[:, 1:2], col.t[:, 2:3],
                        ALU.subtract), deps=[td, te])
    tg = s.op("se", _act(col.t[:, 3:4], col.t[:, 2:3], AF.Sqrt,
                         bias=cm.epscol[0:2, 0:1]), deps=[tf, cm.tepc])
    th = s.op("ve", _recip(col.t[:, 3:4], col.t[:, 3:4]), deps=[tg])
    ti = s.op("ve", _ts(rows.t[:], rows.t[:], col.t[:, 0:1], col.t[:, 3:4],
                        ALU.subtract, ALU.mult), deps=[th, tc], label="gln")
    rows.add(ti)
    col.add(ti)
    tj = s.op("se", _dma(cin.ap(), rows.t[:]), deps=[ti], sem=f"dg{idx}")
    rows.add(tj)
    return (rows, rowg, rsel, rq, rqb, xhf, col, cin, cout, tmpd, tj, idx)


def gather_edge_finish(kb, cm, st, tscale):
    s = kb.s
    rows, rowg, rsel, rq, rqb, xhf, col, cin, cout, tmpd, tj, idx = st
    tk = s.op("gp", _cc_allgather(cin.ap().opt(), cout.ap().opt()),
              deps=[tj], sem=f"cc{idx}", n=1, label=f"ag{idx}")
    tl = s.op("se", _dma(rowg.t[:], cout.ap()), deps=[tk] + rowg.w(),
              sem=f"dg{idx}")
    rowg.add(tl)
    selb = kb.tbuf(f"g{idx}_selb", [16, 2], BF16)
    tsb_ = s.op("ve", _copy(selb.t[:], cm.meta[0:16, 9:11]),
                deps=[cm.tconst] + selb.w(), label="gselc")
    selb.add(tsb_)
    tcast16 = s.op("ve", _copy(rsel.t[:], rowg.t[:]),
                   deps=[tl] + rsel.w(), label="gcast16")
    rsel.add(tcast16)
    rowg.add(tcast16)
    sel_toks = []
    for half in range(2):
        bank = cm.bank()
        ps = bank.t
        tm_ = s.op("pe", _mm(ps[0:2, 0:512], selb.t[:],
                             rsel.t[:, half * 512:(half + 1) * 512],
                             start=True, stop=True),
                   deps=[tsb_, tcast16] + bank.w(), label="gsum")
        selb.add(tm_)
        rsel.add(tm_)
        tev = s.op("ve", _copy(rq.t[:, half * 512:(half + 1) * 512],
                               ps[0:2, 0:512]),
                   deps=[tm_] + (rq.w() if half == 0 else []), label="gsev")
        bank.add(tev)
        rq.add(tev)
        sel_toks.append(tev)
    qc = kb.wtok["l0f"][6]
    tqa = s.op("ve", _red(col.t[:, 4:5], rq.t[:], ALU.max, absv=True),
               deps=sel_toks + [col.deps[-1] if col.deps else None],
               label="gqm")
    col.add(tqa)
    tqb = s.op("ve", _ts(col.t[:, 5:6], col.t[:, 4:5], 1e-5, 1.0 / 127.0,
                         ALU.max, ALU.mult), deps=[tqa], label="gqs")
    col.add(tqb)
    tqc = s.op("ve", _tt(col.t[:, 6:7], col.t[:, 5:6], qc.t[0:2, 4:5],
                         ALU.mult), deps=[tqb, tscale], label="gqc")
    col.add(tqc)
    tqd = s.op("ve", _recip(col.t[:, 5:6], col.t[:, 5:6]), deps=[tqc])
    col.add(tqd)
    tq1 = s.op("ve", _ts(rq.t[:], rq.t[:], col.t[:, 5:6], MAGIC, ALU.mult,
                         ALU.add), deps=[tqd], label="gq1")
    rq.add(tq1)
    tq2 = s.op("ve", _ts(rqb.t[:], rq.t[:], MAGIC, col.t[:, 6:7],
                         ALU.subtract, ALU.mult), deps=[tq1] + rqb.w(),
               label="gq2")
    rqb.add(tq2)
    rq.add(tq2)
    col.add(tq2)
    tq3 = s.op("se", _dma(tmpd.ap(), rqb.t[:]), deps=[tq2], sem=f"dg{idx}")
    rqb.add(tq3)
    xw = xhf.w()
    tq4 = None
    for a in range(2):
        tq4 = s.op("se", _dma(xhf.t[:, a:a + 15:2],
                              tmpd.ap()[a:a + 1, :].rearrange(
                                  "a (k p) -> (a p) k", p=128), slow=True),
                   deps=[tq3] + (xw if a == 0 else []), sem=f"dg{idx}",
                   label="gxhf")
        xhf.add(tq4)
    return xhf, tq4


def emit_l0f(kb, cm, src_dram, dst_dram, src_deps=None, gather=None):
    s = kb.s
    (w1h, tw1, w2h, tw2, cwc, tcw, qc, onesf, tof, tw1c,
     tw2map, ts1g) = kb.wtok["l0f"]
    NH = 32
    GL = 512
    NG = TCORE // GL
    NHF = GL // 128
    W = GL + 2
    hb = [kb.tbuf(f"l0f_h{i}", [128, NH * W], BF16) for i in range(2)]
    fscr = [kb.tbuf(f"l0f_f{i}", [128, D], F32) for i in range(2)]
    xr0 = kb.tbuf("l0f_xr0", [128, D], F32)
    xrb = [xr0, xr0]
    cbuf = [kb.tbuf(f"l0f_cb{i}", [128, GL], F32) for i in range(2)]
    macc = kb.tbuf("l0f_macc", [128, GL], BF16)
    nacc = kb.tbuf("l0f_nacc", [128, GL], BF16)
    mtt = kb.tbuf("l0f_mtt", [128, GL], BF16)
    s2r = kb.tbuf("l0f_s2r", [128, GL], F32)
    srow = s2r
    hh = kb.tbuf("l0f_hh", [128, 64], BF16)
    qcol = kb.tbuf("l0f_qcol", [128, 24], F32)

    if gather is None:
        gather = gather_edge_rows(kb, cm, src_dram, idx=1, src_deps=src_deps)
    xhf, thx = gather_edge_finish(kb, cm, gather, ts1g)

    hh_toks = []
    for m in range(NH):
        bank = cm.bank()
        ps = bank.t
        deps = bank.w() + [thx, tw1c[m // 4]]
        tmm = None
        for k in range(8):
            tmm = s.op("pe", _mm(ps[:, 0:2],
                                 w1h.t[:, k * 4096 + m * 128:
                                       k * 4096 + m * 128 + 128],
                                 xhf.t[:, k * 2:k * 2 + 2],
                                 start=(k == 0), stop=(k == 7)),
                       deps=deps if k == 0 else [], label="hhm")
        w1h.add(tmm)
        xhf.add(tmm)
        tev = s.op("se", _copy(hh.t[:, m * 2:m * 2 + 2], ps[:, 0:2]),
                   deps=[tmm] + (hh.w() if m == 0 else []), label="hhe")
        bank.add(tev)
        hh.add(tev)
        hh_toks.append(tev)

    def lnq_load(chunk, slot):
        t_xn, t_x = cm.ln_load_f32(src_dram, chunk, slot, fscr[slot % 2],
                                   src_deps)
        f = fscr[slot % 2]
        c0 = slot * 2
        tqa = s.op("ve", _red(qcol.t[:, c0:c0 + 1], f.t[:], ALU.max,
                              absv=True), deps=[t_xn] + qcol.w(), label="xqm")
        qcol.add(tqa)
        tqb = s.op("ve", _ts(qcol.t[:, c0:c0 + 1], qcol.t[:, c0:c0 + 1], 1e-5,
                             1.0 / 127.0, ALU.max, ALU.mult), deps=[tqa],
                   label="xqs")
        qcol.add(tqb)
        tqc = s.op("ve", _tt(qcol.t[:, c0 + 1:c0 + 2], qcol.t[:, c0:c0 + 1],
                             qc.t[:, 4:5], ALU.mult), deps=[tqb, tw1],
                   label="xqc")
        qcol.add(tqc)
        qc.add(tqc)
        tqd = s.op("ve", _recip(qcol.t[:, c0:c0 + 1], qcol.t[:, c0:c0 + 1]),
                   deps=[tqb], label="xqr")
        qcol.add(tqd)
        tq1 = s.op("ve", _ts(f.t[:], f.t[:], qcol.t[:, c0:c0 + 1], MAGIC,
                             ALU.mult, ALU.add), deps=[tqd, tqc], label="xq1")
        f.add(tq1)
        tq2 = s.op("ve", _ts(cm.xn[slot].t[:], f.t[:], MAGIC,
                             qcol.t[:, c0 + 1:c0 + 2], ALU.subtract, ALU.mult),
                   deps=[tq1] + cm.xn[slot].w(), label="xq2")
        cm.xn[slot].add(tq2)
        f.add(tq2)
        qcol.add(tq1, tq2)
        return tq2

    store_toks = []
    edge = kb.tbuf("l0f_edge", [128, 32], BF16)

    def lnq_group(g):
        fm_toks = []
        for half in range(NHF):
            chunk = g * NHF + half
            tq2 = lnq_load(chunk, half)
            fm_toks += cm.xn_to_fm(half, half * 128, tq2)
        return fm_toks

    def w1_m(g, m, fm_toks, hw):
        h = hb[g % 2]
        bank = cm.bank()
        ps = bank.t
        deps = bank.w() + fm_toks + [tw1c[m // 4]] + hw
        tmm = None
        for k in range(8):
            tmm = s.op("pe", _mm(ps[:, 0:GL],
                                 w1h.t[:, k * 4096 + m * 128:
                                       k * 4096 + m * 128 + 128],
                                 cm.xnf[k].t[:, 0:GL],
                                 start=(k == 0), stop=(k == 7)),
                       deps=deps if k == 0 else [], label="h1m")
        w1h.add(tmm)
        for b in cm.xnf:
            b.add(tmm)
        tev = s.op("se", _copy(h.t[:, m * W + 1:m * W + 1 + GL],
                               ps[:, 0:GL]), deps=[tmm], label="h1e")
        bank.add(tev)
        h.add(tev)
        return tev

    def compute_h(g):
        fm_toks = lnq_group(g)
        h = hb[g % 2]
        hw = h.w()
        return [w1_m(g, m, fm_toks, hw if m == 0 else []) for m in range(NH)]

    def cycle(g, htoks):
        h = hb[g % 2]
        hn = hb[(g + 1) % 2]
        last = g == NG - 1
        hl = h.t[:, 0:NH * W:W]
        if g == 0:
            tl_ = s.op("ve", _copy(hl, hh.t[:, 0:64:2]),
                       deps=[hh_toks[-1], htoks[-1]], label="el")
        else:
            tl_ = s.op("ve", _copy(hl, edge.t[:, 0:32]),
                       deps=[edge.deps[-1]] if edge.deps else [], label="el")
            edge.add(tl_)
        h.add(tl_)
        tsv = s.op("ve", _copy(edge.t[:, 0:32], h.t[:, GL:NH * W:W]),
                   deps=edge.w() + [htoks[-1]], label="esv")
        edge.add(tsv)
        h.add(tsv)
        fm_toks = [] if last else lnq_group(g + 1)
        if not last:
            hnw = hn.w()
        else:
            tr_all = s.op("ve", _copy(h.t[:, GL + 1:NH * W:W],
                                      hh.t[:, 1:64:2]),
                          deps=[hh_toks[-1], tl_, tsv], label="er")
            h.add(tr_all)
        mtoks = []
        ntoks = []
        for m in range(NH):
            if not last:
                tev = w1_m(g + 1, m, fm_toks, hnw if m == 0 else [])
                ntoks.append(tev)
                tr_ = s.op("ve", _copy(h.t[:, m * W + GL + 1:m * W + GL + 2],
                                       hn.t[:, m * W + 1:m * W + 2]),
                           deps=[tev] + ([tl_, tsv] if m == 0 else []),
                           label="er")
                hn.add(tr_)
                h.add(tr_)
                ed = [tr_]
            else:
                ed = [tr_all] if m == 0 else []
            cb = cbuf[m % 2]
            c1 = s.op("ve", _ts(cb.t[:, 0:GL],
                                h.t[:, m * W + 1:m * W + 1 + GL],
                                cwc[:, 32 + m:33 + m], None, ALU.mult),
                      deps=[tcw] + ed + cb.w(), label="cv1")
            cb.add(c1)
            c2 = s.op("ve", _stt(cb.t[:, 0:GL], h.t[:, m * W:m * W + GL],
                                 cwc[:, m:m + 1], cb.t[:, 0:GL], ALU.mult,
                                 ALU.add), deps=[c1], label="cv2")
            cb.add(c2)
            c3 = s.op("ve", _stt(cb.t[:, 0:GL],
                                 h.t[:, m * W + 2:m * W + 2 + GL],
                                 cwc[:, 64 + m:65 + m], cb.t[:, 0:GL],
                                 ALU.mult, ALU.add), deps=[c2], label="cv3")
            cb.add(c3)
            c4 = s.op("se", _act(h.t[:, m * W + 1:m * W + 1 + GL],
                                 cb.t[:, 0:GL], AF.Silu,
                                 bias=cwc[:, 96 + m:97 + m]), deps=[c3],
                      label="silu")
            h.add(c4)
            cb.add(c4)
            mtoks.append(c4)
        for m in range(NH):
            hs = h.t[:, m * W + 1:m * W + 1 + GL]
            if m == 0:
                t5 = s.op("ve", _tt(macc.t[:, 0:GL], hs, hs, ALU.max),
                          deps=[mtoks[0]] + macc.w(), label="amx0")
                t6 = s.op("ve", _ts(nacc.t[:, 0:GL], hs, -1.0, None,
                                    ALU.mult), deps=[mtoks[0]] + nacc.w(),
                          label="anx0")
            else:
                t5 = s.op("ve", _tt(macc.t[:, 0:GL], macc.t[:, 0:GL], hs,
                                    ALU.max), deps=[mtoks[m], macc.deps[-1]],
                          label="amx")
                t6 = s.op("ve", _stt(nacc.t[:, 0:GL], hs, -1.0,
                                     nacc.t[:, 0:GL], ALU.mult, ALU.max),
                          deps=[mtoks[m], nacc.deps[-1]], label="anx")
            macc.add(t5)
            nacc.add(t6)
            h.add(t5, t6)
        tmrg = s.op("ve", _tt(mtt.t[:, 0:GL], macc.t[:, 0:GL],
                              nacc.t[:, 0:GL], ALU.max),
                    deps=[macc.deps[-1], nacc.deps[-1]] + mtt.w(),
                    label="amrg")
        macc.add(tmrg)
        nacc.add(tmrg)
        mtt.add(tmrg)
        red_toks = []
        for half in range(NHF):
            tbk = cm.tbank()
            ps = tbk.t
            ttr = s.op("pe", _tr(ps[:, 0:128],
                                 mtt.t[:, half * 128:half * 128 + 128],
                                 cm.ident[:]),
                       deps=[tmrg, cm.tconst] + tbk.w(), label="mtr")
            mtt.add(ttr)
            tev = s.op("ve", _copy(s2r.t[:, half * 128:half * 128 + 128],
                                   ps[:, 0:128]),
                       deps=[ttr] + (s2r.w() if half == 0 else []),
                       label="mtre")
            tbk.add(tev)
            s2r.add(tev)
            red_toks.append(tev)
        c0s = []
        qrow_toks = []
        cast_deps = [tmrg] + mtt.w()
        for half in range(NHF):
            c0 = 8 + (g % 2) * 8 + half * 2
            c0s.append(c0)
            tqa = s.op("ve", _red(qcol.t[:, c0:c0 + 1],
                                  s2r.t[:, half * 128:half * 128 + 128],
                                  ALU.max),
                       deps=[red_toks[half]] + qcol.w(), label="hqm")
            s2r.add(tqa)
            qcol.add(tqa)
            tqb = s.op("ve", _ts(qcol.t[:, c0:c0 + 1], qcol.t[:, c0:c0 + 1],
                                 1e-5, 1.0 / 127.0, ALU.max, ALU.mult),
                       deps=[tqa])
            qcol.add(tqb)
            tqc = s.op("ve", _tt(qcol.t[:, c0 + 1:c0 + 2],
                                 qcol.t[:, c0:c0 + 1], qc.t[:, 6:7], ALU.mult),
                       deps=[tqb, tw2], label="hqc")
            qcol.add(tqc)
            qc.add(tqc)
            tqd = s.op("ve", _recip(qcol.t[:, c0:c0 + 1],
                                    qcol.t[:, c0:c0 + 1]), deps=[tqc],
                       label="hqr")
            qcol.add(tqd)
            tq_cast = s.op("ve", _copy(mtt.t[:, half:half + 1],
                                       qcol.t[:, c0:c0 + 1]),
                           deps=[tqd] + cast_deps, label="s2c")
            cast_deps = []
            mtt.add(tq_cast)
            bank2 = cm.bank()
            ps2 = bank2.t
            ttr2 = s.op("pe", _mm(ps2[0:1, 0:128],
                                  mtt.t[:, half:half + 1], cm.ident[:],
                                  start=True, stop=True),
                        deps=[tq_cast, cm.tconst] + bank2.w(), label="s2tr")
            mtt.add(ttr2)
            tsr = s.op("ve", _copy(srow.t[0:1, half * 128:half * 128 + 128],
                                   ps2[0:1, 0:128]),
                       deps=[ttr2, tqa], label="s2row")
            bank2.add(tsr)
            srow.add(tsr)
            qrow_toks.append(tsr)
        bank3 = cm.bank()
        ps3 = bank3.t
        tbc = s.op("pe", _mm(ps3[:, 0:GL], onesf[0:1, 0:128],
                             srow.t[0:1, 0:GL], start=True, stop=True),
                   deps=qrow_toks + [tof] + bank3.w(), label="s2bc")
        srow.add(tbc)
        tsb = s.op("ve", _copy(s2r.t[:], ps3[:, 0:GL]),
                   deps=[tbc] + s2r.w(), label="s2sb")
        bank3.add(tsb)
        s2r.add(tsb)
        wbanks = {}
        wmm = {}
        for bi in range(4):
            wbanks[bi] = cm.bank()
        wdeps = {bi: wbanks[bi].w() for bi in range(4)}
        qtoks = []
        for m in range(NH):
            q1 = s.op("gp",
                      _tt(cbuf[m % 2].t[:, 0:GL],
                          h.t[:, m * W + 1:m * W + 1 + GL], s2r.t[:],
                          ALU.mult),
                      deps=[tsb, mtoks[m]] + cbuf[m % 2].w(), label="hq1")
            cbuf[m % 2].add(q1)
            q2 = s.op("ve",
                      _ts(h.t[:, m * W + 1:m * W + 1 + GL],
                          cbuf[m % 2].t[:, 0:GL], MAGIC, MAGIC, ALU.add,
                          ALU.subtract), deps=[q1], label="hq2")
            h.add(q2)
            cbuf[m % 2].add(q2)
            qtoks.append(q2)
            for bi in range(4):
                half, gg = bi // 2, bi % 2
                wmm[bi] = s.op(
                    "pe",
                    _mm(wbanks[bi].t[:, 0:512],
                        h.t[:, m * W + 1 + half * 128:
                            m * W + 129 + half * 128],
                        w2h.t[:, m * 1024 + gg * 512:
                              m * 1024 + gg * 512 + 512],
                        start=(m == 0), stop=(m == NH - 1)),
                    deps=[q2, tw2map[(m, gg * 512)]] +
                         (wdeps[bi] if m == 0 else []),
                    label="h2m")
                h.add(wmm[bi])
        def evac_epilogue(half):
            chunk = g * NHF + half
            c0 = c0s[half]
            ob = fscr[half % 2]
            obw = ob.w()
            xr = xrb[half % 2]
            txl = s.op("sp", _dma(xr.t[:],
                                  src_dram.ap()[chunk * 128:
                                                (chunk + 1) * 128, :]),
                       deps=xr.w() + ([src_deps[chunk]] if src_deps else []),
                       sem=f"dxr{half % 2}", label=f"xr{chunk}")
            xr.add(txl)
            o_toks = []
            for gg in range(2):
                bi = half * 2 + gg
                tev = s.op("ve", _ts(ob.t[:, gg * 512:(gg + 1) * 512],
                                     wbanks[bi].t[:, 0:512],
                                     qcol.t[:, c0 + 1:c0 + 2],
                                     None, ALU.mult),
                           deps=[wmm[bi]] + (obw if gg == 0 else []),
                           label="h2e")
                wbanks[bi].add(tev)
                ob.add(tev)
                qcol.add(tev)
                o_toks.append(tev)
            tst, _ = epilogue_store(kb, cm, ob, o_toks, half, dst_dram, chunk,
                                    xbuf=xr)
            store_toks.append(tst)

        evac_epilogue(0)
        evac_epilogue(1)
        for bi in range(4, 2 * NHF):
            half, gg = bi // 2, bi % 2
            bank = cm.bank()
            wbanks[bi] = bank
            deps = bank.w() + [qtoks[-1], tw2]
            tmm = None
            for m in range(NH):
                tmm = s.op("pe",
                           _mm(bank.t[:, 0:512],
                               h.t[:, m * W + 1 + half * 128:
                                   m * W + 129 + half * 128],
                               w2h.t[:, m * 1024 + gg * 512:
                                     m * 1024 + gg * 512 + 512],
                               start=(m == 0), stop=(m == NH - 1)),
                           deps=deps if m == 0 else
                           [tw2map[(m, gg * 512)]], label="h2m")
            wmm[bi] = tmm
            h.add(tmm)
        for bi in wmm:
            w2h.add(wmm[bi])
        evac_epilogue(2)
        evac_epilogue(3)
        return ntoks

    htoks = compute_h(0)
    for g in range(NG):
        htoks = cycle(g, htoks)
    return store_toks



def prep_l0a(kb, cm):
    s = kb.s
    wA = kb.tbuf("l0a_wA", [128, 8 * D], BF16)
    wB = kb.tbuf("l0a_wB", [128, 8 * D], BF16)
    wC = kb.tbuf("l0a_wC", [128, 8 * D], BF16)
    ldt = kb.tbuf("l0a_ld", [128, 8 * 64], BF16)
    lut = kb.tbuf("l0a_lu", [64, D], BF16)
    kb.sbuf("l0a_vec_sb", [128, 32], F32)
    ind = np.zeros((128, 8 * 16), dtype=BF)
    for p in range(8):
        ind[0:64, p * 16 + 2 * p] = 1
        ind[64:128, p * 16 + 2 * p + 1] = 1
    indh = kb.inline("c_l0aind", ind)
    bc = np.zeros((16, 8 * 128), dtype=BF)
    for p in range(8):
        for j in range(128):
            bc[2 * p + j // 64, p * 128 + j] = 1
    bch = kb.inline("c_l0abc", bc)
    kb.sbuf("l0a_ind", [128, 8 * 16], BF16)
    kb.sbuf("l0a_bc", [16, 8 * 128], BF16)
    t1 = s.op("sp", _dma(kb.sb["l0a_ind"][:], indh.ap()), sem=kb.small_sem)
    t2 = s.op("sp", _dma(kb.sb["l0a_bc"][:], bch.ap()), sem=kb.small_sem)
    t3 = s.op("sp", _dma(kb.sb["l0a_vec_sb"][:], kb.dram["l0a_vec"].ap()),
              sem=kb.small_sem)
    tk = load_wt(kb, cm, kb.dram["l0a_wkT"], wA, D, D, slot=0)
    tv = load_wt(kb, cm, kb.dram["l0a_wvT"], wB, D, D, slot=1)
    to = load_wt(kb, cm, kb.dram["l0a_woT"], wC, D, D, slot=0)
    tld = load_wt(kb, cm, kb.dram["l0a_ldT"], ldt, D, 64, slot=1)
    tlu = None
    for jj in range(2):
        stage = cm.wstage[jj]
        tlu1 = s.op("sp" if jj == 0 else "gp",
                    _dma(stage.t[0:64, 0:512],
                         kb.dram["l0a_luT"].ap()[:, jj * 512:(jj + 1) * 512]),
                    deps=stage.w(), sem=f"dwt{jj}", label="lul")
        stage.add(tlu1)
        tlu = s.op("ve", _copy(lut.t[:, jj * 512:(jj + 1) * 512],
                               stage.t[0:64, 0:512]),
                   deps=[tlu1] + (lut.w() if jj == 0 else []), label="luc")
        stage.add(tlu)
        lut.add(tlu)
    kb.wtok["l0a"] = (wA, tk, wB, tv, wC, to, ldt, tld, lut, tlu, t3, t1, t2)


def emit_l0a(kb, cm, src_dram, dst_dram, src_deps=None, pump=None,
             pump_drain=None):
    s = kb.s
    (wA, twk, wB, twv, wC, two, ldt, tld, lut, tlu, tvec, tind, tbc) = \
        kb.wtok["l0a"]
    vec = kb.sb["l0a_vec_sb"]
    indt = kb.sb["l0a_ind"]
    bct = kb.sb["l0a_bc"]
    kraw = [kb.tbuf(f"l0a_k{m}", [128, TCORE], BF16) for m in range(8)]
    vst = [kb.tbuf(f"l0a_v{i}", [128, D], BF16) for i in range(NCH)]
    S = kb.tbuf("l0a_S", [128, 512], F32)
    Sbf = kb.tbuf("l0a_Sbf", [128, 512], BF16)
    Sg = kb.tbuf("l0a_Sg", [128, 520], F32)
    Ltot = kb.tbuf("l0a_Lt", [128, 8], F32)
    wp = [kb.tbuf(f"l0a_wp{m}", [128, 512], F32) for m in range(8)]
    Qt = kb.tbuf("l0a_Q", [128, 512], F32)
    eK = kb.tbuf("l0a_eK", [128, 512], BF16)
    eR = kb.tbuf("l0a_eR", [128, 512], BF16)
    kt_ = kb.tbuf("l0a_kt", [128, 512], BF16)
    trm = kt_
    rt_ = kb.tbuf("l0a_rt", [128, 512], BF16)
    kut = kb.tbuf("l0a_ku", [128, 512], BF16)
    ktB = kb.tbuf("l0a_ktB", [128, 512], BF16)
    kuB = kb.tbuf("l0a_kuB", [128, 512], BF16)
    kh = kb.tbuf("l0a_kh", [128, 512], BF16)
    ktm = kb.tbuf("l0a_ktm", [128, 512], BF16)
    coltmp = kb.tbuf("l0a_ct", [128, 48], F32)
    scb = kb.tbuf("l0a_scb", [128, 256], BF16)
    sc2b = kb.tbuf("l0a_sc2", [128, 256], BF16)
    rfm = [kb.tbuf(f"l0a_r{m}", [128, 256], BF16) for m in range(8)]
    gfm = [kb.tbuf(f"l0a_g{m}", [128, 256], BF16) for m in range(8)]
    wkv = kb.tbuf("l0a_wkv", [128, 1024], BF16)
    gng = kb.tbuf("l0a_gng", [128, 1024], BF16)
    st16 = gng
    G16 = 768
    gnt = Sg
    oti = kb.tbuf("l0a_o0", [128, D], F32)
    cq = oti
    stf = oti
    SOFF = 96
    ccin = kb.dram_tmp("l0a_ccin", [128, 520])
    ccout = kb.dram_tmp("l0a_ccout", [NCORES * 128, 520])

    tS0 = s.op("ve", _memset(S.t[:], 0.0), deps=S.w(), label="S0a")
    S.add(tS0)
    tL0 = s.op("ve", _memset(Ltot.t[:], 0.0), deps=Ltot.w(), label="L0")
    Ltot.add(tL0)
    tm1 = s.op("ve", _memset(vec[0:64, 18:19], 1.0), deps=[tvec],
               label="mA1")
    tm2 = s.op("ve", _memset(vec[64:128, 18:19], 0.0), deps=[tm1],
               label="mA0")
    tm3 = s.op("ve", _memset(vec[0:64, 19:20], 0.0), deps=[tm2], label="mB0")
    tm4 = s.op("ve", _memset(vec[64:128, 19:20], 1.0), deps=[tm3],
               label="mB1")
    tmu = tm4
    for m in range(8):
        tmu = s.op("ve", _tt(vec[:, 20 + m:21 + m], vec[:, 8 + m:9 + m],
                             vec[:, 18:19], ALU.mult), deps=[tmu],
                   label="uA")

    def w_chain(fm_toks):
        bank = cm.bank()
        ps = bank.t
        deps = bank.w() + fm_toks + [tld]
        tmm = None
        for k in range(8):
            tmm = s.op("pe", _mm(ps[0:64, 0:512],
                                 ldt.t[:, k * 64:(k + 1) * 64],
                                 cm.xnf[k].t[:, 0:512],
                                 start=(k == 0), stop=(k == 7)),
                       deps=deps if k == 0 else [], label="lora1")
        ldt.add(tmm)
        for b in cm.xnf:
            b.add(tmm)
        tth = s.op("se", _act(trm.t[0:64, :], ps[0:64, 0:512], AF.Tanh),
                   deps=[tmm] + trm.w(), label="ltanh")
        bank.add(tth)
        trm.add(tth)
        wtoks = []
        for m in range(8):
            bank2 = cm.bank()
            ps2 = bank2.t
            tm2 = s.op("pe", _mm(ps2[:, 0:512],
                                 lut.t[:, m * 128:(m + 1) * 128],
                                 trm.t[0:64, :],
                                 start=True, stop=True),
                       deps=[tth, tlu] + bank2.w(), label="lora2")
            lut.add(tm2)
            trm.add(tm2)
            e1 = s.op("ve", _ts(wp[m].t[:], ps2[:, 0:512], vec[:, m:m + 1],
                                None, ALU.add), deps=[tm2, tvec] + wp[m].w(),
                      label="spz")
            wp[m].add(e1)
            bank2.add(e1)
            e2 = s.op("se", _act(cm.sqp[m % 2].t[:, 0:512], wp[m].t[:], AF.Exp),
                      deps=[e1] + cm.sqp[m % 2].w(), label="spe")
            cm.sqp[m % 2].add(e2)
            e3 = s.op("se", _act(wp[m].t[:], cm.sqp[m % 2].t[:, 0:512], AF.Ln,
                                 bias=1.0), deps=[e2], label="spl")
            wp[m].add(e3)
            cm.sqp[m % 2].add(e3)
            wtoks.append(e3)
        return wtoks

    def chunk_decay(c, mh, wtoks, track_ltot):
        cc = c * 128
        sc_toks = []
        qtw = Qt.w()
        for i in range(4):
            m = mh * 4 + i
            t1 = s.op("ve", _scan_add(Qt.t[:, i * 128:(i + 1) * 128],
                                      wp[m].t[:, cc:cc + 128],
                                      cm.zeros[:, 0:128]),
                      deps=[wtoks[m], cm.tz] + (qtw if i == 0 else []),
                      label="scan")
            Qt.add(t1)
            wp[m].add(t1)
            sc_toks.append(t1)
        ctw = coltmp.w()
        ct_toks = []
        for i in range(4):
            m = mh * 4 + i
            t2 = s.op("ve", _copy(coltmp.t[:, m:m + 1],
                                  Qt.t[:, i * 128 + 63:i * 128 + 64]),
                      deps=[sc_toks[i]] + (ctw if i == 0 else []),
                      label="qmid")
            coltmp.add(t2)
            Qt.add(t2)
            t3 = s.op("ve", _ts(Qt.t[:, i * 128:(i + 1) * 128],
                                Qt.t[:, i * 128:(i + 1) * 128],
                                coltmp.t[:, m:m + 1], None, ALU.subtract),
                      deps=[t2], label="qsub")
            Qt.add(t3)
            coltmp.add(t3)
            t4 = s.op("ve", _copy(coltmp.t[:, 8 + m:9 + m],
                                  Qt.t[:, i * 128 + 127:i * 128 + 128]),
                      deps=[t3], label="qlast")
            coltmp.add(t4)
            Qt.add(t4)
            ct_toks.append(t4)
        c4_ = mh * 4
        teK = s.op("se", _act(eK.t[:], Qt.t[:], AF.Exp),
                   deps=ct_toks + eK.w(), label="ek")
        eK.add(teK)
        Qt.add(teK)
        t5 = s.op("se", _act(coltmp.t[:, 16 + c4_:20 + c4_],
                             coltmp.t[:, 8 + c4_:12 + c4_], AF.Exp,
                             scale=-1.0), deps=ct_toks, label="eql")
        coltmp.add(t5)
        t6 = s.op("ve", _tt(coltmp.t[:, 32 + c4_:36 + c4_],
                            coltmp.t[:, c4_:4 + c4_],
                            coltmp.t[:, 8 + c4_:12 + c4_], ALU.add),
                  deps=ct_toks, label="qlsum")
        coltmp.add(t6)
        t7 = s.op("se", _act(coltmp.t[:, 24 + c4_:28 + c4_],
                             coltmp.t[:, 32 + c4_:36 + c4_], AF.Exp,
                             scale=-1.0), deps=[t6], label="dcol")
        coltmp.add(t7)
        t7b = s.op("se", _act(coltmp.t[:, 40 + c4_:44 + c4_],
                              coltmp.t[:, c4_:4 + c4_], AF.Exp,
                              scale=-1.0), deps=ct_toks, label="eqm")
        coltmp.add(t7b)
        if track_ltot:
            t8 = s.op("ve", _tt(Ltot.t[:, c4_:4 + c4_],
                                Ltot.t[:, c4_:4 + c4_],
                                coltmp.t[:, 32 + c4_:36 + c4_],
                                ALU.add), deps=[t6, Ltot.deps[-1]],
                      label="lacc")
            Ltot.add(t8)
            coltmp.add(t8)
        return teK, t5, t7

    def chunk_state(g, c, mh, teK, teql, tdcol):
        cc = c * 128
        gc = g * 4 + c
        khw = kh.w()
        kh_toks = []
        for i in range(4):
            m = mh * 4 + i
            t1 = s.op("ve", _tt(kh.t[:, i * 128:(i + 1) * 128],
                                kraw[m].t[:, g * 512 + cc:g * 512 + cc + 128],
                                eK.t[:, i * 128:(i + 1) * 128], ALU.mult),
                      deps=[teK] +
                           ([kraw[m].deps[-1]] if kraw[m].deps else []) +
                           (khw if i == 0 else []), label="kh1")
            kh.add(t1)
            kraw[m].add(t1)
            eK.add(t1)
            t2 = s.op("ve", _ts(kh.t[:, i * 128:(i + 1) * 128],
                                kh.t[:, i * 128:(i + 1) * 128],
                                coltmp.t[:, 16 + m:17 + m], None, ALU.mult),
                      deps=[t1, teql], label="kh2")
            kh.add(t2)
            coltmp.add(t2)
            kh_toks.append(t2)
        kt_toks = []
        ktw = ktm.w()
        for i in range(4):
            tb = cm.tbank()
            ps = tb.t
            tt_ = s.op("pe", _tr(ps[:, 0:128], kh.t[:, i * 128:(i + 1) * 128],
                                 cm.ident[:]),
                       deps=[kh_toks[i], cm.tconst] + tb.w() +
                            (ktw if i == 0 else []), label="khtr")
            te = s.op("se", _copy(ktm.t[:, i * 128:(i + 1) * 128],
                                  ps[:, 0:128]), deps=[tt_], label="khtre")
            tb.add(te)
            ktm.add(te)
            kh.add(tt_)
            kt_toks.append(te)
        supd = []
        for i in range(4):
            m = mh * 4 + i
            bank = cm.bank()
            ps = bank.t
            bw = bank.w()
            m1 = s.op("pe", _mm(ps[0:64, 0:64],
                                ktm.t[:, i * 128:i * 128 + 64],
                                vst[gc].t[:, (2 * m) * 64:(2 * m + 1) * 64],
                                start=True, stop=True, tp=(0, 0)),
                      deps=[kt_toks[i],
                            vst[gc].deps[-1] if vst[gc].deps else None] + bw,
                      label="stA")
            m2 = s.op("pe", _mm(ps[64:128, 0:64],
                                ktm.t[:, i * 128 + 64:i * 128 + 128],
                                vst[gc].t[:, (2 * m + 1) * 64:
                                          (2 * m + 2) * 64],
                                start=True, stop=True, tp=(0, 64)),
                      deps=[], label="stB")
            ktm.add(m1, m2)
            vst[gc].add(m1, m2)
            tu = s.op("ve", _stt(S.t[:, m * 64:(m + 1) * 64],
                                 S.t[:, m * 64:(m + 1) * 64],
                                 coltmp.t[:, 24 + m:25 + m], ps[:, 0:64],
                                 ALU.mult, ALU.add),
                      deps=[m1, m2, tdcol] +
                           ([S.deps[-1]] if S.deps else []), label="sup")
            bank.add(tu)
            S.add(tu)
            coltmp.add(tu)
            supd.append(tu)
        return supd

    for g in range(4):
        fm_toks, per_chunk = cm.group_fm(src_dram, g, src_deps=src_deps)
        kevs = []

        def k_cb(m, ps, deps):
            t = s.op("se", _copy(kraw[m].t[:, g * 512:(g + 1) * 512], ps),
                     deps=deps + kraw[m].w(), label="kev")
            kraw[m].add(t)
            kevs.append(t)
            return t
        proj_fm(kb, cm, cm.xnf, 0, 512, wA, [twk], fm_toks, k_cb)

        def v_cb(ci, gg, ps, deps):
            vb = vst[g * 4 + ci]
            t = s.op("ve", _copy(vb.t[:, gg * 512:(gg + 1) * 512], ps),
                     deps=deps + (vb.w() if gg == 0 else []), label="vev")
            vb.add(t)
            return t
        proj_tm(kb, cm, cm.xnf, 0, wB, [twv], fm_toks, 4, v_cb)

        wtoks = w_chain(fm_toks)
        for c in range(4):
            for mh in range(2):
                teK, teql, tdcol = chunk_decay(c, mh, wtoks, True)
                if pump is not None:
                    pump(2)
                chunk_state(g, c, mh, teK, teql, tdcol)
                if pump is not None:
                    pump(2)

    if pump_drain is not None:
        pump_drain()
    slast = S.deps[-1]
    tc1 = s.op("sp", _dma(ccin.ap()[:, 0:512], S.t[:]), deps=[slast],
               sem="dgw")
    S.add(tc1)
    tc2 = s.op("sp", _dma(ccin.ap()[:, 512:520], Ltot.t[:]),
               deps=[Ltot.deps[-1]], sem="dgw")
    Ltot.add(tc2)
    tcc = s.op("gp", _cc_allgather(ccin.ap().opt(), ccout.ap().opt()),
               deps=[tc1, tc2], sem="ccw", n=1, label="agW")
    twr = load_wt(kb, cm, kb.dram["l0a_wrT"], wA, D, D, slot=0)
    twg = load_wt(kb, cm, kb.dram["l0a_wgT"], wB, D, D, slot=1)
    tz1 = s.op("ve", _memset(cq.t[:, 0:96], 0.0), deps=cq.w(), label="cqz")
    cq.add(tz1)
    tprev = tz1
    for c in range(NCORES):
        tg = s.op("sp", _dma(Sg.t[:, 512:520],
                             ccout.ap()[c * 128:(c + 1) * 128, 512:520]),
                  deps=[tcc] + Sg.w(), sem="dgw", label="sgq")
        Sg.add(tg)
        t1 = s.op("ve", _tt(cq.t[:, (c + 1) * 8:(c + 2) * 8],
                            cq.t[:, c * 8:(c + 1) * 8], Sg.t[:, 512:520],
                            ALU.add), deps=[tg, tprev], label="cum")
        cq.add(t1)
        Sg.add(t1)
        t2 = s.op("ve", _stt(cq.t[:, 80:88], Sg.t[:, 512:520],
                             cm.meta[:, 16 + c:17 + c], cq.t[:, 80:88],
                             ALU.mult, ALU.add), deps=[tg, tprev, cm.tconst],
                  label="myq")
        cq.add(t2)
        Sg.add(t2)
        tprev = t2
    tS1 = s.op("ve", _memset(S.t[:], 0.0), deps=S.w(), label="S0b")
    S.add(tS1)
    tprev2 = tS1
    for c in range(NCORES):
        tg = s.op("sp", _dma(Sg.t[:, 0:512],
                             ccout.ap()[c * 128:(c + 1) * 128, 0:512]),
                  deps=[tcc] + Sg.w(), sem="dgw", label="sgl")
        Sg.add(tg)
        t1 = s.op("ve", _tt(cq.t[:, 88:96], cq.t[:, (c + 1) * 8:(c + 2) * 8],
                            cq.t[:, 80:88], ALU.subtract),
                  deps=[tprev, tprev2] + ([cq.deps[-1]] if cq.deps else []),
                  label="efs")
        cq.add(t1)
        t2 = s.op("ve", _ts(cq.t[:, 88:96], cq.t[:, 88:96], 0.0, None,
                            ALU.min), deps=[t1], label="efc")
        cq.add(t2)
        t3 = s.op("se", _act(cq.t[:, 88:96], cq.t[:, 88:96], AF.Exp),
                  deps=[t2], label="efe")
        cq.add(t3)
        t4 = s.op("ve", _ts(cq.t[:, 88:96], cq.t[:, 88:96],
                            cm.meta[:, c:c + 1], None, ALU.mult),
                  deps=[t3, cm.tconst], label="efm")
        cq.add(t4)
        for m in range(8):
            t5 = s.op("ve", _stt(S.t[:, m * 64:(m + 1) * 64],
                                 Sg.t[:, m * 64:(m + 1) * 64],
                                 cq.t[:, 88 + m:89 + m],
                                 S.t[:, m * 64:(m + 1) * 64], ALU.mult,
                                 ALU.add), deps=[t4, tg] +
                      ([S.deps[-1]] if S.deps else []), label="sacc")
            S.add(t5)
            Sg.add(t5)
            cq.add(t5)
        tprev2 = S.deps[-1]
    tsb = s.op("ve", _copy(Sbf.t[:], S.t[:]), deps=[S.deps[-1]] + Sbf.w(),
               label="sbf")
    Sbf.add(tsb)
    S.add(tsb)

    store_toks = []
    for g in range(4):
        fm_toks, per_chunk = cm.group_fm(src_dram, g, src_deps=src_deps)
        wtoks = w_chain(fm_toks)
        for hq in range(2):
            revs = []

            def r_cb(m, ps, deps):
                t = s.op("se", _copy(rfm[m].t[:], ps),
                         deps=deps + rfm[m].w(), label="rev")
                rfm[m].add(t)
                revs.append(t)
                return t
            proj_fm(kb, cm, cm.xnf, hq * 256, 256, wA, [twr], fm_toks, r_cb)
            gevs = []

            def g_cb(m, ps, deps):
                t = s.op("se", _act(gfm[m].t[:], ps, AF.Sigmoid),
                         deps=deps + gfm[m].w(), label="gev")
                gfm[m].add(t)
                gevs.append(t)
                return t
            proj_fm(kb, cm, cm.xnf, hq * 256, 256, wB, [twg], fm_toks, g_cb)

            for c01 in range(2):
                c = hq * 2 + c01
                gc = g * 4 + c
                cc = c * 128
                rc = c01 * 128
                wkw = wkv.w()
                wkv_toks = []
                for mh in range(2):
                    teK, teql, tdcol = chunk_decay(c, mh, wtoks, False)
                    er_toks = []
                    for i in range(4):
                        m = mh * 4 + i
                        t1 = s.op("ve", _tt(Qt.t[:, i * 128:(i + 1) * 128],
                                            wp[m].t[:, cc:cc + 128],
                                            Qt.t[:, i * 128:(i + 1) * 128],
                                            ALU.subtract),
                                  deps=[teK], label="ers")
                        Qt.add(t1)
                        wp[m].add(t1)
                        er_toks.append(t1)
                    teR = s.op("se", _act(eR.t[:], Qt.t[:], AF.Exp),
                               deps=er_toks + eR.w(), label="er")
                    eR.add(teR)
                    Qt.add(teR)
                    ktoks, rtoks, kutoks = [], [], []
                    ktw, rtw, kuw = kt_.w(), rt_.w(), kut.w()
                    ktBw = ktB.w()
                    kuBw = kuB.w()
                    for i in range(4):
                        m = mh * 4 + i
                        sl = slice(i * 128, (i + 1) * 128)
                        kr_sl = kraw[m].t[:, g * 512 + cc:g * 512 + cc + 128]
                        krd = ([kraw[m].deps[-1]] if kraw[m].deps else [])
                        t1 = s.op("ve", _stt(kt_.t[:, sl], kr_sl,
                                             vec[:, 18:19], eK.t[:, sl],
                                             ALU.mult, ALU.mult),
                                  deps=[teK, tmu] + krd +
                                       (ktw if i == 0 else []),
                                  label="ktil")
                        kt_.add(t1)
                        kraw[m].add(t1)
                        eK.add(t1)
                        t1b = s.op("ve", _stt(ktB.t[:, sl], kr_sl,
                                              vec[:, 19:20], eK.t[:, sl],
                                              ALU.mult, ALU.mult),
                                   deps=[teK, tmu] + krd +
                                        (ktBw if i == 0 else []),
                                   label="ktilB")
                        ktB.add(t1b)
                        kraw[m].add(t1b)
                        eK.add(t1b)
                        ktoks.append((t1, t1b))
                        t2 = s.op("ve", _tt(rt_.t[:, sl],
                                            rfm[m].t[:, rc:rc + 128],
                                            eR.t[:, sl], ALU.mult),
                                  deps=[teR, revs[m]] +
                                       (rtw if i == 0 else []),
                                  label="rtil")
                        rt_.add(t2)
                        rfm[m].add(t2)
                        eR.add(t2)
                        rtoks.append(t2)
                        t3 = s.op("ve", _ts(kut.t[:, sl], kr_sl,
                                            vec[:, 20 + m:21 + m], None,
                                            ALU.mult),
                                  deps=[tmu] + krd +
                                       (kuw if i == 0 else []),
                                  label="kut")
                        kut.add(t3)
                        kraw[m].add(t3)
                        t3b = s.op("ve", _stt(kuB.t[:, sl], kr_sl,
                                              vec[:, 8 + m:9 + m],
                                              kut.t[:, sl],
                                              ALU.mult, ALU.subtract),
                                   deps=[t3] + krd +
                                        (kuBw if i == 0 else []),
                                   label="kutB")
                        kuB.add(t3b)
                        kut.add(t3b)
                        kraw[m].add(t3b)
                        kutoks.append((t3, t3b))
                    rh_toks = []
                    khw0 = kh.w()
                    for i in range(4):
                        m = mh * 4 + i
                        sl = slice(i * 128, (i + 1) * 128)
                        t4 = s.op("ve", _ts(kh.t[:, sl], rt_.t[:, sl],
                                            coltmp.t[:, 40 + m:41 + m], None,
                                            ALU.mult),
                                  deps=[rtoks[i], coltmp.deps[-1]] +
                                       (khw0 if i == 0 else []), label="rhat")
                        kh.add(t4)
                        rt_.add(t4)
                        coltmp.add(t4)
                        rh_toks.append(t4)
                    for i in range(4):
                        m = mh * 4 + i
                        sl = slice(i * 128, (i + 1) * 128)
                        bsc = cm.bank()
                        ps1 = bsc.t
                        b1 = bsc.w()
                        sA = s.op("pe", _mm(ps1[:, 0:128],
                                            kt_.t[:, sl],
                                            rt_.t[:, sl],
                                            start=True, stop=True),
                                  deps=[ktoks[i][0], rtoks[i]] + b1,
                                  label="scA")
                        sB = s.op("pe", _mm(ps1[:, 128:256],
                                            ktB.t[:, sl],
                                            rt_.t[:, sl],
                                            start=True, stop=True),
                                  deps=[ktoks[i][1]], label="scB")
                        kt_.add(sA)
                        ktB.add(sB)
                        rt_.add(sA, sB)
                        bsc2 = cm.bank()
                        ps2 = bsc2.t
                        b2 = bsc2.w()
                        dA = s.op("pe", _mm(ps2[:, 0:128],
                                            kut.t[:, sl],
                                            rfm[m].t[:, rc:rc + 128],
                                            start=True, stop=True),
                                  deps=[kutoks[i][0], revs[m]] + b2,
                                  label="sdA")
                        dB = s.op("pe", _mm(ps2[:, 128:256],
                                            kuB.t[:, sl],
                                            rfm[m].t[:, rc:rc + 128],
                                            start=True, stop=True),
                                  deps=[kutoks[i][1]], label="sdB")
                        kut.add(dA)
                        kuB.add(dB)
                        rfm[m].add(dA, dB)
                        e1 = s.op("ve", _stt(scb.t[:], ps1[:, 0:256], 1.0,
                                             cm.msl2[:], ALU.bypass,
                                             ALU.mult),
                                  deps=[sA, sB, cm.tconst] + scb.w(),
                                  label="scev")
                        bsc.add(e1)
                        scb.add(e1)
                        e2 = s.op("ve", _stt(sc2b.t[:], ps2[:, 0:256], 1.0,
                                             cm.idd2[:], ALU.bypass,
                                             ALU.mult),
                                  deps=[dA, dB, cm.tconst] + sc2b.w(),
                                  label="sc2ev")
                        bsc2.add(e2)
                        sc2b.add(e2)
                        bo = cm.bank()
                        pso = bo.t
                        b3 = bo.w()
                        vdep = [vst[gc].deps[-1]] if vst[gc].deps else []
                        o1 = s.op("pe", _mm(pso[0:64, 0:128],
                                            vst[gc].t[:, (2 * m) * 64:
                                                      (2 * m + 1) * 64],
                                            scb.t[:, 0:128], start=True,
                                            stop=False, tp=(0, 0)),
                                  deps=[e1] + vdep + b3, label="oiA")
                        o2 = s.op("pe", _mm(pso[0:64, 0:128],
                                            vst[gc].t[:, (2 * m) * 64:
                                                      (2 * m + 1) * 64],
                                            sc2b.t[:, 0:128], start=False,
                                            stop=False, tp=(0, 0)),
                                  deps=[e2], label="odA")
                        o3 = s.op("pe", _mm(pso[0:64, 0:128],
                                            Sbf.t[0:64, m * 64:(m + 1) * 64],
                                            kh.t[0:64, sl],
                                            start=False, stop=True,
                                            tp=(0, 0)),
                                  deps=[rh_toks[i]] +
                                       ([Sbf.deps[-1]] if Sbf.deps else []),
                                  label="onA")
                        o4 = s.op("pe", _mm(pso[64:128, 0:128],
                                            vst[gc].t[:, (2 * m + 1) * 64:
                                                      (2 * m + 2) * 64],
                                            scb.t[:, 128:256], start=True,
                                            stop=False, tp=(0, 64)),
                                  deps=[e1], label="oiB")
                        o5 = s.op("pe", _mm(pso[64:128, 0:128],
                                            vst[gc].t[:, (2 * m + 1) * 64:
                                                      (2 * m + 2) * 64],
                                            sc2b.t[:, 128:256], start=False,
                                            stop=False, tp=(0, 64)),
                                  deps=[e2], label="odB")
                        o6 = s.op("pe", _mm(pso[64:128, 0:128],
                                            Sbf.t[64:128,
                                                  m * 64:(m + 1) * 64],
                                            kh.t[64:128, sl],
                                            start=False, stop=True,
                                            tp=(64, 64)),
                                  deps=[], label="onB")
                        vst[gc].add(o1, o2, o4, o5)
                        scb.add(o1, o4)
                        sc2b.add(o2, o5)
                        Sbf.add(o3, o6)
                        kh.add(o3, o6)
                        ev = s.op("ve", _copy(wkv.t[:, m * 128:(m + 1) * 128],
                                              pso[:, 0:128]),
                                  deps=[o3, o6] +
                                       (wkw if (mh == 0 and i == 0) else []),
                                  label="wkvev")
                        bo.add(ev)
                        wkv.add(ev)
                        wkv_toks.append(ev)
                    chunk_state(g, c, mh, teK, teql, tdcol)
                tsb2 = s.op("ve", _copy(Sbf.t[:], S.t[:]),
                            deps=[S.deps[-1]] + Sbf.w(), label="sbf2")
                Sbf.add(tsb2)
                S.add(tsb2)

                sqv = cm.sq()
                tsq = s.op("se", _act(sqv.t[:], wkv.t[:], AF.Square),
                           deps=[wkv.deps[-1]] + sqv.w(), label="gnsq")
                sqv.add(tsq)
                wkv.add(tsq)
                bst = cm.bank()
                psst = bst.t
                bstw = bst.w()
                tmm = None
                for p in range(8):
                    tmm = s.op("pe", _mm(psst[0:16, 0:128],
                                         indt[:, p * 16:(p + 1) * 16],
                                         wkv.t[:, p * 128:(p + 1) * 128],
                                         start=(p == 0), stop=False),
                               deps=([wkv.deps[-1], tind] + bstw)
                               if p == 0 else [], label="gns")
                for p in range(8):
                    tmm = s.op("pe", _mm(psst[0:16, 128:256],
                                         indt[:, p * 16:(p + 1) * 16],
                                         sqv.t[:, p * 128:(p + 1) * 128],
                                         start=False, stop=(p == 7)),
                               deps=[tsq] if p == 0 else [], label="gns2")
                wkv.add(tmm)
                sqv.add(tmm)
                c1 = s.op("ve", _ts(stf.t[0:16, SOFF:SOFF + 128], psst[0:16, 0:128],
                                    1.0 / 64, None, ALU.mult),
                          deps=[tmm] + stf.w(), label="gm")
                stf.add(c1)
                c2 = s.op("ve", _ts(stf.t[0:16, SOFF + 128:SOFF + 256], psst[0:16, 128:256],
                                    1.0 / 64, None, ALU.mult), deps=[tmm],
                          label="gq")
                stf.add(c2)
                bst.add(c1, c2)
                c3 = s.op("ve", _tt(stf.t[0:16, SOFF + 256:SOFF + 384],
                                    stf.t[0:16, SOFF:SOFF + 128],
                                    stf.t[0:16, SOFF:SOFF + 128],
                                    ALU.mult), deps=[c1],
                          label="gm2")
                stf.add(c3)
                c4 = s.op("ve", _tt(stf.t[0:16, SOFF + 256:SOFF + 384],
                                    stf.t[0:16, SOFF + 128:SOFF + 256],
                                    stf.t[0:16, SOFF + 256:SOFF + 384],
                                    ALU.subtract),
                          deps=[c2, c3], label="gvar")
                stf.add(c4)
                c5 = s.op("se", _act(stf.t[0:16, SOFF + 256:SOFF + 384],
                                     stf.t[0:16, SOFF + 256:SOFF + 384],
                                     AF.Sqrt, bias=cm.epscol[0:16, 0:1]),
                          deps=[c4, cm.tepc], label="gsd")
                stf.add(c5)
                c6 = s.op("ve", _recip(stf.t[0:16, SOFF + 256:SOFF + 384],
                                       stf.t[0:16, SOFF + 256:SOFF + 384]),
                          deps=[c5], label="grs")
                stf.add(c6)
                c7 = s.op("ve", _copy(st16.t[0:16, G16:G16 + 128],
                                      stf.t[0:16, SOFF + 256:SOFF + 384]),
                          deps=[c6] + st16.w(), label="gc1")
                st16.add(c7)
                c8 = s.op("ve", _tt(stf.t[0:16, SOFF:SOFF + 128],
                                    stf.t[0:16, SOFF:SOFF + 128],
                                    stf.t[0:16, SOFF + 256:SOFF + 384],
                                    ALU.mult), deps=[c6],
                          label="gmr")
                stf.add(c8)
                c9 = s.op("ve", _copy(st16.t[0:16, G16 + 128:G16 + 256],
                                      stf.t[0:16, SOFF:SOFF + 128]),
                          deps=[c8], label="gc2")
                st16.add(c9)
                stf.add(c9)
                ggw = gng.w()
                ga4s = []
                for mh2 in range(2):
                    bA = cm.bank()
                    psA = bA.t
                    bAw = bA.w()
                    bB = cm.bank()
                    psB = bB.t
                    bBw = bB.w()
                    ta = tb_ = None
                    for i in range(4):
                        p = mh2 * 4 + i
                        ta = s.op("pe", _mm(psA[:, i * 128:(i + 1) * 128],
                                            bct[:, p * 128:(p + 1) * 128],
                                            st16.t[0:16, G16:G16 + 128],
                                            start=(i == 0), stop=(i == 3)),
                                  deps=([c7, tbc] + bAw) if i == 0 else [],
                                  label="gbA")
                        tb_ = s.op("pe", _mm(psB[:, i * 128:(i + 1) * 128],
                                             bct[:, p * 128:(p + 1) * 128],
                                             st16.t[0:16,
                                                    G16 + 128:G16 + 256],
                                             start=(i == 0), stop=(i == 3)),
                                   deps=([c9, tbc] + bBw) if i == 0 else [],
                                   label="gbB")
                    st16.add(ta, tb_)
                    a1 = s.op("ve", _tt(gnt.t[:, 0:512],
                                        wkv.t[:, mh2 * 512:
                                              mh2 * 512 + 512],
                                        psA[:, 0:512], ALU.mult),
                              deps=[ta, wkv.deps[-1]] + gnt.w(), label="ga1")
                    gnt.add(a1)
                    bA.add(a1)
                    wkv.add(a1)
                    a2 = s.op("ve", _tt(gnt.t[:, 0:512], gnt.t[:, 0:512],
                                        psB[:, 0:512], ALU.subtract),
                              deps=[a1, tb_], label="ga2")
                    gnt.add(a2)
                    bB.add(a2)
                    a3 = s.op("ve", _ts(gnt.t[:, 0:512], gnt.t[:, 0:512],
                                        vec[:, 16:17], vec[:, 17:18],
                                        ALU.mult, ALU.add),
                              deps=[a2, tvec], label="ga3")
                    gnt.add(a3)
                    for i in range(4):
                        p = mh2 * 4 + i
                        a4 = s.op("ve", _tt(gng.t[:, p * 128:(p + 1) * 128],
                                            gnt.t[:, i * 128:(i + 1) * 128],
                                            gfm[p].t[:, rc:rc + 128],
                                            ALU.mult),
                                  deps=[a3, gevs[p]] +
                                       (ggw if p == 0 else []),
                                  label="ga4")
                        gng.add(a4)
                        gnt.add(a4)
                        gfm[p].add(a4)
                        ga4s.append(a4)
                ob = oti
                obw = ob.w()
                o_toks = []
                for nn_ in range(2):
                    bank = cm.bank()
                    ps = bank.t
                    deps = bank.w() + [gng.deps[-1], two]
                    tmm = None
                    for p in range(8):
                        tmm = s.op("pe", _mm(ps[:, 0:512],
                                             gng.t[:, p * 128:(p + 1) * 128],
                                             wC.t[:, p * D + nn_ * 512:
                                                  p * D + nn_ * 512 + 512],
                                             start=(p == 0), stop=(p == 7)),
                                   deps=deps if p == 0 else [], label="wom")
                    wC.add(tmm)
                    gng.add(tmm)
                    tev = s.op("se", _copy(ob.t[:, nn_ * 512:(nn_ + 1) * 512],
                                           ps[:, 0:512]),
                               deps=[tmm] + (obw if nn_ == 0 else []),
                               label="woev")
                    bank.add(tev)
                    ob.add(tev)
                    o_toks.append(tev)
                tst, _ = epilogue_store(kb, cm, ob, o_toks, c, dst_dram, gc)
                store_toks.append(tst)
    return store_toks



def build(blocks):
    nc = bass.Bass("TRN2", target_bir_lowering=False)
    kb = KB(nc)
    s = kb.s

    x_h = kb.dram_in("x", [TCORE, D])
    y_h = kb.dram_out("y", [TCORE, D])
    kb.dram_in("meta_in", [128, 32])
    if "l1f" in blocks:
        kb.dram_in("l1f_wrT", [2 * D, 512])
        kb.dram_in("l1f_wkT", [2 * D, 512])
        kb.dram_in("l1f_wvT", [2 * D, 512])
        kb.dram_in("l1f_mu", [128, 16])
    if "l0a" in blocks:
        for nm in ("wr", "wk", "wv", "wg", "wo"):
            kb.dram_in(f"l0a_{nm}T", [2 * D, 512])
        kb.dram_in("l0a_ldT", [D, 64])
        kb.dram_in("l0a_luT", [64, D])
        kb.dram_in("l0a_vec", [128, 32])
    if "l0f" in blocks:
        kb.dram_in("l0f_w1T", [64 * 128, 512])
        kb.dram_in("l0f_w2T", [64 * 128, 512])
        kb.dram_in("l0f_cw", [128, 128])
    if "l1a" in blocks:
        for nm in ("wq", "wk", "wv", "wo"):
            kb.dram_in(f"l1a_{nm}T", [2 * D, 512])
        kb.dram_in("l1a_bqk", [128, 16])
        kb.dram_in("l1a_bvo", [128, 2 * D])

    bufs = [x_h]
    for i in range(len(blocks) - 1):
        bufs.append(kb.dram_tmp(f"xb{i + 1}", [TCORE, D]))
    bufs.append(y_h)

    cm = Common(kb)
    cm.setup()

    from contextlib import ExitStack
    all_store = []
    prev_st = None
    l0f_fin = None
    for bi, bl in enumerate(blocks):
        src, dst = bufs[bi], bufs[bi + 1]
        kb.scope = ExitStack()
        kb.small_sem = f"dms{bi}"
        s.batch_sems.add(kb.small_sem)
        if bl == "l1f":
            prep_l1f(kb, cm)
            st = emit_l1f(kb, cm, src, dst, src_deps=prev_st)
        elif bl == "l1a":
            prep_l1a(kb, cm)
            st = emit_l1a(kb, cm, src, dst, src_deps=prev_st)
        elif bl == "l0f":
            ged = gather_edge_rows(kb, cm, src, idx=1, src_deps=prev_st)
            prep_l0f(kb, cm, pass1_finish=l0f_fin)
            st = emit_l0f(kb, cm, src, dst, src_deps=prev_st, gather=ged)
        elif bl == "l0a":
            prep_l0a(kb, cm)
            pump = pdrain = None
            if bi + 1 < len(blocks) and blocks[bi + 1] == "l0f":
                pump, pdrain, l0f_fin = make_l0f_pass1(kb, cm)
            st = emit_l0a(kb, cm, src, dst, src_deps=prev_st, pump=pump,
                          pump_drain=pdrain)
        else:
            raise NotImplementedError(bl)
        s.barrier()
        kb.scope.close()
        kb.scope = None
        all_store += st
        prev_st = st

    s.op("sp", _nop(), deps=all_store, label="fin")

    with nc.Block() as block:
        s.emit(block)
    return nc



def _blk(wT, cc=512):
    wT = np.ascontiguousarray(wT)
    R, C = wT.shape
    cc = min(C, cc)
    return np.ascontiguousarray(
        wT.reshape(R // 128, 128, C // cc, cc).transpose(0, 2, 1, 3)
        .reshape(-1, cc))


def _t(a):
    return np.ascontiguousarray(np.asarray(a, dtype=np.float32).T)


def _fmcols(v, ncol):
    v = np.asarray(v, dtype=np.float32).reshape(ncol, 128).T
    return np.ascontiguousarray(v)


def kernel(L0_attn_x, L0_attn_w_r, L0_attn_w_k, L0_attn_w_v, L0_attn_w_o,
           L0_attn_w_g, L0_attn_w_base, L0_attn_w_lora_down, L0_attn_w_lora_up,
           L0_attn_u, L0_attn_ln_gamma, L0_attn_ln_beta, L0_ffn_w1,
           L0_ffn_conv_w, L0_ffn_conv_b, L0_ffn_w2, L1_attn_Wq, L1_attn_bq,
           L1_attn_Wk, L1_attn_bk, L1_attn_Wv, L1_attn_bv, L1_attn_Wo,
           L1_attn_bo, L1_ffn_W_r, L1_ffn_W_k, L1_ffn_W_v, L1_ffn_mu_r,
           L1_ffn_mu_k):
    blocks = list(BLOCKS)
    x = np.ascontiguousarray(np.asarray(L0_attn_x, dtype=np.float32)
                             .reshape(2 * 8192, D))
    shards = [x[c * TCORE:(c + 1) * TCORE] for c in range(NCORES)]

    nc = build(blocks)

    shared = {}
    if "l0a" in blocks:
        shared["l0a_wrT"] = _blk(_t(L0_attn_w_r))
        shared["l0a_wkT"] = _blk(_t(L0_attn_w_k))
        shared["l0a_wvT"] = _blk(_t(L0_attn_w_v))
        shared["l0a_wgT"] = _blk(_t(L0_attn_w_g))
        shared["l0a_woT"] = _blk(_t(L0_attn_w_o))
        shared["l0a_ldT"] = _blk(_t(L0_attn_w_lora_down), cc=64)
        shared["l0a_luT"] = _t(L0_attn_w_lora_up)
        av = np.zeros((128, 32), dtype=np.float32)
        av[:, 0:8] = _fmcols(L0_attn_w_base, 8)
        u = np.asarray(L0_attn_u, np.float32)
        up = np.zeros((128, 8), dtype=np.float32)
        for h in range(16):
            up[(h % 2) * 64:(h % 2) * 64 + 64, h // 2] = u[h]
        av[:, 8:16] = up
        av[:, 16] = np.tile(np.asarray(L0_attn_ln_gamma, np.float32), 2)
        av[:, 17] = np.tile(np.asarray(L0_attn_ln_beta, np.float32), 2)
        shared["l0a_vec"] = av
    if "l0f" in blocks:
        shared["l0f_w1T"] = _blk(_t(L0_ffn_w1))
        shared["l0f_w2T"] = _blk(_t(L0_ffn_w2))
        cw = np.zeros((128, 128), dtype=np.float32)
        cwa = np.asarray(L0_ffn_conv_w, np.float32)
        for t in range(3):
            cw[:, t * 32:(t + 1) * 32] = _fmcols(cwa[:, 0, t], 32)
        cw[:, 96:128] = _fmcols(L0_ffn_conv_b, 32)
        shared["l0f_cw"] = cw
    if "l1a" in blocks:
        shared["l1a_wqT"] = _blk(_t(L1_attn_Wq))
        shared["l1a_wkT"] = _blk(_t(L1_attn_Wk))
        shared["l1a_wvT"] = _blk(_t(L1_attn_Wv))
        shared["l1a_woT"] = _blk(_t(L1_attn_Wo))
        bqk = np.zeros((128, 16), dtype=np.float32)
        bqk[:, 0:8] = _fmcols(L1_attn_bq, 8)
        bqk[:, 8:16] = _fmcols(L1_attn_bk, 8)
        shared["l1a_bqk"] = bqk
        bvo = np.zeros((128, 2 * D), dtype=np.float32)
        bvo[:, 0:D] = np.tile(np.asarray(L1_attn_bv, np.float32)[None, :],
                              (128, 1))
        bvo[:, D:] = np.tile(np.asarray(L1_attn_bo, np.float32)[None, :],
                             (128, 1))
        shared["l1a_bvo"] = bvo
    if "l1f" in blocks:
        shared["l1f_wrT"] = _blk(_t(L1_ffn_W_r))
        shared["l1f_wkT"] = _blk(_t(L1_ffn_W_k))
        shared["l1f_wvT"] = _blk(_t(L1_ffn_W_v))
        mu = np.zeros((128, 16), dtype=np.float32)
        mu[:, 0:8] = _fmcols(L1_ffn_mu_r, 8)
        mu[:, 8:16] = _fmcols(L1_ffn_mu_k, 8)
        shared["l1f_mu"] = mu

    in_maps = []
    for c in range(NCORES):
        meta = np.zeros((128, 32), dtype=np.float32)
        grp = c // 4
        for cc in range(NCORES):
            if cc // 4 == grp and cc < c:
                meta[:, cc] = 1.0
        if c % 4 != 0:
            meta[c - 1, 8] = 1.0
            meta[2 * (c - 1) + 1, 9] = 1.0
        if c % 4 != 3:
            meta[2 * (c + 1), 10] = 1.0
        for cc in range(NCORES):
            if cc < c:
                meta[:, 16 + cc] = 1.0
        m = {"x": shards[c], "meta_in": meta}
        m.update(shared)
        in_maps.append(m)

    res = run_bass_kernel_spmd(nc, in_maps, core_ids=list(range(NCORES)))
    out = np.concatenate([res.results[c]["y"] for c in range(NCORES)], axis=0)
    return out.reshape(2, 8192, D)


# version 14 (final)
# speedup vs baseline: 1.0236x; 1.0236x over previous
import numpy as np
import ml_dtypes
import concourse.bass as bass
import concourse.mybir as mybir
from concourse.bass_utils import run_bass_kernel_spmd

F32 = mybir.dt.float32
BF16 = mybir.dt.bfloat16
AF = mybir.ActivationFunctionType
ALU = mybir.AluOpType
AX = mybir.AxisListType
BF = ml_dtypes.bfloat16
FP8 = mybir.dt.float8e4

NCORES = 8
D = 1024
H = 16
DH = 64
TCORE = 2048
NCH = TCORE // 128
MAGIC = 12582912.0
LN_EPS = 1e-5

BLOCKS = ["l0a", "l0f", "l1a", "l1f"]
L0A_DEBUG = 0

ENGS = ("pe", "ve", "se", "gp", "sp")



class Op:
    __slots__ = ("eng", "fn", "deps", "sem", "n", "val", "needed", "label")

    def __init__(self, eng, fn, deps, sem, n, label):
        self.eng = eng
        self.fn = fn
        self.deps = deps
        self.sem = sem
        self.n = n
        self.val = None
        self.needed = False
        self.label = label


class Sched:
    def __init__(self, nc):
        self.nc = nc
        self.ops = []
        self.batch_sems = {"dms"}
        self.floor = {}

    def op(self, eng, fn, deps=(), sem=None, n=None, label=""):
        deps = [d for d in deps if d is not None]
        f = self.floor.get(eng)
        if f is not None:
            deps.append(f)
        if n is None:
            n = 16 if (sem is not None and sem[0] == "d") else 1
        o = Op(eng, fn, deps, sem if sem else eng, n, label)
        self.ops.append(o)
        return o

    def barrier(self):
        last = {}
        for o in self.ops:
            last[o.sem] = o
        toks = [o for o in last.values()]
        self.floor = {}
        for e in ENGS:
            self.floor[e] = self.op(e, _nop(), deps=toks, label="bar")

    def emit(self, block):
        for o in self.ops:
            if o.sem[0] == "d" or o.sem in self.batch_sems:
                o.needed = True
            for d in o.deps:
                d.needed = True
        cnt = {}
        for o in self.ops:
            if o.needed:
                cnt[o.sem] = cnt.get(o.sem, 0) + o.n
                o.val = cnt[o.sem]
        for name in self.batch_sems & set(cnt):
            total = cnt[name]
            for o in self.ops:
                if o.sem == name:
                    o.val = total
        sems = {name: self.nc.alloc_semaphore(f"s_{name}") for name in cnt}
        ops_by_eng = {e: [o for o in self.ops if o.eng == e] for e in ENGS}
        engmap = {"pe": block.tensor, "ve": block.vector, "se": block.scalar,
                  "gp": block.gpsimd, "sp": block.sync}

        def make_body(eng_name, eng_ops):
            def body(engine):
                waited = {}
                for o in eng_ops:
                    need = {}
                    for d in o.deps:
                        if need.get(d.sem, 0) < d.val:
                            need[d.sem] = d.val
                    for sm, v in need.items():
                        if waited.get(sm, 0) < v:
                            engine.wait_ge(sems[sm], v)
                            waited[sm] = v
                    inst = o.fn(engine)
                    if o.needed:
                        if inst is None:
                            raise RuntimeError(f"op {o.label} no inst")
                        inst.then_inc(sems[o.sem], o.n)
            return body

        for e in ENGS:
            if ops_by_eng[e]:
                engmap[e](make_body(e, ops_by_eng[e]))


class TBuf:

    def __init__(self, t):
        self.t = t
        self.deps = []

    def w(self):
        d = self.deps
        self.deps = []
        return d

    def add(self, *toks):
        for t in toks:
            if t is not None:
                self.deps.append(t)



def _dma(dst, src, slow=False):
    if slow:
        return lambda e: e.dma_start(out=dst, in_=src,
                                     allow_slow_non_contiguous=True)
    return lambda e: e.dma_start(out=dst, in_=src)


def _copy(dst, src):
    def f(e):
        if hasattr(e, "tensor_copy"):
            return e.tensor_copy(dst, src)
        return e.copy(dst, src)
    return f


def _act(dst, src, func, bias=0.0, scale=1.0, accum=None):
    def f(e):
        kw = {}
        if accum is not None:
            kw["accum_out"] = accum
        return e.activation(dst, src, func, bias=bias, scale=scale, **kw)
    return f


def _ts(dst, src, s1, s2, op0, op1=ALU.bypass):
    return lambda e: e.tensor_scalar(dst, src, s1, s2, op0, op1)


def _tt(dst, a, b, op):
    return lambda e: e.tensor_tensor(dst, a, b, op)


def _stt(dst, in0, scalar, in1, op0, op1):
    return lambda e: e.scalar_tensor_tensor(dst, in0, scalar, in1, op0, op1)


def _mm(out, lhsT, rhs, start, stop, tp=None):
    return lambda e: e.matmul(out, lhsT, rhs, start=start, stop=stop,
                              tile_position=tp)


def _tr(out, in_, ident):
    return lambda e: e.transpose(out, in_, ident)


def _red(dst, src, op, absv=False):
    return lambda e: e.tensor_reduce(dst, src, AX.X, op,
                                     apply_absolute_value=absv)


def _recip(dst, src):
    return lambda e: e.reciprocal(dst, src)


def _memset(dst, v):
    return lambda e: e.memset(dst, v)


def _scan_add(dst, src, zeros):
    return lambda e: e.tensor_tensor_scan(dst, src, zeros, 0.0, ALU.add,
                                          ALU.bypass)


def _cc_allgather(in_ap, out_ap):
    def f(e):
        return e.collective_compute(
            "AllGather", ALU.bypass,
            replica_groups=[list(range(NCORES))],
            ins=[in_ap], outs=[out_ap])
    return f


def _nop():
    return lambda e: e.nop()



class KB:
    def __init__(self, nc):
        self.nc = nc
        self.s = Sched(nc)
        self.sb = {}
        self.dram = {}
        self.wtok = {}
        self.scope = None
        self.small_sem = "dms"

    def sbuf(self, name, shape, dtype):
        if self.scope is not None:
            t = self.scope.enter_context(
                self.nc.sbuf_tensor(name, list(shape), dtype))
        else:
            t = self.nc.alloc_sbuf_tensor(name, list(shape), dtype)
        self.sb[name] = t
        return t

    def tbuf(self, name, shape, dtype):
        return TBuf(self.sbuf(name, shape, dtype))

    def dram_in(self, name, shape, dtype=F32):
        t = self.nc.dram_tensor(name, list(shape), dtype, kind="ExternalInput")
        self.dram[name] = t
        return t

    def dram_out(self, name, shape, dtype=F32):
        t = self.nc.dram_tensor(name, list(shape), dtype, kind="ExternalOutput")
        self.dram[name] = t
        return t

    def dram_tmp(self, name, shape, dtype=F32):
        t = self.nc.dram_tensor(name, list(shape), dtype, kind="Internal")
        self.dram[name] = t
        return t

    def inline(self, name, arr):
        return self.nc.inline_tensor(np.ascontiguousarray(arr), name=name)


class Common:
    def __init__(self, kb: KB):
        self.kb = kb
        nc = kb.nc
        self.ident_h = kb.inline("c_ident", np.eye(128, dtype=BF))
        self.ident = kb.sbuf("ident", [128, 128], BF16)
        self.onescol_h = kb.inline("c_ones128", np.ones((128, 1), dtype=BF))
        self.onescol = kb.sbuf("ones128", [128, 1], BF16)
        ml = np.triu(np.ones((128, 128), dtype=BF), 0)
        self.ml1_h = kb.inline("c_ml1", ml)
        self.ml1 = kb.sbuf("ml1", [128, 128], BF16)
        msl = np.triu(np.ones((128, 128), dtype=BF), 1)
        self.msl2_h = kb.inline("c_msl2", np.concatenate([msl, msl], axis=1))
        self.msl2 = kb.sbuf("msl2", [128, 256], BF16)
        eye = np.eye(128, dtype=BF)
        self.idd2_h = kb.inline("c_idd2", np.concatenate([eye, eye], axis=1))
        self.idd2 = kb.sbuf("idd2", [128, 256], BF16)

        self.meta = kb.sbuf("meta", [128, 32], F32)
        self.l0f_qc = kb.tbuf("l0f_qc", [128, 16], F32)
        self.wq80 = kb.tbuf("wq80", [128, 512], FP8)
        self.onesf_h = kb.inline("c_ones128f",
                                 np.ones((128, 128), dtype=np.float32))
        self.onesf = kb.sbuf("onesf", [128, 128], F32)

        self.x32 = [kb.tbuf(f"x32_{i}", [128, D], F32) for i in range(4)]
        self.xn = [kb.tbuf(f"xn_{i}", [128, D], BF16) for i in range(4)]
        self.xnf = [kb.tbuf(f"xnf_{k}", [128, 512], BF16) for k in range(8)]
        self.col = kb.tbuf("ln_cols", [128, 16], F32)
        self.col2 = kb.tbuf("ep_cols", [128, 16], F32)
        self.colg = kb.tbuf("g_cols", [1, 8], F32)
        sq0 = kb.tbuf("sqp0", [128, D], BF16)
        self.sqp = [sq0, sq0]
        self._sqi = 0
        self.zeros = kb.sbuf("z_zero", [128, 128], F32)
        ws0 = kb.tbuf("wstage0", [128, 512], F32)
        ws1 = kb.tbuf("wstage1", [128, 512], F32)
        self.wstage = [ws0, ws1]
        self.pb = [TBuf(nc.alloc_psum_tensor(f"pb{i}", [128, 512], F32))
                   for i in range(6)]
        self.pt = [TBuf(nc.alloc_psum_tensor(f"pt{i}", [128, 1024], BF16))
                   for i in range(2)]
        self._rr = 0
        self._rrt = 0

    def setup(self):
        s = self.kb.s
        t = None
        for src_h, dst in ((self.ident_h, self.ident),
                           (self.onescol_h, self.onescol),
                           (self.ml1_h, self.ml1),
                           (self.msl2_h, self.msl2), (self.idd2_h, self.idd2),
                           (self.onesf_h, self.onesf),
                           (self.kb.dram["meta_in"], self.meta)):
            t = s.op("sp", _dma(dst[:], src_h.ap()), sem="dms", label="const")
        self.tconst = t
        self.tz = s.op("ve", _memset(self.zeros[:], 0.0), label="zeros")
        self.epscol = self.kb.sbuf("epscol", [128, 1], F32)
        self.tepc = s.op("ve", _memset(self.epscol[:], LN_EPS), label="eps")

    def bank(self):
        b = self.pb[self._rr % len(self.pb)]
        self._rr += 1
        return b

    def tbank(self):
        b = self.pt[self._rrt % len(self.pt)]
        self._rrt += 1
        return b

    def sq(self):
        b = self.sqp[self._sqi % len(self.sqp)]
        self._sqi += 1
        return b

    def ln_load(self, src_dram, chunk, slot, src_deps=None):
        s = self.kb.s
        x32 = self.x32[slot]
        xn = self.xn[slot]
        col = self.col
        c0 = 4 * slot
        sqb = self.sq()
        tload = s.op("sp", _dma(x32.t[:],
                                src_dram.ap()[chunk * 128:(chunk + 1) * 128, :]),
                     deps=x32.w() + ([src_deps[chunk]] if src_deps else []),
                     sem=f"dxi{slot}", label=f"ld{chunk}")
        x32.add(tload)
        t1 = s.op("ve", _red(col.t[:, c0:c0 + 1], x32.t[:], ALU.add),
                  deps=[tload] + col.w())
        sqw = sqb.w()
        t2 = s.op("se", _act(sqb.t[:], x32.t[:], AF.Square,
                             accum=col.t[:, c0 + 1:c0 + 2]),
                  deps=[tload] + sqw, label="lnsq")
        sqb.add(t2)
        t3 = s.op("ve", _ts(col.t[:, c0:c0 + 1], col.t[:, c0:c0 + 1], 1.0 / D,
                            None, ALU.mult), deps=[t1])
        t4 = s.op("ve", _stt(col.t[:, c0 + 2:c0 + 3], col.t[:, c0:c0 + 1], 0.0,
                             col.t[:, c0:c0 + 1], ALU.bypass, ALU.mult),
                  deps=[t3])
        t5 = s.op("ve", _ts(col.t[:, c0 + 1:c0 + 2], col.t[:, c0 + 1:c0 + 2],
                            1.0 / D, None, ALU.mult), deps=[t2])
        t6 = s.op("ve", _tt(col.t[:, c0 + 2:c0 + 3], col.t[:, c0 + 1:c0 + 2],
                            col.t[:, c0 + 2:c0 + 3], ALU.subtract),
                  deps=[t4, t5])
        t7 = s.op("se", _act(col.t[:, c0 + 3:c0 + 4], col.t[:, c0 + 2:c0 + 3],
                             AF.Sqrt, bias=self.epscol[:, 0:1]),
                  deps=[t6, self.tepc])
        t8 = s.op("ve", _recip(col.t[:, c0 + 3:c0 + 4], col.t[:, c0 + 3:c0 + 4]),
                  deps=[t7])
        t9 = s.op("ve", _ts(xn.t[:], x32.t[:], col.t[:, c0:c0 + 1],
                            col.t[:, c0 + 3:c0 + 4], ALU.subtract, ALU.mult),
                  deps=[t8, t3] + xn.w(), label="lnapply")
        xn.add(t9)
        x32.add(t9)
        col.add(t9)
        return t9, tload

    def ln_load_f32(self, src_dram, chunk, slot, dstf, src_deps=None):
        s = self.kb.s
        x32 = self.x32[slot]
        col = self.col
        c0 = 4 * slot
        sqb = self.sq()
        tload = s.op("sp", _dma(x32.t[:],
                                src_dram.ap()[chunk * 128:(chunk + 1) * 128, :]),
                     deps=x32.w() + ([src_deps[chunk]] if src_deps else []),
                     sem=f"dxi{slot}", label=f"lq{chunk}")
        x32.add(tload)
        t1 = s.op("ve", _red(col.t[:, c0:c0 + 1], x32.t[:], ALU.add),
                  deps=[tload] + col.w())
        sqw = sqb.w()
        t2 = s.op("se", _act(sqb.t[:], x32.t[:], AF.Square,
                             accum=col.t[:, c0 + 1:c0 + 2]),
                  deps=[tload] + sqw, label="lnsq")
        sqb.add(t2)
        t3 = s.op("ve", _ts(col.t[:, c0:c0 + 1], col.t[:, c0:c0 + 1], 1.0 / D,
                            None, ALU.mult), deps=[t1])
        t4 = s.op("ve", _stt(col.t[:, c0 + 2:c0 + 3], col.t[:, c0:c0 + 1], 0.0,
                             col.t[:, c0:c0 + 1], ALU.bypass, ALU.mult),
                  deps=[t3])
        t5 = s.op("ve", _ts(col.t[:, c0 + 1:c0 + 2], col.t[:, c0 + 1:c0 + 2],
                            1.0 / D, None, ALU.mult), deps=[t2])
        t6 = s.op("ve", _tt(col.t[:, c0 + 2:c0 + 3], col.t[:, c0 + 1:c0 + 2],
                            col.t[:, c0 + 2:c0 + 3], ALU.subtract),
                  deps=[t4, t5])
        t7 = s.op("se", _act(col.t[:, c0 + 3:c0 + 4], col.t[:, c0 + 2:c0 + 3],
                             AF.Sqrt, bias=self.epscol[:, 0:1]),
                  deps=[t6, self.tepc])
        t8 = s.op("ve", _recip(col.t[:, c0 + 3:c0 + 4], col.t[:, c0 + 3:c0 + 4]),
                  deps=[t7])
        t9 = s.op("ve", _ts(dstf.t[:], x32.t[:], col.t[:, c0:c0 + 1],
                            col.t[:, c0 + 3:c0 + 4], ALU.subtract, ALU.mult),
                  deps=[t8, t3] + dstf.w(), label="lnappf")
        dstf.add(t9)
        x32.add(t9)
        col.add(t9)
        return t9, tload

    def xn_to_fm(self, slot, group_col, tok_xn):
        s = self.kb.s
        toks = []
        for k in range(8):
            tb = self.tbank()
            ps = tb.t
            tt_ = s.op("pe", _tr(ps[:, 0:128],
                                 self.xn[slot].t[:, k * 128:(k + 1) * 128],
                                 self.ident[:]),
                       deps=[tok_xn, self.tconst] + tb.w() + self.xnf[k].w(),
                       label=f"tr{k}")
            te = s.op("se", _copy(self.xnf[k].t[:, group_col:group_col + 128],
                                  ps[:, 0:128]), deps=[tt_], label=f"tre{k}")
            tb.add(te)
            self.xnf[k].add(te)
            self.xn[slot].add(tt_)
            toks.append(te)
        return toks

    def group_fm(self, src_dram, g, src_deps=None):
        fm_toks = []
        per_chunk = []
        for ci in range(4):
            chunk = g * 4 + ci
            t_xn, t_x = self.ln_load(src_dram, chunk, ci, src_deps=src_deps)
            fm_toks += self.xn_to_fm(ci, ci * 128, t_xn)
            per_chunk.append((t_x, t_xn, chunk))
        return fm_toks, per_chunk


def load_wt(kb, cm, dram_t, dst, rows, cols, slot=0, cast_eng="ve"):
    s = kb.s
    toks = list(dst.w())
    cc = min(cols, 512)
    i = slot
    first = {0: True, 1: True}
    blk = 0
    for k in range(rows // 128):
        for j in range(0, cols, cc):
            sl = i % 2
            stage = cm.wstage[sl]
            tl = s.op("sp" if sl == 0 else "gp",
                      _dma(stage.t[:, 0:cc],
                           dram_t.ap()[blk * 128:(blk + 1) * 128, 0:cc]),
                      deps=stage.w() + (toks if first[sl] else []),
                      sem="dwt0" if sl == 0 else "dq1", label="wld")
            first[sl] = False
            blk += 1
            stage.add(tl)
            tc = s.op(cast_eng, _copy(dst.t[:, k * cols + j:k * cols + j + cc],
                                      stage.t[:, 0:cc]), deps=[tl],
                      label="wcast")
            stage.add(tc)
            dst.add(tc)
            i += 1
    return dst.deps[-1]


def proj_tm(kb, cm, xbufs, xcol, wt, wtoks, xtoks, nchunks, out_cb,
            nout=D):
    s = kb.s
    toks = []
    for ci in range(nchunks):
        for gg in range(nout // 512):
            bank = cm.bank()
            ps = bank.t
            deps = bank.w() + list(xtoks) + list(wtoks)
            tmm = None
            for k in range(8):
                tmm = s.op("pe", _mm(ps[:, 0:512],
                                     xbufs[k].t[:, xcol + ci * 128:xcol + ci * 128 + 128],
                                     wt.t[:, k * nout + gg * 512:k * nout + gg * 512 + 512],
                                     start=(k == 0), stop=(k == 7)),
                           deps=deps if k == 0 else [], label=f"ptm{ci}.{gg}.{k}")
            tev = out_cb(ci, gg, ps[:, 0:512], [tmm])
            bank.add(tev)
            wt.add(tev)
            for b in xbufs:
                b.add(tev)
            toks.append(tev)
    return toks


def proj_fm(kb, cm, xbufs, xcol, glen, wt, wtoks, xtoks, out_cb, nout=D):
    s = kb.s
    toks = []
    for m in range(nout // 128):
        bank = cm.bank()
        ps = bank.t
        deps = bank.w() + list(xtoks) + list(wtoks)
        tmm = None
        for k in range(8):
            tmm = s.op("pe", _mm(ps[:, 0:glen],
                                 wt.t[:, k * nout + m * 128:k * nout + m * 128 + 128],
                                 xbufs[k].t[:, xcol:xcol + glen],
                                 start=(k == 0), stop=(k == 7)),
                       deps=deps if k == 0 else [], label=f"pfm{m}.{k}")
        tev = out_cb(m, ps[:, 0:glen], [tmm])
        bank.add(tev)
        wt.add(tev)
        for b in xbufs:
            b.add(tev)
        toks.append(tev)
    return toks


def epilogue_store(kb, cm, o_buf, o_toks, slot, dst_dram, chunk, xbuf=None):
    s = kb.s
    col = cm.col2
    x32 = cm.x32[slot] if xbuf is None else xbuf
    c0 = 4 * slot
    sqb = cm.sq()
    t1 = s.op("ve", _red(col.t[:, c0:c0 + 1], o_buf.t[:], ALU.add),
              deps=list(o_toks) + col.w())
    sqw = sqb.w()
    t2 = s.op("se", _act(sqb.t[:], o_buf.t[:], AF.Square,
                         accum=col.t[:, c0 + 1:c0 + 2]),
              deps=list(o_toks) + sqw, label="esq")
    sqb.add(t2)
    t3 = s.op("ve", _ts(col.t[:, c0:c0 + 1], col.t[:, c0:c0 + 1], 1.0 / D, None,
                        ALU.mult), deps=[t1])
    t4 = s.op("ve", _stt(col.t[:, c0 + 2:c0 + 3], col.t[:, c0:c0 + 1], 0.0,
                         col.t[:, c0:c0 + 1], ALU.bypass, ALU.mult), deps=[t3])
    t5 = s.op("ve", _ts(col.t[:, c0 + 1:c0 + 2], col.t[:, c0 + 1:c0 + 2],
                        1.0 / D, None, ALU.mult), deps=[t2])
    t6 = s.op("ve", _tt(col.t[:, c0 + 2:c0 + 3], col.t[:, c0 + 1:c0 + 2],
                        col.t[:, c0 + 2:c0 + 3], ALU.subtract), deps=[t4, t5])
    t7 = s.op("se", _act(col.t[:, c0 + 3:c0 + 4], col.t[:, c0 + 2:c0 + 3],
                         AF.Sqrt, bias=cm.epscol[:, 0:1]), deps=[t6, cm.tepc])
    t8 = s.op("ve", _recip(col.t[:, c0 + 3:c0 + 4], col.t[:, c0 + 3:c0 + 4]),
              deps=[t7])
    t9 = s.op("ve", _ts(o_buf.t[:], o_buf.t[:], col.t[:, c0:c0 + 1],
                        col.t[:, c0 + 3:c0 + 4], ALU.subtract, ALU.mult),
              deps=[t8, t3, t2], label="eln")
    o_buf.add(t9)
    t10 = s.op("ve", _tt(x32.t[:], x32.t[:], o_buf.t[:], ALU.add),
               deps=[t9] + (x32.deps[-1:] if xbuf is not None else []),
               label="resid")
    o_buf.add(t10)
    x32.add(t10)
    col.add(t9)
    tst = s.op("sp", _dma(dst_dram.ap()[chunk * 128:(chunk + 1) * 128, :],
                          x32.t[:]), deps=[t10], sem=f"dxo{slot}",
               label=f"st{chunk}")
    x32.add(tst)
    return tst, t10


def gather_prev_row_fm(kb, cm, src_dram, hx, idx, src_deps=None):
    s = kb.s
    row = kb.tbuf(f"g{idx}_row", [1, D], F32)
    rowb = kb.tbuf(f"g{idx}_rowb", [8, D], F32)
    rowb16 = kb.tbuf(f"g{idx}_rowb16", [8, D], BF16)
    rowo = kb.tbuf(f"g{idx}_rowo", [1, D], BF16)
    sqg = kb.tbuf(f"g{idx}_sq", [1, D], F32)
    cin = kb.dram_tmp(f"g{idx}_in", [1, D])
    cout = kb.dram_tmp(f"g{idx}_out", [8, D])
    tmpd = kb.dram_tmp(f"g{idx}_tmp", [1, D], BF16)
    col = cm.colg

    tl = s.op("sp", _dma(row.t[:], src_dram.ap()[TCORE - 1:TCORE, :]),
              deps=row.w() + ([src_deps[-1]] if src_deps else []),
              sem=f"dg{idx}", label="grow")
    row.add(tl)
    t1 = s.op("ve", _red(col.t[:, 0:1], row.t[:], ALU.add),
              deps=[tl] + col.w())
    t2 = s.op("se", _act(sqg.t[:], row.t[:], AF.Square, accum=col.t[:, 1:2]),
              deps=[tl], label="gsq")
    t3 = s.op("ve", _ts(col.t[:, 0:1], col.t[:, 0:1], 1.0 / D, None, ALU.mult),
              deps=[t1])
    t4 = s.op("ve", _stt(col.t[:, 2:3], col.t[:, 0:1], 0.0, col.t[:, 0:1],
                         ALU.bypass, ALU.mult), deps=[t3])
    t5 = s.op("ve", _ts(col.t[:, 1:2], col.t[:, 1:2], 1.0 / D, None, ALU.mult),
              deps=[t2])
    t6 = s.op("ve", _tt(col.t[:, 2:3], col.t[:, 1:2], col.t[:, 2:3],
                        ALU.subtract), deps=[t4, t5])
    t7 = s.op("se", _act(col.t[:, 3:4], col.t[:, 2:3], AF.Sqrt,
                         bias=cm.epscol[0:1, 0:1]), deps=[t6, cm.tepc])
    t8 = s.op("ve", _recip(col.t[:, 3:4], col.t[:, 3:4]), deps=[t7])
    t9 = s.op("ve", _ts(row.t[:], row.t[:], col.t[:, 0:1], col.t[:, 3:4],
                        ALU.subtract, ALU.mult), deps=[t8, t3, t2])
    row.add(t9)
    col.add(t9)
    ta = s.op("sp", _dma(cin.ap(), row.t[:]), deps=[t9], sem=f"dg{idx}")
    row.add(ta)
    tc = s.op("gp", _cc_allgather(cin.ap().opt(), cout.ap().opt()),
              deps=[ta], sem=f"cc{idx}", n=1, label=f"ag{idx}")
    tb = s.op("sp", _dma(rowb.t[:], cout.ap()), deps=[tc] + rowb.w(),
              sem=f"dg{idx}")
    rowb.add(tb)
    tsl = s.op("ve", _ts(rowb.t[:], rowb.t[:], cm.meta[0:8, 8:9], None,
                         ALU.mult), deps=[tb, cm.tconst], label="gsel")
    rowb.add(tsl)
    tcast = s.op("ve", _copy(rowb16.t[:], rowb.t[:]),
                 deps=[tsl] + rowb16.w(), label="gcast")
    rowb16.add(tcast)
    evs = []
    for half in range(2):
        bank = cm.bank()
        ps = bank.t
        tm_ = s.op("pe", _mm(ps[0:1, 0:512], cm.onescol[0:8, :],
                             rowb16.t[0:8, half * 512:(half + 1) * 512],
                             start=True, stop=True),
                   deps=[tcast, cm.tconst] + bank.w())
        tev = s.op("se", _copy(rowo.t[0:1, half * 512:(half + 1) * 512],
                               ps[0:1, 0:512]),
                   deps=[tm_] + (rowo.w() if half == 0 else []))
        bank.add(tev)
        rowo.add(tev)
        evs.append(tev)
    td = s.op("sp", _dma(tmpd.ap(), rowo.t[:]), deps=evs, sem=f"dg{idx}")
    rowo.add(td)
    te = s.op("sp", _dma(hx.t[:, 0:8],
                         tmpd.ap().rearrange("a (k p) -> (a p) k", p=128),
                         slow=True),
              deps=[td] + hx.w(), sem=f"dg{idx}", label="ghx")
    hx.add(te)
    return te



def prep_l1f(kb, cm):
    wr = kb.tbuf("l1f_wr", [128, 8 * D], BF16)
    wk = kb.tbuf("l1f_wk", [128, 8 * D], BF16)
    wv = kb.tbuf("l1f_wv", [128, 8 * D], BF16)
    kb.sbuf("l1f_mu_sb", [128, 16], F32)
    kb.wtok["l1f_wr"] = (wr, load_wt(kb, cm, kb.dram["l1f_wrT"], wr, D, D,
                                     slot=0))
    kb.wtok["l1f_wk"] = (wk, load_wt(kb, cm, kb.dram["l1f_wkT"], wk, D, D,
                                     slot=1))
    kb.wtok["l1f_wv"] = (wv, load_wt(kb, cm, kb.dram["l1f_wvT"], wv, D, D,
                                     slot=0))
    kb.wtok["l1f_mu"] = kb.s.op("sp", _dma(kb.sb["l1f_mu_sb"][:],
                                           kb.dram["l1f_mu"].ap()),
                                sem=kb.small_sem, label="mu")


def emit_l1f(kb, cm, src_dram, dst_dram, src_deps=None):
    s = kb.s
    wr, twr = kb.wtok["l1f_wr"]
    wk, twk = kb.wtok["l1f_wk"]
    wv, twv = kb.wtok["l1f_wv"]
    mu = kb.sb["l1f_mu_sb"]
    tmu = kb.wtok["l1f_mu"]
    xr = [kb.tbuf(f"l1f_xr{k}", [128, 512], BF16) for k in range(8)]
    xk = [kb.tbuf(f"l1f_xk{k}", [128, 512], BF16) for k in range(8)]
    k2 = [kb.tbuf(f"l1f_k2{k}", [128, 512], BF16) for k in range(8)]
    rsb = [kb.tbuf(f"l1f_r{i}", [128, D], BF16) for i in range(4)]
    dti = [kb.tbuf(f"l1f_d{i}", [128, 512], F32) for i in range(2)]
    oti = [kb.tbuf(f"l1f_o{i}", [128, D], F32) for i in range(4)]
    krl = [kb.tbuf(f"l1f_kr{i}", [128, 512], F32) for i in range(2)]
    hx = kb.tbuf("l1f_hx", [128, 8], BF16)
    pl = kb.tbuf("l1f_pl", [128, 8], BF16)

    thx = gather_prev_row_fm(kb, cm, src_dram, hx, idx=3, src_deps=src_deps)

    store_toks = []
    for g in range(4):
        fm_toks, per_chunk = cm.group_fm(src_dram, g, src_deps=src_deps)
        mix_r, mix_k = [], []
        for k in range(8):
            xf = cm.xnf[k]
            dt_ = dti[k % 2]
            if g == 0:
                pcol, pctok = hx.t[:, k:k + 1], thx
                hxread = hx
            else:
                pcol, pctok = pl.t[:, k:k + 1], pl.deps[-1] if pl.deps else None
                hxread = pl
            ta = s.op("ve", _tt(dt_.t[:, 1:512], xf.t[:, 1:512],
                                xf.t[:, 0:511], ALU.subtract),
                      deps=fm_toks + dt_.w(), label=f"mxd{k}")
            dt_.add(ta)
            tb = s.op("ve", _tt(dt_.t[:, 0:1], xf.t[:, 0:1], pcol,
                                ALU.subtract), deps=[pctok], label=f"mxd0{k}")
            dt_.add(tb)
            hxread.add(tb)
            t1 = s.op("ve", _stt(xr[k].t[:, 1:512], dt_.t[:, 1:512],
                                 mu[:, k:k + 1], xf.t[:, 0:511], ALU.mult,
                                 ALU.add), deps=[ta, tmu] + xr[k].w(),
                      label=f"mxr{k}")
            t1b = s.op("ve", _stt(xr[k].t[:, 0:1], dt_.t[:, 0:1],
                                  mu[:, k:k + 1], pcol, ALU.mult, ALU.add),
                       deps=[tb])
            t2 = s.op("ve", _stt(xk[k].t[:, 1:512], dt_.t[:, 1:512],
                                 mu[:, 8 + k:9 + k], xf.t[:, 0:511], ALU.mult,
                                 ALU.add), deps=[ta] + xk[k].w(),
                      label=f"mxk{k}")
            t2b = s.op("ve", _stt(xk[k].t[:, 0:1], dt_.t[:, 0:1],
                                  mu[:, 8 + k:9 + k], pcol, ALU.mult, ALU.add),
                       deps=[tb])
            xr[k].add(t1, t1b)
            xk[k].add(t2, t2b)
            hxread.add(t1b, t2b)
            dt_.add(t1, t1b, t2, t2b)
            xf.add(ta, tb, t1, t2)
            mix_r += [t1, t1b]
            mix_k += [t2, t2b]
        if g < 3:
            for k in range(8):
                tpl = s.op("ve", _copy(pl.t[:, k:k + 1],
                                       cm.xnf[k].t[:, 511:512]),
                           deps=pl.w() + [fm_toks[-1]], label=f"pl{k}")
                pl.add(tpl)
                cm.xnf[k].add(tpl)

        def r_cb(ci, gg, ps, deps):
            t = s.op("se", _act(rsb[ci].t[:, gg * 512:(gg + 1) * 512], ps,
                                AF.Sigmoid),
                     deps=deps + (rsb[ci].w() if gg == 0 else []), label="rsig")
            rsb[ci].add(t)
            return t
        r_toks = proj_tm(kb, cm, xr, 0, wr, [twr], mix_r, 4, r_cb)

        k2_toks = []

        def k_cb(m, ps, deps):
            kb_ = krl[m % 2]
            trl = s.op("se", _act(kb_.t[:, 0:512], ps, AF.Relu),
                       deps=deps + kb_.w(), label="krelu")
            kb_.add(trl)
            tq = s.op("ve", _tt(k2[m].t[:, 0:512], kb_.t[:, 0:512],
                                kb_.t[:, 0:512], ALU.mult),
                      deps=[trl] + k2[m].w(), label="ksq")
            kb_.add(tq)
            k2[m].add(tq)
            k2_toks.append(tq)
            return tq
        proj_fm(kb, cm, xk, 0, 512, wk, [twk], mix_k, k_cb)

        def v_cb(ci, gg, ps, deps):
            ob = oti[ci]
            t = s.op("ve", _tt(ob.t[:, gg * 512:(gg + 1) * 512],
                               rsb[ci].t[:, gg * 512:(gg + 1) * 512], ps,
                               ALU.mult),
                     deps=deps + [r_toks[ci * 2 + gg]] +
                          (ob.w() if gg == 0 else []), label="vmul")
            ob.add(t)
            rsb[ci].add(t)
            return t
        v_toks = proj_tm(kb, cm, k2, 0, wv, [twv], k2_toks, 4, v_cb)

        for ci in range(4):
            _x_tok, _xn_tok, chunk = per_chunk[ci]
            ob = oti[ci]
            tst, _t10 = epilogue_store(kb, cm, ob, [v_toks[ci * 2],
                                                    v_toks[ci * 2 + 1]],
                                       ci, dst_dram, chunk)
            store_toks.append(tst)
    return store_toks





def prep_l1a(kb, cm):
    wA = kb.tbuf("l1a_wA", [128, 8 * D], BF16)
    wB = kb.tbuf("l1a_wB", [128, 8 * D], BF16)
    kb.sbuf("l1a_bqk_sb", [128, 16], F32)
    kb.sbuf("l1a_bvo_sb", [128, 2 * D], F32)
    tk = load_wt(kb, cm, kb.dram["l1a_wkT"], wA, D, D, slot=0)
    tv = load_wt(kb, cm, kb.dram["l1a_wvT"], wB, D, D, slot=1)
    kb.wtok["l1a_w"] = (wA, tk, wB, tv)
    kb.wtok["l1a_bqk"] = kb.s.op("sp", _dma(kb.sb["l1a_bqk_sb"][:],
                                            kb.dram["l1a_bqk"].ap()),
                                 sem=kb.small_sem)
    kb.wtok["l1a_bvo"] = kb.s.op("sp", _dma(kb.sb["l1a_bvo_sb"][:],
                                            kb.dram["l1a_bvo"].ap()),
                                 sem=kb.small_sem)


def phi_evac(kb, cm, scr, dst_ap, ps_ap, bias_col, deps, war):
    s = kb.s
    t1 = s.op("ve", _ts(scr.t[:, 0:512], ps_ap, bias_col, None, ALU.add),
              deps=deps + scr.w(), label="phz")
    t2 = s.op("ve", _ts(scr.t[:, 512:1024], ps_ap, bias_col, 0.0, ALU.add,
                        ALU.min), deps=deps, label="phm")
    t3 = s.op("se", _act(scr.t[:, 512:1024], scr.t[:, 512:1024], AF.Exp),
              deps=[t2], label="phe")
    t4 = s.op("ve", _stt(dst_ap, scr.t[:, 0:512], 0.0, scr.t[:, 512:1024],
                         ALU.max, ALU.add), deps=[t1, t3] + war, label="phf")
    scr.add(t3, t4)
    return t4


def transpose_128(kb, cm, src_buf, src_col, dst_buf, dst_col, n, src_toks,
                  eng="se"):
    s = kb.s
    toks = []
    for i in range(n):
        tb = cm.tbank()
        ps = tb.t
        tt_ = s.op("pe", _tr(ps[:, 0:128],
                             src_buf.t[:, src_col + i * 128:src_col + (i + 1) * 128],
                             cm.ident[:]),
                   deps=list(src_toks) + [cm.tconst] + tb.w() +
                        (dst_buf.w() if i == 0 else []), label="tr")
        te = s.op(eng, _copy(dst_buf.t[:, dst_col + i * 128:dst_col + (i + 1) * 128],
                             ps[:, 0:128]), deps=[tt_], label="tre")
        tb.add(te)
        dst_buf.add(te)
        src_buf.add(tt_)
        toks.append(te)
    return toks


def emit_l1a(kb, cm, src_dram, dst_dram, src_deps=None):
    s = kb.s
    wA, twk, wB, twv = kb.wtok["l1a_w"]
    bqk, tbqk = kb.sb["l1a_bqk_sb"], kb.wtok["l1a_bqk"]
    bvo, tbvo = kb.sb["l1a_bvo_sb"], kb.wtok["l1a_bvo"]
    kfm = [kb.tbuf(f"l1a_k{m}", [128, TCORE], BF16) for m in range(8)]
    vtm = [kb.tbuf(f"l1a_v{c}", [128, D], BF16) for c in range(NCH)]
    ktm = [kb.tbuf(f"l1a_kt{i}", [128, D], BF16) for i in range(2)]
    qfm = [kb.tbuf(f"l1a_q{m}", [128, 512], BF16) for m in range(8)]
    S = kb.tbuf("l1a_S", [128, 8 * 65], F32)
    Sbf = kb.tbuf("l1a_Sbf", [128, 8 * 65], BF16)
    Sg = kb.tbuf("l1a_Sg", [128, 520], F32)
    scr = [kb.tbuf(f"l1a_scr{i}", [128, 1024], F32) for i in range(2)]
    scb = [kb.tbuf(f"l1a_sc{i}", [128, 128], BF16) for i in range(2)]
    otm = [kb.tbuf(f"l1a_ot{i}", [128, D], BF16) for i in range(2)]
    ofm = [kb.tbuf(f"l1a_of{i}", [128, 8 * 128], BF16) for i in range(2)]
    oti = [kb.tbuf(f"l1a_o{i}", [128, D], F32) for i in range(2)]
    rcol = kb.tbuf("l1a_rc", [128, 4], F32)
    ccin = kb.dram_tmp("l1a_ccin", [128, 520])
    ccout = kb.dram_tmp("l1a_ccout", [NCORES * 128, 520])

    tS0 = s.op("ve", _memset(S.t[:], 0.0), deps=S.w(), label="S0")
    S.add(tS0)

    def state_update(c, kt, kttoks, vdeps, extra):
        for p in range(8):
            bank = cm.bank()
            ps = bank.t
            bd = bank.w()
            m1 = s.op("pe", _mm(ps[0:64, 0:64],
                                kt.t[:, p * 128:p * 128 + 64],
                                vtm[c].t[:, p * 128:p * 128 + 64],
                                start=True, stop=True, tp=(0, 0)),
                      deps=list(kttoks) + list(vdeps) + bd + list(extra),
                      label="sA")
            m2 = s.op("pe", _mm(ps[64:128, 0:64],
                                kt.t[:, p * 128 + 64:p * 128 + 128],
                                vtm[c].t[:, p * 128 + 64:p * 128 + 128],
                                start=True, stop=True, tp=(0, 64)),
                      deps=[], label="sB")
            tu = s.op("ve", _tt(S.t[:, p * 65:p * 65 + 64],
                                S.t[:, p * 65:p * 65 + 64], ps[:, 0:64],
                                ALU.add), deps=[m1, m2] + list(extra) +
                      ([S.deps[-1]] if S.deps else []), label="Sup")
            bank.add(tu)
            S.add(tu)
            kt.add(m1, m2)
            vtm[c].add(m1, m2)
        for m in range(8):
            tz1 = s.op("ve", _red(rcol.t[:, 0:1],
                                  kfm[m].t[:, c * 128:(c + 1) * 128],
                                  ALU.add), deps=rcol.w() +
                       ([kfm[m].deps[-1]] if kfm[m].deps else []), label="zr")
            tz2 = s.op("ve", _tt(S.t[:, m * 65 + 64:m * 65 + 65],
                                 S.t[:, m * 65 + 64:m * 65 + 65],
                                 rcol.t[:, 0:1], ALU.add), deps=[tz1],
                       label="zup")
            rcol.add(tz2)
            S.add(tz2)
            kfm[m].add(tz1)

    for g in range(4):
        fm_toks, per_chunk = cm.group_fm(src_dram, g, src_deps=src_deps)
        kevs = []

        def k_cb(m, ps, deps):
            t = phi_evac(kb, cm, scr[m % 2],
                         kfm[m].t[:, g * 512:(g + 1) * 512], ps,
                         bqk[:, 8 + m:9 + m], deps + [tbqk], kfm[m].w())
            kfm[m].add(t)
            kevs.append(t)
            return t
        proj_fm(kb, cm, cm.xnf, 0, 512, wA, [twk], fm_toks, k_cb)

        vevs = {}

        def v_cb(ci, gg, ps, deps):
            vb = vtm[g * 4 + ci]
            t = s.op("ve", _tt(vb.t[:, gg * 512:(gg + 1) * 512], ps,
                               bvo[:, gg * 512:(gg + 1) * 512], ALU.add),
                     deps=deps + [tbvo] + (vb.w() if gg == 0 else []),
                     label="vbias")
            vb.add(t)
            vevs.setdefault(ci, []).append(t)
            return t
        proj_tm(kb, cm, cm.xnf, 0, wB, [twv], fm_toks, 4, v_cb)

        for ci in range(4):
            c = g * 4 + ci
            kt = ktm[ci % 2]
            kts = []
            for m in range(8):
                kts += transpose_128(kb, cm, kfm[m], c * 128, kt, m * 128, 1,
                                     [kevs[m]])
            state_update(c, kt, kts[-1:], vevs[ci], [])

    slast = S.deps[-1]
    tci = s.op("sp", _dma(ccin.ap(), S.t[:, 0:520]), deps=[slast], sem="dga")
    S.add(tci)
    tcc = s.op("gp", _cc_allgather(ccin.ap().opt(), ccout.ap().opt()),
               deps=[tci], sem="cca", n=1, label="agS")
    twq = load_wt(kb, cm, kb.dram["l1a_wqT"], wA, D, D, slot=0)
    two = load_wt(kb, cm, kb.dram["l1a_woT"], wB, D, D, slot=1)
    tprev = None
    for c in range(NCORES):
        tg = s.op("sp", _dma(Sg.t[:, 0:520],
                             ccout.ap()[c * 128:(c + 1) * 128, :]),
                  deps=[tcc] + Sg.w(), sem="dga", label="sgl")
        Sg.add(tg)
        if c == 0:
            tco = s.op("ve", _ts(S.t[:, 0:520], Sg.t[:, 0:520],
                                 cm.meta[:, 0:1], None, ALU.mult),
                       deps=[tg, cm.tconst] + S.w(), label="comb0")
        else:
            tco = s.op("ve", _stt(S.t[:, 0:520], Sg.t[:, 0:520],
                                  cm.meta[:, c:c + 1], S.t[:, 0:520],
                                  ALU.mult, ALU.add), deps=[tg, tprev],
                       label=f"comb{c}")
        Sg.add(tco)
        S.add(tco)
        tprev = tco
    tsb = s.op("ve", _copy(Sbf.t[:], S.t[:]), deps=[tprev] + Sbf.w(),
               label="Scast")
    Sbf.add(tsb)
    S.add(tsb)

    store_toks = []
    for g in range(4):
        fm_toks, per_chunk = cm.group_fm(src_dram, g, src_deps=src_deps)
        qevs = []

        def q_cb(m, ps, deps):
            t = phi_evac(kb, cm, scr[m % 2], qfm[m].t[:, 0:512], ps,
                         bqk[:, m:m + 1], deps + [tbqk], qfm[m].w())
            qfm[m].add(t)
            qevs.append(t)
            return t
        proj_fm(kb, cm, cm.xnf, 0, 512, wA, [twq], fm_toks, q_cb)

        for ci in range(4):
            c = g * 4 + ci
            ob = oti[ci % 2]
            otb = otm[ci % 2]
            ofb = ofm[ci % 2]
            out_evs = []
            ot_war = otb.w()
            for h in range(16):
                m, part = h // 2, (h % 2) * 64
                bsc = cm.bank()
                pssc = bsc.t
                tm1 = s.op("pe", _mm(pssc[:, 0:128],
                                     kfm[m].t[part:part + 64,
                                              c * 128:(c + 1) * 128],
                                     qfm[m].t[part:part + 64,
                                              ci * 128:(ci + 1) * 128],
                                     start=True, stop=True),
                           deps=[qevs[m]] + bsc.w() +
                                ([kfm[m].deps[-1]] if kfm[m].deps else []),
                           label="scm")
                kfm[m].add(tm1)
                qfm[m].add(tm1)
                sb_ = scb[h % 2]
                tev = s.op("ve", _stt(sb_.t[:], pssc[:, 0:128], 1.0,
                                      cm.ml1[:], ALU.bypass, ALU.mult),
                           deps=[tm1, cm.tconst] + sb_.w(), label="scev")
                bsc.add(tev)
                sb_.add(tev)
                bnd = cm.bank()
                psnd = bnd.t
                bndw = bnd.w()
                n1 = s.op("pe", _mm(psnd[:, 0:64], sb_.t[:],
                                    vtm[c].t[:, h * 64:(h + 1) * 64],
                                    start=True, stop=False),
                          deps=[tev] + bndw, label="num")
                n2 = s.op("pe", _mm(psnd[:, 64:65], sb_.t[:],
                                    cm.onescol[:], start=False, stop=False),
                          deps=[cm.tconst], label="den")
                n3 = s.op("pe", _mm(psnd[:, 0:65],
                                    qfm[m].t[part:part + 64,
                                             ci * 128:(ci + 1) * 128],
                                    Sbf.t[part:part + 64, m * 65:(m + 1) * 65],
                                    start=False, stop=True),
                          deps=[Sbf.deps[-1]] if Sbf.deps else [],
                          label="inter")
                Sbf.add(n3)
                sb_.add(n1, n2)
                vtm[c].add(n1)
                tr1 = s.op("ve", _recip(rcol.t[:, 1:2], psnd[:, 64:65]),
                           deps=[n3] + rcol.w(), label="rec")
                tr2 = s.op("ve", _ts(otb.t[:, h * 64:(h + 1) * 64],
                                     psnd[:, 0:64], rcol.t[:, 1:2], None,
                                     ALU.mult),
                           deps=[tr1] + (ot_war if h == 0 else []),
                           label="oev")
                bnd.add(tr2)
                rcol.add(tr2)
                otb.add(tr2)
                out_evs.append(tr2)
            kt = ktm[ci % 2]
            kts = []
            for m in range(8):
                kts += transpose_128(kb, cm, kfm[m], c * 128, kt, m * 128, 1,
                                     [kfm[m].deps[-1]] if kfm[m].deps else [])
            state_update(c, kt, kts[-1:], [vtm[c].deps[-1]] if vtm[c].deps
                         else [], out_evs[-1:])
            tsb2 = s.op("ve", _copy(Sbf.t[:], S.t[:]),
                        deps=[S.deps[-1]] + Sbf.w(), label="Scast2")
            Sbf.add(tsb2)
            S.add(tsb2)
            of_toks = transpose_128(kb, cm, otb, 0, ofb, 0, 8, out_evs[-1:])
            o_toks = []
            obw = ob.w()
            for gg in range(2):
                bank = cm.bank()
                ps = bank.t
                deps = bank.w() + of_toks + [two]
                tmm = None
                for kkm in range(8):
                    tmm = s.op("pe", _mm(ps[:, 0:512],
                                         ofb.t[:, kkm * 128:(kkm + 1) * 128],
                                         wB.t[:, kkm * D + gg * 512:
                                              kkm * D + gg * 512 + 512],
                                         start=(kkm == 0), stop=(kkm == 7)),
                               deps=deps if kkm == 0 else [], label="wo")
                wB.add(tmm)
                for bfx in [ofb]:
                    bfx.add(tmm)
                tev = s.op("ve", _tt(ob.t[:, gg * 512:(gg + 1) * 512],
                                     ps[:, 0:512],
                                     bvo[:, D + gg * 512:D + gg * 512 + 512],
                                     ALU.add),
                           deps=[tmm, tbvo] + (obw if gg == 0 else []),
                           label="obias")
                bank.add(tev)
                ob.add(tev)
                o_toks.append(tev)
            tst, _ = epilogue_store(kb, cm, ob, o_toks, ci, dst_dram, c)
            store_toks.append(tst)
    return store_toks





def _quant_pass(kb, cm, dram_t, dst, rows, cols, dstmap, s_col, sinv_col,
                tscale, slot, jouter=False):
    s = kb.s
    i = slot
    first = {0: True, 1: True}
    tokmap = {}
    nj = cols // 512
    if jouter:
        order = [(k, j) for j in range(0, cols, 512)
                 for k in range(rows // 128)]
    else:
        order = [(k, j) for k in range(rows // 128)
                 for j in range(0, cols, 512)]
    for k, j in order:
            blk = k * nj + j // 512
            sl = i % 2
            stage = cm.wstage[sl]
            tl = s.op("sp" if sl == 0 else "gp",
                      _dma(stage.t[:, 0:512],
                           dram_t.ap()[blk * 128:(blk + 1) * 128, 0:512]),
                      deps=stage.w(), sem="dwt0" if sl == 0 else "dq1",
                      label="qld")
            stage.add(tl)
            t1 = s.op("ve", _ts(stage.t[:, 0:512], stage.t[:, 0:512], s_col,
                                MAGIC, ALU.mult, ALU.add),
                      deps=[tl, tscale], label="qr1")
            stage.add(t1)
            t2 = s.op("ve", _ts(stage.t[:, 0:512], stage.t[:, 0:512],
                                MAGIC, 1.0, ALU.subtract, ALU.min),
                      deps=[t1], label="qr2")
            stage.add(t2)
            t3 = s.op("ve", _ts(dst.t[:, dstmap(k, j):dstmap(k, j) + 512],
                                stage.t[:, 0:512], -1.0, None, ALU.max),
                      deps=[t2] + (dst.w() if first[sl] else []),
                      label="qr3")
            first[sl] = False
            stage.add(t3)
            dst.add(t3)
            tokmap[(k, j)] = t3
            i += 1
    return dst.deps[-1], tokmap


def make_l0f_pass1(kb, cm):
    s = kb.s
    qc = cm.l0f_qc
    w1q = kb.dram_tmp("l0f_w1q", [128, 8 * 4096], FP8)
    w2q = kb.dram_tmp("l0f_w2q", [128, 32 * 1024], FP8)
    wq8 = [cm.wq80, cm.wq80]
    tz = s.op("ve", _memset(qc.t[:], 0.0), deps=qc.w(), label="qcz")
    qc.add(tz)

    def gen1():
        for nm, nblk, acc_col, tmpbase in (("l0f_w1T", 64, 0, 2),
                                           ("l0f_w2T", 64, 1, 7)):
            for blk in range(nblk):
                yield kb.dram[nm], blk, acc_col, tmpbase
    it1 = gen1()
    st = {"i": 0, "phase": 1, "scales": None}
    last_tacc = {0: None, 1: None}
    last_acc = [None]
    pend = []

    def do_abs(item):
        sl, acc_col, tmpbase, tl = item
        tc_ = tmpbase + 8 * sl
        ta = s.op("se", _act(cm.sqp[sl].t[:, 0:512],
                             cm.wstage[sl].t[:, 0:512], AF.Abs,
                             accum=qc.t[:, tc_:tc_ + 1]),
                  deps=[tl, tz] + cm.sqp[sl].w() +
                       ([last_tacc[sl]] if last_tacc[sl] else []),
                  label="aab")
        cm.sqp[sl].add(ta)
        cm.wstage[sl].add(ta)
        tacc = s.op("ve", _tt(qc.t[:, acc_col:acc_col + 1],
                              qc.t[:, acc_col:acc_col + 1],
                              qc.t[:, tc_:tc_ + 1],
                              ALU.add),
                    deps=[ta, tz] +
                         ([last_acc[0]] if last_acc[0] else []),
                    label="aacc")
        qc.add(tacc)
        last_tacc[sl] = tacc
        last_acc[0] = tacc

    def scales(acc_col, mean_col, s_coli, nelem, dep):
        bank = cm.bank()
        ps = bank.t
        tm_ = s.op("pe", _mm(ps[:, 0:1], cm.onesf[:],
                             qc.t[:, acc_col:acc_col + 1], start=True,
                             stop=True), deps=[dep, cm.tconst] + bank.w(),
                   label="qfold")
        qc.add(tm_)
        t1 = s.op("ve", _ts(qc.t[:, mean_col:mean_col + 1], ps[:, 0:1],
                            1.0 / nelem, 1e-5, ALU.mult, ALU.max),
                  deps=[tm_], label="qmean")
        bank.add(t1)
        qc.add(t1)
        t2 = s.op("ve", _recip(qc.t[:, s_coli:s_coli + 1],
                               qc.t[:, mean_col:mean_col + 1]), deps=[t1],
                  label="qs")
        qc.add(t2)
        return t2

    def transition():
        while pend:
            do_abs(pend.pop(0))
        ts1 = scales(0, 4, 3, 1024 * 4096, qc.deps[-1])
        ts2 = scales(1, 6, 5, 4096 * 1024, qc.deps[-1])
        st["scales"] = (ts1, ts2)
        st["phase"] = 2

    def gen2():
        for jj in range(8):
            for k in range(8):
                yield (kb.dram["l0f_w1T"], k * 8 + jj, w1q,
                       k * 4096 + jj * 512, 3)
        for k in range(32):
            for jj in range(2):
                yield (kb.dram["l0f_w2T"], k * 2 + jj, w2q,
                       k * 1024 + jj * 512, 5)
    it2 = gen2()

    def do_quant(item):
        sl, dram_out, dcol, s_coli, tl = item
        stage = cm.wstage[sl]
        ts1, ts2 = st["scales"]
        tsc = ts1 if s_coli == 3 else ts2
        t1 = s.op("ve", _ts(stage.t[:, 0:512], stage.t[:, 0:512],
                            qc.t[:, s_coli:s_coli + 1],
                            MAGIC, ALU.mult, ALU.add),
                  deps=[tl, tsc], label="qr1")
        stage.add(t1)
        t2 = s.op("ve", _ts(stage.t[:, 0:512], stage.t[:, 0:512],
                            MAGIC, 1.0, ALU.subtract, ALU.min),
                  deps=[t1], label="qr2")
        stage.add(t2)
        t3 = s.op("ve", _ts(wq8[sl].t[:, 0:512], stage.t[:, 0:512],
                            -1.0, None, ALU.max),
                  deps=[t2] + wq8[sl].w(), label="qr3")
        wq8[sl].add(t3)
        stage.add(t3)
        t4 = s.op("gp", _dma(dram_out.ap()[:, dcol:dcol + 512],
                             wq8[sl].t[:, 0:512]),
                  deps=[t3], sem=f"dws{sl}", label="qst")
        wq8[sl].add(t4)
        st.setdefault("tqm", {})[(id(dram_out), sl)] = t4

    def step():
        if st["phase"] == 1:
            try:
                dram_t, blk, acc_col, tmpbase = next(it1)
            except StopIteration:
                transition()
                return step()
            if len(pend) >= 2:
                do_abs(pend.pop(0))
            sl = st["i"] % 2
            st["i"] += 1
            stage = cm.wstage[sl]
            tl = s.op("gp",
                      _dma(stage.t[:, 0:512],
                           dram_t.ap()[blk * 128:(blk + 1) * 128, 0:512]),
                      deps=stage.w(), sem=f"dq{2 - sl}", label="ald")
            stage.add(tl)
            pend.append((sl, acc_col, tmpbase, tl))
            return True
        try:
            dram_t, blk, dram_out, dcol, s_coli = next(it2)
        except StopIteration:
            st["phase"] = 3
            return False
        if len(pend) >= 2:
            do_quant(pend.pop(0))
        sl = st["i"] % 2
        st["i"] += 1
        stage = cm.wstage[sl]
        tl = s.op("gp",
                  _dma(stage.t[:, 0:512],
                       dram_t.ap()[blk * 128:(blk + 1) * 128, 0:512]),
                  deps=stage.w(), sem=f"dq{2 - sl}", label="qld")
        stage.add(tl)
        pend.append((sl, dram_out, dcol, s_coli, tl))
        return True

    def pump(n):
        for _ in range(n):
            if not step():
                return

    def drain():
        if st["phase"] == 1:
            while pend:
                do_abs(pend.pop(0))
        else:
            while pend:
                do_quant(pend.pop(0))

    def finish():
        while st["phase"] != 3:
            pump(16)
        drain()
        ts1, ts2 = st["scales"]
        tqm = st["tqm"]
        tq1 = [t for (k, _), t in tqm.items() if k == id(w1q)]
        tq2 = [t for (k, _), t in tqm.items() if k == id(w2q)]
        return ts1, ts2, w1q, w2q, tq1, tq2
    return pump, drain, finish


def prep_l0f(kb, cm, pass1_finish=None):
    s = kb.s
    w1h = kb.tbuf("l0f_w1h", [128, 8 * 4096], FP8)
    w2h = kb.tbuf("l0f_w2h", [128, 32 * 1024], FP8)
    cwc = kb.sbuf("l0f_cw_sb", [128, 128], F32)
    onesf = cm.onesf
    tof = cm.tconst
    qc = cm.l0f_qc
    tcw = s.op("sp", _dma(cwc[:], kb.dram["l0f_cw"].ap()), sem=kb.small_sem)

    if pass1_finish is None:
        _, _, pass1_finish = make_l0f_pass1(kb, cm)
    ts1, ts2, w1q, w2q, tq1, tq2 = pass1_finish()

    def reload(dst, src_dram, tdeps, nch):
        cw_ = 32768 // nch
        toks = list(dst.w())
        last = None
        for c in range(nch):
            last = s.op("sp" if c % 2 == 0 else "gp",
                        _dma(dst.t[:, c * cw_:(c + 1) * cw_],
                             src_dram.ap()[:, c * cw_:(c + 1) * cw_]),
                        deps=list(tdeps) + (toks if c < 2 else []),
                        sem="dwq0" if c % 2 == 0 else "dq3",
                        label="wq8ld")
            dst.add(last)
        return last

    tw1 = reload(w1h, w1q, tq1, 16)
    tw2 = reload(w2h, w2q, tq2, 16)
    tw1c = [tw1] * 8
    tw2map = {(m, gg * 512): tw2 for m in range(32) for gg in range(2)}
    kb.wtok["l0f"] = (w1h, tw1, w2h, tw2, cwc, tcw, qc, onesf, tof,
                      tw1c, tw2map, ts1)


def gather_edge_rows(kb, cm, src_dram, idx, src_deps=None):
    s = kb.s
    rows = kb.tbuf(f"g{idx}_rows", [2, D], F32)
    rowg = kb.tbuf(f"g{idx}_rowg", [16, D], F32)
    rsel = kb.tbuf(f"g{idx}_rsel", [16, D], BF16)
    rq = rows
    rqb = kb.tbuf(f"g{idx}_rqb", [2, D], BF16)
    xhf = kb.tbuf(f"g{idx}_xhf", [128, 16], BF16)
    sqg = cm.sq()
    col = kb.tbuf(f"g{idx}_col", [2, 8], F32)
    cin = kb.dram_tmp(f"g{idx}_in", [2, D])
    cout = kb.dram_tmp(f"g{idx}_out", [16, D])
    tmpd = kb.dram_tmp(f"g{idx}_tmp", [2, D], BF16)

    t1 = s.op("se", _dma(rows.t[0:1, :], src_dram.ap()[0:1, :]),
              deps=rows.w() + ([src_deps[0]] if src_deps else []),
              sem=f"dg{idx}", label="ger0")
    t2 = s.op("se", _dma(rows.t[1:2, :], src_dram.ap()[TCORE - 1:TCORE, :]),
              deps=[src_deps[-1]] if src_deps else [], sem=f"dg{idx}2",
              label="ger1")
    rows.add(t1, t2)
    ta = s.op("ve", _red(col.t[:, 0:1], rows.t[:], ALU.add),
              deps=[t1, t2] + col.w())
    tb = s.op("se", _act(sqg.t[0:2, :], rows.t[:], AF.Square,
                         accum=col.t[:, 1:2]),
              deps=[t1, t2] + sqg.w(), label="gsq")
    sqg.add(tb)
    tc = s.op("ve", _ts(col.t[:, 0:1], col.t[:, 0:1], 1.0 / D, None, ALU.mult),
              deps=[ta])
    td = s.op("ve", _stt(col.t[:, 2:3], col.t[:, 0:1], 0.0, col.t[:, 0:1],
                         ALU.bypass, ALU.mult), deps=[tc])
    te = s.op("ve", _ts(col.t[:, 1:2], col.t[:, 1:2], 1.0 / D, None, ALU.mult),
              deps=[tb])
    tf = s.op("ve", _tt(col.t[:, 2:3], col.t[:, 1:2], col.t[:, 2:3],
                        ALU.subtract), deps=[td, te])
    tg = s.op("se", _act(col.t[:, 3:4], col.t[:, 2:3], AF.Sqrt,
                         bias=cm.epscol[0:2, 0:1]), deps=[tf, cm.tepc])
    th = s.op("ve", _recip(col.t[:, 3:4], col.t[:, 3:4]), deps=[tg])
    ti = s.op("ve", _ts(rows.t[:], rows.t[:], col.t[:, 0:1], col.t[:, 3:4],
                        ALU.subtract, ALU.mult), deps=[th, tc], label="gln")
    rows.add(ti)
    col.add(ti)
    tj = s.op("se", _dma(cin.ap(), rows.t[:]), deps=[ti], sem=f"dg{idx}")
    rows.add(tj)
    return (rows, rowg, rsel, rq, rqb, xhf, col, cin, cout, tmpd, tj, idx)


def gather_edge_finish(kb, cm, st, tscale):
    s = kb.s
    rows, rowg, rsel, rq, rqb, xhf, col, cin, cout, tmpd, tj, idx = st
    tk = s.op("gp", _cc_allgather(cin.ap().opt(), cout.ap().opt()),
              deps=[tj], sem=f"cc{idx}", n=1, label=f"ag{idx}")
    tl = s.op("se", _dma(rowg.t[:], cout.ap()), deps=[tk] + rowg.w(),
              sem=f"dg{idx}")
    rowg.add(tl)
    selb = kb.tbuf(f"g{idx}_selb", [16, 2], BF16)
    tsb_ = s.op("ve", _copy(selb.t[:], cm.meta[0:16, 9:11]),
                deps=[cm.tconst] + selb.w(), label="gselc")
    selb.add(tsb_)
    tcast16 = s.op("ve", _copy(rsel.t[:], rowg.t[:]),
                   deps=[tl] + rsel.w(), label="gcast16")
    rsel.add(tcast16)
    rowg.add(tcast16)
    sel_toks = []
    for half in range(2):
        bank = cm.bank()
        ps = bank.t
        tm_ = s.op("pe", _mm(ps[0:2, 0:512], selb.t[:],
                             rsel.t[:, half * 512:(half + 1) * 512],
                             start=True, stop=True),
                   deps=[tsb_, tcast16] + bank.w(), label="gsum")
        selb.add(tm_)
        rsel.add(tm_)
        tev = s.op("ve", _copy(rq.t[:, half * 512:(half + 1) * 512],
                               ps[0:2, 0:512]),
                   deps=[tm_] + (rq.w() if half == 0 else []), label="gsev")
        bank.add(tev)
        rq.add(tev)
        sel_toks.append(tev)
    qc = kb.wtok["l0f"][6]
    tqa = s.op("ve", _red(col.t[:, 4:5], rq.t[:], ALU.max, absv=True),
               deps=sel_toks + [col.deps[-1] if col.deps else None],
               label="gqm")
    col.add(tqa)
    tqb = s.op("ve", _ts(col.t[:, 5:6], col.t[:, 4:5], 1e-5, 1.0 / 127.0,
                         ALU.max, ALU.mult), deps=[tqa], label="gqs")
    col.add(tqb)
    tqc = s.op("ve", _tt(col.t[:, 6:7], col.t[:, 5:6], qc.t[0:2, 4:5],
                         ALU.mult), deps=[tqb, tscale], label="gqc")
    col.add(tqc)
    tqd = s.op("ve", _recip(col.t[:, 5:6], col.t[:, 5:6]), deps=[tqc])
    col.add(tqd)
    tq1 = s.op("ve", _ts(rq.t[:], rq.t[:], col.t[:, 5:6], MAGIC, ALU.mult,
                         ALU.add), deps=[tqd], label="gq1")
    rq.add(tq1)
    tq2 = s.op("ve", _ts(rqb.t[:], rq.t[:], MAGIC, col.t[:, 6:7],
                         ALU.subtract, ALU.mult), deps=[tq1] + rqb.w(),
               label="gq2")
    rqb.add(tq2)
    rq.add(tq2)
    col.add(tq2)
    tq3 = s.op("se", _dma(tmpd.ap(), rqb.t[:]), deps=[tq2], sem=f"dg{idx}")
    rqb.add(tq3)
    xw = xhf.w()
    tq4 = None
    for a in range(2):
        tq4 = s.op("se", _dma(xhf.t[:, a:a + 15:2],
                              tmpd.ap()[a:a + 1, :].rearrange(
                                  "a (k p) -> (a p) k", p=128), slow=True),
                   deps=[tq3] + (xw if a == 0 else []), sem=f"dg{idx}",
                   label="gxhf")
        xhf.add(tq4)
    return xhf, tq4


def emit_l0f(kb, cm, src_dram, dst_dram, src_deps=None, gather=None):
    s = kb.s
    (w1h, tw1, w2h, tw2, cwc, tcw, qc, onesf, tof, tw1c,
     tw2map, ts1g) = kb.wtok["l0f"]
    NH = 32
    GL = 512
    NG = TCORE // GL
    NHF = GL // 128
    W = GL + 2
    hb = [kb.tbuf(f"l0f_h{i}", [128, NH * W], BF16) for i in range(2)]
    fscr = [kb.tbuf(f"l0f_f{i}", [128, D], F32) for i in range(2)]
    xr0 = kb.tbuf("l0f_xr0", [128, D], F32)
    xrb = [xr0, xr0]
    cbuf = [kb.tbuf(f"l0f_cb{i}", [128, GL], F32) for i in range(2)]
    macc = kb.tbuf("l0f_macc", [128, GL], BF16)
    nacc = kb.tbuf("l0f_nacc", [128, GL], BF16)
    mtt = kb.tbuf("l0f_mtt", [128, GL], BF16)
    s2r = kb.tbuf("l0f_s2r", [128, GL], F32)
    srow = s2r
    hh = kb.tbuf("l0f_hh", [128, 64], BF16)
    qcol = kb.tbuf("l0f_qcol", [128, 24], F32)

    if gather is None:
        gather = gather_edge_rows(kb, cm, src_dram, idx=1, src_deps=src_deps)
    xhf, thx = gather_edge_finish(kb, cm, gather, ts1g)

    hh_toks = []
    for m in range(NH):
        bank = cm.bank()
        ps = bank.t
        deps = bank.w() + [thx, tw1c[m // 4]]
        tmm = None
        for k in range(8):
            tmm = s.op("pe", _mm(ps[:, 0:2],
                                 w1h.t[:, k * 4096 + m * 128:
                                       k * 4096 + m * 128 + 128],
                                 xhf.t[:, k * 2:k * 2 + 2],
                                 start=(k == 0), stop=(k == 7)),
                       deps=deps if k == 0 else [], label="hhm")
        w1h.add(tmm)
        xhf.add(tmm)
        tev = s.op("se", _copy(hh.t[:, m * 2:m * 2 + 2], ps[:, 0:2]),
                   deps=[tmm] + (hh.w() if m == 0 else []), label="hhe")
        bank.add(tev)
        hh.add(tev)
        hh_toks.append(tev)

    def lnq_load(chunk, slot):
        t_xn, t_x = cm.ln_load_f32(src_dram, chunk, slot, fscr[slot % 2],
                                   src_deps)
        f = fscr[slot % 2]
        c0 = slot * 2
        tqa = s.op("ve", _red(qcol.t[:, c0:c0 + 1], f.t[:], ALU.max,
                              absv=True), deps=[t_xn] + qcol.w(), label="xqm")
        qcol.add(tqa)
        tqb = s.op("ve", _ts(qcol.t[:, c0:c0 + 1], qcol.t[:, c0:c0 + 1], 1e-5,
                             1.0 / 127.0, ALU.max, ALU.mult), deps=[tqa],
                   label="xqs")
        qcol.add(tqb)
        tqc = s.op("ve", _tt(qcol.t[:, c0 + 1:c0 + 2], qcol.t[:, c0:c0 + 1],
                             qc.t[:, 4:5], ALU.mult), deps=[tqb, tw1],
                   label="xqc")
        qcol.add(tqc)
        qc.add(tqc)
        tqd = s.op("ve", _recip(qcol.t[:, c0:c0 + 1], qcol.t[:, c0:c0 + 1]),
                   deps=[tqb], label="xqr")
        qcol.add(tqd)
        tq1 = s.op("ve", _ts(f.t[:], f.t[:], qcol.t[:, c0:c0 + 1], MAGIC,
                             ALU.mult, ALU.add), deps=[tqd, tqc], label="xq1")
        f.add(tq1)
        tq2 = s.op("ve", _ts(cm.xn[slot].t[:], f.t[:], MAGIC,
                             qcol.t[:, c0 + 1:c0 + 2], ALU.subtract, ALU.mult),
                   deps=[tq1] + cm.xn[slot].w(), label="xq2")
        cm.xn[slot].add(tq2)
        f.add(tq2)
        qcol.add(tq1, tq2)
        return tq2

    store_toks = []
    edge = kb.tbuf("l0f_edge", [128, 32], BF16)

    def lnq_group(g):
        fm_toks = []
        for half in range(NHF):
            chunk = g * NHF + half
            tq2 = lnq_load(chunk, half)
            fm_toks += cm.xn_to_fm(half, half * 128, tq2)
        return fm_toks

    def w1_m(g, m, fm_toks, hw):
        h = hb[g % 2]
        bank = cm.bank()
        ps = bank.t
        deps = bank.w() + fm_toks + [tw1c[m // 4]] + hw
        tmm = None
        for k in range(8):
            tmm = s.op("pe", _mm(ps[:, 0:GL],
                                 w1h.t[:, k * 4096 + m * 128:
                                       k * 4096 + m * 128 + 128],
                                 cm.xnf[k].t[:, 0:GL],
                                 start=(k == 0), stop=(k == 7)),
                       deps=deps if k == 0 else [], label="h1m")
        w1h.add(tmm)
        for b in cm.xnf:
            b.add(tmm)
        tev = s.op("se", _copy(h.t[:, m * W + 1:m * W + 1 + GL],
                               ps[:, 0:GL]), deps=[tmm], label="h1e")
        bank.add(tev)
        h.add(tev)
        return tev

    def compute_h(g):
        fm_toks = lnq_group(g)
        h = hb[g % 2]
        hw = h.w()
        return [w1_m(g, m, fm_toks, hw if m == 0 else []) for m in range(NH)]

    def cycle(g, htoks):
        h = hb[g % 2]
        hn = hb[(g + 1) % 2]
        last = g == NG - 1
        hl = h.t[:, 0:NH * W:W]
        if g == 0:
            tl_ = s.op("ve", _copy(hl, hh.t[:, 0:64:2]),
                       deps=[hh_toks[-1], htoks[-1]], label="el")
        else:
            tl_ = s.op("ve", _copy(hl, edge.t[:, 0:32]),
                       deps=[edge.deps[-1]] if edge.deps else [], label="el")
            edge.add(tl_)
        h.add(tl_)
        tsv = s.op("ve", _copy(edge.t[:, 0:32], h.t[:, GL:NH * W:W]),
                   deps=edge.w() + [htoks[-1]], label="esv")
        edge.add(tsv)
        h.add(tsv)
        fm_toks = [] if last else lnq_group(g + 1)
        if not last:
            hnw = hn.w()
        else:
            tr_all = s.op("ve", _copy(h.t[:, GL + 1:NH * W:W],
                                      hh.t[:, 1:64:2]),
                          deps=[hh_toks[-1], tl_, tsv], label="er")
            h.add(tr_all)
        mtoks = []
        ntoks = []
        for m in range(NH):
            if not last:
                tev = w1_m(g + 1, m, fm_toks, hnw if m == 0 else [])
                ntoks.append(tev)
                tr_ = s.op("ve", _copy(h.t[:, m * W + GL + 1:m * W + GL + 2],
                                       hn.t[:, m * W + 1:m * W + 2]),
                           deps=[tev] + ([tl_, tsv] if m == 0 else []),
                           label="er")
                hn.add(tr_)
                h.add(tr_)
                ed = [tr_]
            else:
                ed = [tr_all] if m == 0 else []
            cb = cbuf[m % 2]
            c1 = s.op("ve", _ts(cb.t[:, 0:GL],
                                h.t[:, m * W + 1:m * W + 1 + GL],
                                cwc[:, 32 + m:33 + m], None, ALU.mult),
                      deps=[tcw] + ed + cb.w(), label="cv1")
            cb.add(c1)
            c2 = s.op("ve", _stt(cb.t[:, 0:GL], h.t[:, m * W:m * W + GL],
                                 cwc[:, m:m + 1], cb.t[:, 0:GL], ALU.mult,
                                 ALU.add), deps=[c1], label="cv2")
            cb.add(c2)
            c3 = s.op("ve", _stt(cb.t[:, 0:GL],
                                 h.t[:, m * W + 2:m * W + 2 + GL],
                                 cwc[:, 64 + m:65 + m], cb.t[:, 0:GL],
                                 ALU.mult, ALU.add), deps=[c2], label="cv3")
            cb.add(c3)
            c4 = s.op("se", _act(h.t[:, m * W + 1:m * W + 1 + GL],
                                 cb.t[:, 0:GL], AF.Silu,
                                 bias=cwc[:, 96 + m:97 + m]), deps=[c3],
                      label="silu")
            h.add(c4)
            cb.add(c4)
            mtoks.append(c4)
        for m in range(NH):
            hs = h.t[:, m * W + 1:m * W + 1 + GL]
            if m == 0:
                t5 = s.op("ve", _tt(macc.t[:, 0:GL], hs, hs, ALU.max),
                          deps=[mtoks[0]] + macc.w(), label="amx0")
                t6 = s.op("ve", _ts(nacc.t[:, 0:GL], hs, -1.0, None,
                                    ALU.mult), deps=[mtoks[0]] + nacc.w(),
                          label="anx0")
            else:
                t5 = s.op("ve", _tt(macc.t[:, 0:GL], macc.t[:, 0:GL], hs,
                                    ALU.max), deps=[mtoks[m], macc.deps[-1]],
                          label="amx")
                t6 = s.op("ve", _stt(nacc.t[:, 0:GL], hs, -1.0,
                                     nacc.t[:, 0:GL], ALU.mult, ALU.max),
                          deps=[mtoks[m], nacc.deps[-1]], label="anx")
            macc.add(t5)
            nacc.add(t6)
            h.add(t5, t6)
        tmrg = s.op("ve", _tt(mtt.t[:, 0:GL], macc.t[:, 0:GL],
                              nacc.t[:, 0:GL], ALU.max),
                    deps=[macc.deps[-1], nacc.deps[-1]] + mtt.w(),
                    label="amrg")
        macc.add(tmrg)
        nacc.add(tmrg)
        mtt.add(tmrg)
        red_toks = []
        for half in range(NHF):
            tbk = cm.tbank()
            ps = tbk.t
            ttr = s.op("pe", _tr(ps[:, 0:128],
                                 mtt.t[:, half * 128:half * 128 + 128],
                                 cm.ident[:]),
                       deps=[tmrg, cm.tconst] + tbk.w(), label="mtr")
            mtt.add(ttr)
            tev = s.op("ve", _copy(s2r.t[:, half * 128:half * 128 + 128],
                                   ps[:, 0:128]),
                       deps=[ttr] + (s2r.w() if half == 0 else []),
                       label="mtre")
            tbk.add(tev)
            s2r.add(tev)
            red_toks.append(tev)
        c0s = []
        qrow_toks = []
        cast_deps = [tmrg] + mtt.w()
        for half in range(NHF):
            c0 = 8 + (g % 2) * 8 + half * 2
            c0s.append(c0)
            tqa = s.op("ve", _red(qcol.t[:, c0:c0 + 1],
                                  s2r.t[:, half * 128:half * 128 + 128],
                                  ALU.max),
                       deps=[red_toks[half]] + qcol.w(), label="hqm")
            s2r.add(tqa)
            qcol.add(tqa)
            tqb = s.op("ve", _ts(qcol.t[:, c0:c0 + 1], qcol.t[:, c0:c0 + 1],
                                 1e-5, 1.0 / 127.0, ALU.max, ALU.mult),
                       deps=[tqa])
            qcol.add(tqb)
            tqc = s.op("ve", _tt(qcol.t[:, c0 + 1:c0 + 2],
                                 qcol.t[:, c0:c0 + 1], qc.t[:, 6:7], ALU.mult),
                       deps=[tqb, tw2], label="hqc")
            qcol.add(tqc)
            qc.add(tqc)
            tqd = s.op("ve", _recip(qcol.t[:, c0:c0 + 1],
                                    qcol.t[:, c0:c0 + 1]), deps=[tqc],
                       label="hqr")
            qcol.add(tqd)
            tq_cast = s.op("ve", _copy(mtt.t[:, half:half + 1],
                                       qcol.t[:, c0:c0 + 1]),
                           deps=[tqd] + cast_deps, label="s2c")
            cast_deps = []
            mtt.add(tq_cast)
            bank2 = cm.bank()
            ps2 = bank2.t
            ttr2 = s.op("pe", _mm(ps2[0:1, 0:128],
                                  mtt.t[:, half:half + 1], cm.ident[:],
                                  start=True, stop=True),
                        deps=[tq_cast, cm.tconst] + bank2.w(), label="s2tr")
            mtt.add(ttr2)
            tsr = s.op("ve", _copy(srow.t[0:1, half * 128:half * 128 + 128],
                                   ps2[0:1, 0:128]),
                       deps=[ttr2, tqa], label="s2row")
            bank2.add(tsr)
            srow.add(tsr)
            qrow_toks.append(tsr)
        bank3 = cm.bank()
        ps3 = bank3.t
        tbc = s.op("pe", _mm(ps3[:, 0:GL], onesf[0:1, 0:128],
                             srow.t[0:1, 0:GL], start=True, stop=True),
                   deps=qrow_toks + [tof] + bank3.w(), label="s2bc")
        srow.add(tbc)
        tsb = s.op("ve", _copy(s2r.t[:], ps3[:, 0:GL]),
                   deps=[tbc] + s2r.w(), label="s2sb")
        bank3.add(tsb)
        s2r.add(tsb)
        wbanks = {}
        wmm = {}
        for bi in range(4):
            wbanks[bi] = cm.bank()
        wdeps = {bi: wbanks[bi].w() for bi in range(4)}
        qtoks = []
        for m in range(NH):
            q1 = s.op("gp",
                      _tt(cbuf[m % 2].t[:, 0:GL],
                          h.t[:, m * W + 1:m * W + 1 + GL], s2r.t[:],
                          ALU.mult),
                      deps=[tsb, mtoks[m]] + cbuf[m % 2].w(), label="hq1")
            cbuf[m % 2].add(q1)
            q2 = s.op("ve",
                      _ts(h.t[:, m * W + 1:m * W + 1 + GL],
                          cbuf[m % 2].t[:, 0:GL], MAGIC, MAGIC, ALU.add,
                          ALU.subtract), deps=[q1], label="hq2")
            h.add(q2)
            cbuf[m % 2].add(q2)
            qtoks.append(q2)
            for bi in range(4):
                half, gg = bi // 2, bi % 2
                wmm[bi] = s.op(
                    "pe",
                    _mm(wbanks[bi].t[:, 0:512],
                        h.t[:, m * W + 1 + half * 128:
                            m * W + 129 + half * 128],
                        w2h.t[:, m * 1024 + gg * 512:
                              m * 1024 + gg * 512 + 512],
                        start=(m == 0), stop=(m == NH - 1)),
                    deps=[q2, tw2map[(m, gg * 512)]] +
                         (wdeps[bi] if m == 0 else []),
                    label="h2m")
                h.add(wmm[bi])
        def evac_epilogue(half):
            chunk = g * NHF + half
            c0 = c0s[half]
            ob = fscr[half % 2]
            obw = ob.w()
            xr = xrb[half % 2]
            txl = s.op("sp", _dma(xr.t[:],
                                  src_dram.ap()[chunk * 128:
                                                (chunk + 1) * 128, :]),
                       deps=xr.w() + ([src_deps[chunk]] if src_deps else []),
                       sem=f"dxr{half % 2}", label=f"xr{chunk}")
            xr.add(txl)
            o_toks = []
            for gg in range(2):
                bi = half * 2 + gg
                tev = s.op("ve", _ts(ob.t[:, gg * 512:(gg + 1) * 512],
                                     wbanks[bi].t[:, 0:512],
                                     qcol.t[:, c0 + 1:c0 + 2],
                                     None, ALU.mult),
                           deps=[wmm[bi]] + (obw if gg == 0 else []),
                           label="h2e")
                wbanks[bi].add(tev)
                ob.add(tev)
                qcol.add(tev)
                o_toks.append(tev)
            tst, _ = epilogue_store(kb, cm, ob, o_toks, half, dst_dram, chunk,
                                    xbuf=xr)
            store_toks.append(tst)

        evac_epilogue(0)
        evac_epilogue(1)
        for bi in range(4, 2 * NHF):
            half, gg = bi // 2, bi % 2
            bank = cm.bank()
            wbanks[bi] = bank
            deps = bank.w() + [qtoks[-1], tw2]
            tmm = None
            for m in range(NH):
                tmm = s.op("pe",
                           _mm(bank.t[:, 0:512],
                               h.t[:, m * W + 1 + half * 128:
                                   m * W + 129 + half * 128],
                               w2h.t[:, m * 1024 + gg * 512:
                                     m * 1024 + gg * 512 + 512],
                               start=(m == 0), stop=(m == NH - 1)),
                           deps=deps if m == 0 else
                           [tw2map[(m, gg * 512)]], label="h2m")
            wmm[bi] = tmm
            h.add(tmm)
        for bi in wmm:
            w2h.add(wmm[bi])
        evac_epilogue(2)
        evac_epilogue(3)
        return ntoks

    htoks = compute_h(0)
    for g in range(NG):
        htoks = cycle(g, htoks)
    return store_toks



def prep_l0a(kb, cm):
    s = kb.s
    wA = kb.tbuf("l0a_wA", [128, 8 * D], BF16)
    wB = kb.tbuf("l0a_wB", [128, 8 * D], BF16)
    wC = kb.tbuf("l0a_wC", [128, 8 * D], BF16)
    ldt = kb.tbuf("l0a_ld", [128, 8 * 64], BF16)
    lut = kb.tbuf("l0a_lu", [64, D], BF16)
    kb.sbuf("l0a_vec_sb", [128, 32], F32)
    ind = np.zeros((128, 8 * 16), dtype=BF)
    for p in range(8):
        ind[0:64, p * 16 + 2 * p] = 1
        ind[64:128, p * 16 + 2 * p + 1] = 1
    indh = kb.inline("c_l0aind", ind)
    bc = np.zeros((16, 8 * 128), dtype=BF)
    for p in range(8):
        for j in range(128):
            bc[2 * p + j // 64, p * 128 + j] = 1
    bch = kb.inline("c_l0abc", bc.astype(np.float32))
    kb.sbuf("l0a_ind", [128, 8 * 16], BF16)
    kb.sbuf("l0a_bc", [16, 8 * 128], FP8)
    t1 = s.op("sp", _dma(kb.sb["l0a_ind"][:], indh.ap()), sem=kb.small_sem)
    t2 = s.op("gp", _dma(kb.sb["l0a_bc"][:], bch.ap()), sem="dbc")
    t3 = s.op("sp", _dma(kb.sb["l0a_vec_sb"][:], kb.dram["l0a_vec"].ap()),
              sem=kb.small_sem)
    tk = load_wt(kb, cm, kb.dram["l0a_wkT"], wA, D, D, slot=0)
    tv = load_wt(kb, cm, kb.dram["l0a_wvT"], wB, D, D, slot=1)
    to = load_wt(kb, cm, kb.dram["l0a_woT"], wC, D, D, slot=0)
    tld = load_wt(kb, cm, kb.dram["l0a_ldT"], ldt, D, 64, slot=1)
    tlu = None
    for jj in range(2):
        stage = cm.wstage[jj]
        tlu1 = s.op("sp" if jj == 0 else "gp",
                    _dma(stage.t[0:64, 0:512],
                         kb.dram["l0a_luT"].ap()[:, jj * 512:(jj + 1) * 512]),
                    deps=stage.w(), sem=f"dlu{jj}", label="lul")
        stage.add(tlu1)
        tlu = s.op("ve", _copy(lut.t[:, jj * 512:(jj + 1) * 512],
                               stage.t[0:64, 0:512]),
                   deps=[tlu1] + (lut.w() if jj == 0 else []), label="luc")
        stage.add(tlu)
        lut.add(tlu)
    kb.wtok["l0a"] = (wA, tk, wB, tv, wC, to, ldt, tld, lut, tlu, t3, t1, t2)


def emit_l0a(kb, cm, src_dram, dst_dram, src_deps=None, pump=None,
             pump_drain=None):
    s = kb.s
    (wA, twk, wB, twv, wC, two, ldt, tld, lut, tlu, tvec, tind, tbc) = \
        kb.wtok["l0a"]
    vec = kb.sb["l0a_vec_sb"]
    indt = kb.sb["l0a_ind"]
    bct = kb.sb["l0a_bc"]
    kraw = [kb.tbuf(f"l0a_k{m}", [128, TCORE], BF16) for m in range(8)]
    vst = [kb.tbuf(f"l0a_v{i}", [128, D], BF16) for i in range(NCH)]
    S = kb.tbuf("l0a_S", [128, 512], F32)
    Sbf = kb.tbuf("l0a_Sbf", [128, 512], BF16)
    Sg = kb.tbuf("l0a_Sg", [128, 520], F32)
    Ltot = kb.tbuf("l0a_Lt", [128, 8], F32)
    wp = [kb.tbuf(f"l0a_wp{m}", [128, 512], F32) for m in range(8)]
    Qt = kb.tbuf("l0a_Q", [128, 512], F32)
    eK = kb.tbuf("l0a_eK", [128, 512], BF16)
    eR = kb.tbuf("l0a_eR", [128, 512], BF16)
    kt_ = kb.tbuf("l0a_kt", [128, 512], BF16)
    trm = kt_
    rt_ = kb.tbuf("l0a_rt", [128, 512], BF16)
    kut = kb.tbuf("l0a_ku", [128, 512], BF16)
    ktB = kb.tbuf("l0a_ktB", [128, 512], BF16)
    kuB = kb.tbuf("l0a_kuB", [128, 512], BF16)
    kh = kb.tbuf("l0a_kh", [128, 512], BF16)
    ktm = kb.tbuf("l0a_ktm", [128, 512], BF16)
    coltmp = kb.tbuf("l0a_ct", [128, 48], F32)
    scb = kb.tbuf("l0a_scb", [128, 256], BF16)
    sc2b = kb.tbuf("l0a_sc2", [128, 256], BF16)
    rfm = [kb.tbuf(f"l0a_r{m}", [128, 256], BF16) for m in range(8)]
    gfm = [kb.tbuf(f"l0a_g{m}", [128, 256], BF16) for m in range(8)]
    wkv = kb.tbuf("l0a_wkv", [128, 1024], BF16)
    gng = kb.tbuf("l0a_gng", [128, 1024], BF16)
    st16 = gng
    G16 = 768
    gnt = Sg
    oti = kb.tbuf("l0a_o0", [128, D], F32)
    cq = oti
    stf = oti
    SOFF = 96
    ccin = kb.dram_tmp("l0a_ccin", [128, 520])
    ccout = kb.dram_tmp("l0a_ccout", [NCORES * 128, 520])

    tS0 = s.op("ve", _memset(S.t[:], 0.0), deps=S.w(), label="S0a")
    S.add(tS0)
    tL0 = s.op("ve", _memset(Ltot.t[:], 0.0), deps=Ltot.w(), label="L0")
    Ltot.add(tL0)
    tm1 = s.op("ve", _memset(vec[0:64, 18:19], 1.0), deps=[tvec],
               label="mA1")
    tm2 = s.op("ve", _memset(vec[64:128, 18:19], 0.0), deps=[tm1],
               label="mA0")
    tm3 = s.op("ve", _memset(vec[0:64, 19:20], 0.0), deps=[tm2], label="mB0")
    tm4 = s.op("ve", _memset(vec[64:128, 19:20], 1.0), deps=[tm3],
               label="mB1")
    tmu = tm4
    for m in range(8):
        tmu = s.op("ve", _tt(vec[:, 20 + m:21 + m], vec[:, 8 + m:9 + m],
                             vec[:, 18:19], ALU.mult), deps=[tmu],
                   label="uA")

    def w_chain(fm_toks):
        bank = cm.bank()
        ps = bank.t
        deps = bank.w() + fm_toks + [tld]
        tmm = None
        for k in range(8):
            tmm = s.op("pe", _mm(ps[0:64, 0:512],
                                 ldt.t[:, k * 64:(k + 1) * 64],
                                 cm.xnf[k].t[:, 0:512],
                                 start=(k == 0), stop=(k == 7)),
                       deps=deps if k == 0 else [], label="lora1")
        ldt.add(tmm)
        for b in cm.xnf:
            b.add(tmm)
        tth = s.op("se", _act(trm.t[0:64, :], ps[0:64, 0:512], AF.Tanh),
                   deps=[tmm] + trm.w(), label="ltanh")
        bank.add(tth)
        trm.add(tth)
        wtoks = []
        for m in range(8):
            bank2 = cm.bank()
            ps2 = bank2.t
            tm2 = s.op("pe", _mm(ps2[:, 0:512],
                                 lut.t[:, m * 128:(m + 1) * 128],
                                 trm.t[0:64, :],
                                 start=True, stop=True),
                       deps=[tth, tlu] + bank2.w(), label="lora2")
            lut.add(tm2)
            trm.add(tm2)
            e1 = s.op("ve", _ts(wp[m].t[:], ps2[:, 0:512], vec[:, m:m + 1],
                                None, ALU.add), deps=[tm2, tvec] + wp[m].w(),
                      label="spz")
            wp[m].add(e1)
            bank2.add(e1)
            e2 = s.op("se", _act(cm.sqp[m % 2].t[:, 0:512], wp[m].t[:], AF.Exp),
                      deps=[e1] + cm.sqp[m % 2].w(), label="spe")
            cm.sqp[m % 2].add(e2)
            e3 = s.op("se", _act(wp[m].t[:], cm.sqp[m % 2].t[:, 0:512], AF.Ln,
                                 bias=1.0), deps=[e2], label="spl")
            wp[m].add(e3)
            cm.sqp[m % 2].add(e3)
            wtoks.append(e3)
        return wtoks

    def chunk_decay(c, mh, wtoks, track_ltot):
        cc = c * 128
        sc_toks = []
        qtw = Qt.w()
        for i in range(4):
            m = mh * 4 + i
            t1 = s.op("ve", _scan_add(Qt.t[:, i * 128:(i + 1) * 128],
                                      wp[m].t[:, cc:cc + 128],
                                      cm.zeros[:, 0:128]),
                      deps=[wtoks[m], cm.tz] + (qtw if i == 0 else []),
                      label="scan")
            Qt.add(t1)
            wp[m].add(t1)
            sc_toks.append(t1)
        ctw = coltmp.w()
        ct_toks = []
        for i in range(4):
            m = mh * 4 + i
            t2 = s.op("ve", _copy(coltmp.t[:, m:m + 1],
                                  Qt.t[:, i * 128 + 63:i * 128 + 64]),
                      deps=[sc_toks[i]] + (ctw if i == 0 else []),
                      label="qmid")
            coltmp.add(t2)
            Qt.add(t2)
            t3 = s.op("ve", _ts(Qt.t[:, i * 128:(i + 1) * 128],
                                Qt.t[:, i * 128:(i + 1) * 128],
                                coltmp.t[:, m:m + 1], None, ALU.subtract),
                      deps=[t2], label="qsub")
            Qt.add(t3)
            coltmp.add(t3)
            t4 = s.op("ve", _copy(coltmp.t[:, 8 + m:9 + m],
                                  Qt.t[:, i * 128 + 127:i * 128 + 128]),
                      deps=[t3], label="qlast")
            coltmp.add(t4)
            Qt.add(t4)
            ct_toks.append(t4)
        c4_ = mh * 4
        teK = s.op("se", _act(eK.t[:], Qt.t[:], AF.Exp),
                   deps=ct_toks + eK.w(), label="ek")
        eK.add(teK)
        Qt.add(teK)
        t5 = s.op("se", _act(coltmp.t[:, 16 + c4_:20 + c4_],
                             coltmp.t[:, 8 + c4_:12 + c4_], AF.Exp,
                             scale=-1.0), deps=ct_toks, label="eql")
        coltmp.add(t5)
        t6 = s.op("ve", _tt(coltmp.t[:, 32 + c4_:36 + c4_],
                            coltmp.t[:, c4_:4 + c4_],
                            coltmp.t[:, 8 + c4_:12 + c4_], ALU.add),
                  deps=ct_toks, label="qlsum")
        coltmp.add(t6)
        t7 = s.op("se", _act(coltmp.t[:, 24 + c4_:28 + c4_],
                             coltmp.t[:, 32 + c4_:36 + c4_], AF.Exp,
                             scale=-1.0), deps=[t6], label="dcol")
        coltmp.add(t7)
        t7b = s.op("se", _act(coltmp.t[:, 40 + c4_:44 + c4_],
                              coltmp.t[:, c4_:4 + c4_], AF.Exp,
                              scale=-1.0), deps=ct_toks, label="eqm")
        coltmp.add(t7b)
        if track_ltot:
            t8 = s.op("ve", _tt(Ltot.t[:, c4_:4 + c4_],
                                Ltot.t[:, c4_:4 + c4_],
                                coltmp.t[:, 32 + c4_:36 + c4_],
                                ALU.add), deps=[t6, Ltot.deps[-1]],
                      label="lacc")
            Ltot.add(t8)
            coltmp.add(t8)
        return teK, t5, t7

    def chunk_state(g, c, mh, teK, teql, tdcol):
        cc = c * 128
        gc = g * 4 + c
        khw = kh.w()
        kh_toks = []
        for i in range(4):
            m = mh * 4 + i
            t1 = s.op("ve", _tt(kh.t[:, i * 128:(i + 1) * 128],
                                kraw[m].t[:, g * 512 + cc:g * 512 + cc + 128],
                                eK.t[:, i * 128:(i + 1) * 128], ALU.mult),
                      deps=[teK] +
                           ([kraw[m].deps[-1]] if kraw[m].deps else []) +
                           (khw if i == 0 else []), label="kh1")
            kh.add(t1)
            kraw[m].add(t1)
            eK.add(t1)
            t2 = s.op("ve", _ts(kh.t[:, i * 128:(i + 1) * 128],
                                kh.t[:, i * 128:(i + 1) * 128],
                                coltmp.t[:, 16 + m:17 + m], None, ALU.mult),
                      deps=[t1, teql], label="kh2")
            kh.add(t2)
            coltmp.add(t2)
            kh_toks.append(t2)
        kt_toks = []
        ktw = ktm.w()
        for i in range(4):
            tb = cm.tbank()
            ps = tb.t
            tt_ = s.op("pe", _tr(ps[:, 0:128], kh.t[:, i * 128:(i + 1) * 128],
                                 cm.ident[:]),
                       deps=[kh_toks[i], cm.tconst] + tb.w() +
                            (ktw if i == 0 else []), label="khtr")
            te = s.op("se", _copy(ktm.t[:, i * 128:(i + 1) * 128],
                                  ps[:, 0:128]), deps=[tt_], label="khtre")
            tb.add(te)
            ktm.add(te)
            kh.add(tt_)
            kt_toks.append(te)
        supd = []
        for i in range(4):
            m = mh * 4 + i
            bank = cm.bank()
            ps = bank.t
            bw = bank.w()
            m1 = s.op("pe", _mm(ps[0:64, 0:64],
                                ktm.t[:, i * 128:i * 128 + 64],
                                vst[gc].t[:, (2 * m) * 64:(2 * m + 1) * 64],
                                start=True, stop=True, tp=(0, 0)),
                      deps=[kt_toks[i],
                            vst[gc].deps[-1] if vst[gc].deps else None] + bw,
                      label="stA")
            m2 = s.op("pe", _mm(ps[64:128, 0:64],
                                ktm.t[:, i * 128 + 64:i * 128 + 128],
                                vst[gc].t[:, (2 * m + 1) * 64:
                                          (2 * m + 2) * 64],
                                start=True, stop=True, tp=(0, 64)),
                      deps=[], label="stB")
            ktm.add(m1, m2)
            vst[gc].add(m1, m2)
            tu = s.op("ve", _stt(S.t[:, m * 64:(m + 1) * 64],
                                 S.t[:, m * 64:(m + 1) * 64],
                                 coltmp.t[:, 24 + m:25 + m], ps[:, 0:64],
                                 ALU.mult, ALU.add),
                      deps=[m1, m2, tdcol] +
                           ([S.deps[-1]] if S.deps else []), label="sup")
            bank.add(tu)
            S.add(tu)
            coltmp.add(tu)
            supd.append(tu)
        return supd

    for g in range(4):
        fm_toks, per_chunk = cm.group_fm(src_dram, g, src_deps=src_deps)
        kevs = []

        def k_cb(m, ps, deps):
            t = s.op("se", _copy(kraw[m].t[:, g * 512:(g + 1) * 512], ps),
                     deps=deps + kraw[m].w(), label="kev")
            kraw[m].add(t)
            kevs.append(t)
            return t
        proj_fm(kb, cm, cm.xnf, 0, 512, wA, [twk], fm_toks, k_cb)

        def v_cb(ci, gg, ps, deps):
            vb = vst[g * 4 + ci]
            t = s.op("ve", _copy(vb.t[:, gg * 512:(gg + 1) * 512], ps),
                     deps=deps + (vb.w() if gg == 0 else []), label="vev")
            vb.add(t)
            return t
        proj_tm(kb, cm, cm.xnf, 0, wB, [twv], fm_toks, 4, v_cb)

        wtoks = w_chain(fm_toks)
        for c in range(4):
            for mh in range(2):
                teK, teql, tdcol = chunk_decay(c, mh, wtoks, True)
                if pump is not None:
                    pump(2)
                chunk_state(g, c, mh, teK, teql, tdcol)
                if pump is not None:
                    pump(2)

    if pump_drain is not None:
        pump_drain()
    slast = S.deps[-1]
    tc1 = s.op("sp", _dma(ccin.ap()[:, 0:512], S.t[:]), deps=[slast],
               sem="dgw")
    S.add(tc1)
    tc2 = s.op("sp", _dma(ccin.ap()[:, 512:520], Ltot.t[:]),
               deps=[Ltot.deps[-1]], sem="dgw")
    Ltot.add(tc2)
    tcc = s.op("gp", _cc_allgather(ccin.ap().opt(), ccout.ap().opt()),
               deps=[tc1, tc2], sem="ccw", n=1, label="agW")
    twr = load_wt(kb, cm, kb.dram["l0a_wrT"], wA, D, D, slot=0)
    twg = load_wt(kb, cm, kb.dram["l0a_wgT"], wB, D, D, slot=1)
    tz1 = s.op("ve", _memset(cq.t[:, 0:96], 0.0), deps=cq.w(), label="cqz")
    cq.add(tz1)
    tprev = tz1
    for c in range(NCORES):
        tg = s.op("sp", _dma(Sg.t[:, 512:520],
                             ccout.ap()[c * 128:(c + 1) * 128, 512:520]),
                  deps=[tcc] + Sg.w(), sem="dgw", label="sgq")
        Sg.add(tg)
        t1 = s.op("ve", _tt(cq.t[:, (c + 1) * 8:(c + 2) * 8],
                            cq.t[:, c * 8:(c + 1) * 8], Sg.t[:, 512:520],
                            ALU.add), deps=[tg, tprev], label="cum")
        cq.add(t1)
        Sg.add(t1)
        t2 = s.op("ve", _stt(cq.t[:, 80:88], Sg.t[:, 512:520],
                             cm.meta[:, 16 + c:17 + c], cq.t[:, 80:88],
                             ALU.mult, ALU.add), deps=[tg, tprev, cm.tconst],
                  label="myq")
        cq.add(t2)
        Sg.add(t2)
        tprev = t2
    tS1 = s.op("ve", _memset(S.t[:], 0.0), deps=S.w(), label="S0b")
    S.add(tS1)
    tprev2 = tS1
    for c in range(NCORES):
        tg = s.op("sp", _dma(Sg.t[:, 0:512],
                             ccout.ap()[c * 128:(c + 1) * 128, 0:512]),
                  deps=[tcc] + Sg.w(), sem="dgw", label="sgl")
        Sg.add(tg)
        t1 = s.op("ve", _tt(cq.t[:, 88:96], cq.t[:, (c + 1) * 8:(c + 2) * 8],
                            cq.t[:, 80:88], ALU.subtract),
                  deps=[tprev, tprev2] + ([cq.deps[-1]] if cq.deps else []),
                  label="efs")
        cq.add(t1)
        t2 = s.op("ve", _ts(cq.t[:, 88:96], cq.t[:, 88:96], 0.0, None,
                            ALU.min), deps=[t1], label="efc")
        cq.add(t2)
        t3 = s.op("se", _act(cq.t[:, 88:96], cq.t[:, 88:96], AF.Exp),
                  deps=[t2], label="efe")
        cq.add(t3)
        t4 = s.op("ve", _ts(cq.t[:, 88:96], cq.t[:, 88:96],
                            cm.meta[:, c:c + 1], None, ALU.mult),
                  deps=[t3, cm.tconst], label="efm")
        cq.add(t4)
        for m in range(8):
            t5 = s.op("ve", _stt(S.t[:, m * 64:(m + 1) * 64],
                                 Sg.t[:, m * 64:(m + 1) * 64],
                                 cq.t[:, 88 + m:89 + m],
                                 S.t[:, m * 64:(m + 1) * 64], ALU.mult,
                                 ALU.add), deps=[t4, tg] +
                      ([S.deps[-1]] if S.deps else []), label="sacc")
            S.add(t5)
            Sg.add(t5)
            cq.add(t5)
        tprev2 = S.deps[-1]
    tsb = s.op("ve", _copy(Sbf.t[:], S.t[:]), deps=[S.deps[-1]] + Sbf.w(),
               label="sbf")
    Sbf.add(tsb)
    S.add(tsb)

    store_toks = []
    for g in range(4):
        fm_toks, per_chunk = cm.group_fm(src_dram, g, src_deps=src_deps)
        wtoks = w_chain(fm_toks)
        for hq in range(2):
            revs = []

            def r_cb(m, ps, deps):
                t = s.op("se", _copy(rfm[m].t[:], ps),
                         deps=deps + rfm[m].w(), label="rev")
                rfm[m].add(t)
                revs.append(t)
                return t
            proj_fm(kb, cm, cm.xnf, hq * 256, 256, wA, [twr], fm_toks, r_cb)
            gevs = []

            def g_cb(m, ps, deps):
                t = s.op("se", _act(gfm[m].t[:], ps, AF.Sigmoid),
                         deps=deps + gfm[m].w(), label="gev")
                gfm[m].add(t)
                gevs.append(t)
                return t
            proj_fm(kb, cm, cm.xnf, hq * 256, 256, wB, [twg], fm_toks, g_cb)

            for c01 in range(2):
                c = hq * 2 + c01
                gc = g * 4 + c
                cc = c * 128
                rc = c01 * 128
                wkw = wkv.w()
                wkv_toks = []
                for mh in range(2):
                    teK, teql, tdcol = chunk_decay(c, mh, wtoks, False)
                    if pump is not None:
                        pump(2)
                    er_toks = []
                    for i in range(4):
                        m = mh * 4 + i
                        t1 = s.op("ve", _tt(Qt.t[:, i * 128:(i + 1) * 128],
                                            wp[m].t[:, cc:cc + 128],
                                            Qt.t[:, i * 128:(i + 1) * 128],
                                            ALU.subtract),
                                  deps=[teK], label="ers")
                        Qt.add(t1)
                        wp[m].add(t1)
                        er_toks.append(t1)
                    teR = s.op("se", _act(eR.t[:], Qt.t[:], AF.Exp),
                               deps=er_toks + eR.w(), label="er")
                    eR.add(teR)
                    Qt.add(teR)
                    ktoks, rtoks, kutoks = [], [], []
                    ktw, rtw, kuw = kt_.w(), rt_.w(), kut.w()
                    ktBw = ktB.w()
                    kuBw = kuB.w()
                    for i in range(4):
                        m = mh * 4 + i
                        sl = slice(i * 128, (i + 1) * 128)
                        kr_sl = kraw[m].t[:, g * 512 + cc:g * 512 + cc + 128]
                        krd = ([kraw[m].deps[-1]] if kraw[m].deps else [])
                        t1 = s.op("ve", _stt(kt_.t[:, sl], kr_sl,
                                             vec[:, 18:19], eK.t[:, sl],
                                             ALU.mult, ALU.mult),
                                  deps=[teK, tmu] + krd +
                                       (ktw if i == 0 else []),
                                  label="ktil")
                        kt_.add(t1)
                        kraw[m].add(t1)
                        eK.add(t1)
                        t1b = s.op("ve", _stt(ktB.t[:, sl], kr_sl,
                                              vec[:, 19:20], eK.t[:, sl],
                                              ALU.mult, ALU.mult),
                                   deps=[teK, tmu] + krd +
                                        (ktBw if i == 0 else []),
                                   label="ktilB")
                        ktB.add(t1b)
                        kraw[m].add(t1b)
                        eK.add(t1b)
                        ktoks.append((t1, t1b))
                        t2 = s.op("ve", _tt(rt_.t[:, sl],
                                            rfm[m].t[:, rc:rc + 128],
                                            eR.t[:, sl], ALU.mult),
                                  deps=[teR, revs[m]] +
                                       (rtw if i == 0 else []),
                                  label="rtil")
                        rt_.add(t2)
                        rfm[m].add(t2)
                        eR.add(t2)
                        rtoks.append(t2)
                        t3 = s.op("ve", _ts(kut.t[:, sl], kr_sl,
                                            vec[:, 20 + m:21 + m], None,
                                            ALU.mult),
                                  deps=[tmu] + krd +
                                       (kuw if i == 0 else []),
                                  label="kut")
                        kut.add(t3)
                        kraw[m].add(t3)
                        t3b = s.op("ve", _stt(kuB.t[:, sl], kr_sl,
                                              vec[:, 8 + m:9 + m],
                                              kut.t[:, sl],
                                              ALU.mult, ALU.subtract),
                                   deps=[t3] + krd +
                                        (kuBw if i == 0 else []),
                                   label="kutB")
                        kuB.add(t3b)
                        kut.add(t3b)
                        kraw[m].add(t3b)
                        kutoks.append((t3, t3b))
                    rh_toks = []
                    khw0 = kh.w()
                    for i in range(4):
                        m = mh * 4 + i
                        sl = slice(i * 128, (i + 1) * 128)
                        t4 = s.op("ve", _ts(kh.t[:, sl], rt_.t[:, sl],
                                            coltmp.t[:, 40 + m:41 + m], None,
                                            ALU.mult),
                                  deps=[rtoks[i], coltmp.deps[-1]] +
                                       (khw0 if i == 0 else []), label="rhat")
                        kh.add(t4)
                        rt_.add(t4)
                        coltmp.add(t4)
                        rh_toks.append(t4)
                    for i in range(4):
                        m = mh * 4 + i
                        sl = slice(i * 128, (i + 1) * 128)
                        bsc = cm.bank()
                        ps1 = bsc.t
                        b1 = bsc.w()
                        sA = s.op("pe", _mm(ps1[:, 0:128],
                                            kt_.t[:, sl],
                                            rt_.t[:, sl],
                                            start=True, stop=True),
                                  deps=[ktoks[i][0], rtoks[i]] + b1,
                                  label="scA")
                        sB = s.op("pe", _mm(ps1[:, 128:256],
                                            ktB.t[:, sl],
                                            rt_.t[:, sl],
                                            start=True, stop=True),
                                  deps=[ktoks[i][1]], label="scB")
                        kt_.add(sA)
                        ktB.add(sB)
                        rt_.add(sA, sB)
                        bsc2 = cm.bank()
                        ps2 = bsc2.t
                        b2 = bsc2.w()
                        dA = s.op("pe", _mm(ps2[:, 0:128],
                                            kut.t[:, sl],
                                            rfm[m].t[:, rc:rc + 128],
                                            start=True, stop=True),
                                  deps=[kutoks[i][0], revs[m]] + b2,
                                  label="sdA")
                        dB = s.op("pe", _mm(ps2[:, 128:256],
                                            kuB.t[:, sl],
                                            rfm[m].t[:, rc:rc + 128],
                                            start=True, stop=True),
                                  deps=[kutoks[i][1]], label="sdB")
                        kut.add(dA)
                        kuB.add(dB)
                        rfm[m].add(dA, dB)
                        e1 = s.op("ve", _stt(scb.t[:], ps1[:, 0:256], 1.0,
                                             cm.msl2[:], ALU.bypass,
                                             ALU.mult),
                                  deps=[sA, sB, cm.tconst] + scb.w(),
                                  label="scev")
                        bsc.add(e1)
                        scb.add(e1)
                        e2 = s.op("ve", _stt(sc2b.t[:], ps2[:, 0:256], 1.0,
                                             cm.idd2[:], ALU.bypass,
                                             ALU.mult),
                                  deps=[dA, dB, cm.tconst] + sc2b.w(),
                                  label="sc2ev")
                        bsc2.add(e2)
                        sc2b.add(e2)
                        bo = cm.bank()
                        pso = bo.t
                        b3 = bo.w()
                        vdep = [vst[gc].deps[-1]] if vst[gc].deps else []
                        o1 = s.op("pe", _mm(pso[0:64, 0:128],
                                            vst[gc].t[:, (2 * m) * 64:
                                                      (2 * m + 1) * 64],
                                            scb.t[:, 0:128], start=True,
                                            stop=False, tp=(0, 0)),
                                  deps=[e1] + vdep + b3, label="oiA")
                        o2 = s.op("pe", _mm(pso[0:64, 0:128],
                                            vst[gc].t[:, (2 * m) * 64:
                                                      (2 * m + 1) * 64],
                                            sc2b.t[:, 0:128], start=False,
                                            stop=False, tp=(0, 0)),
                                  deps=[e2], label="odA")
                        o3 = s.op("pe", _mm(pso[0:64, 0:128],
                                            Sbf.t[0:64, m * 64:(m + 1) * 64],
                                            kh.t[0:64, sl],
                                            start=False, stop=True,
                                            tp=(0, 0)),
                                  deps=[rh_toks[i]] +
                                       ([Sbf.deps[-1]] if Sbf.deps else []),
                                  label="onA")
                        o4 = s.op("pe", _mm(pso[64:128, 0:128],
                                            vst[gc].t[:, (2 * m + 1) * 64:
                                                      (2 * m + 2) * 64],
                                            scb.t[:, 128:256], start=True,
                                            stop=False, tp=(0, 64)),
                                  deps=[e1], label="oiB")
                        o5 = s.op("pe", _mm(pso[64:128, 0:128],
                                            vst[gc].t[:, (2 * m + 1) * 64:
                                                      (2 * m + 2) * 64],
                                            sc2b.t[:, 128:256], start=False,
                                            stop=False, tp=(0, 64)),
                                  deps=[e2], label="odB")
                        o6 = s.op("pe", _mm(pso[64:128, 0:128],
                                            Sbf.t[64:128,
                                                  m * 64:(m + 1) * 64],
                                            kh.t[64:128, sl],
                                            start=False, stop=True,
                                            tp=(64, 64)),
                                  deps=[], label="onB")
                        vst[gc].add(o1, o2, o4, o5)
                        scb.add(o1, o4)
                        sc2b.add(o2, o5)
                        Sbf.add(o3, o6)
                        kh.add(o3, o6)
                        ev = s.op("ve", _copy(wkv.t[:, m * 128:(m + 1) * 128],
                                              pso[:, 0:128]),
                                  deps=[o3, o6] +
                                       (wkw if (mh == 0 and i == 0) else []),
                                  label="wkvev")
                        bo.add(ev)
                        wkv.add(ev)
                        wkv_toks.append(ev)
                    chunk_state(g, c, mh, teK, teql, tdcol)
                    if pump is not None:
                        pump(2)
                tsb2 = s.op("ve", _copy(Sbf.t[:], S.t[:]),
                            deps=[S.deps[-1]] + Sbf.w(), label="sbf2")
                Sbf.add(tsb2)
                S.add(tsb2)

                sqv = cm.sq()
                tsq = s.op("se", _act(sqv.t[:], wkv.t[:], AF.Square),
                           deps=[wkv.deps[-1]] + sqv.w(), label="gnsq")
                sqv.add(tsq)
                wkv.add(tsq)
                bst = cm.bank()
                psst = bst.t
                bstw = bst.w()
                tmm = None
                for p in range(8):
                    tmm = s.op("pe", _mm(psst[0:16, 0:128],
                                         indt[:, p * 16:(p + 1) * 16],
                                         wkv.t[:, p * 128:(p + 1) * 128],
                                         start=(p == 0), stop=False),
                               deps=([wkv.deps[-1], tind] + bstw)
                               if p == 0 else [], label="gns")
                for p in range(8):
                    tmm = s.op("pe", _mm(psst[0:16, 128:256],
                                         indt[:, p * 16:(p + 1) * 16],
                                         sqv.t[:, p * 128:(p + 1) * 128],
                                         start=False, stop=(p == 7)),
                               deps=[tsq] if p == 0 else [], label="gns2")
                wkv.add(tmm)
                sqv.add(tmm)
                c1 = s.op("ve", _ts(stf.t[0:16, SOFF:SOFF + 128], psst[0:16, 0:128],
                                    1.0 / 64, None, ALU.mult),
                          deps=[tmm] + stf.w(), label="gm")
                stf.add(c1)
                c2 = s.op("ve", _ts(stf.t[0:16, SOFF + 128:SOFF + 256], psst[0:16, 128:256],
                                    1.0 / 64, None, ALU.mult), deps=[tmm],
                          label="gq")
                stf.add(c2)
                bst.add(c1, c2)
                c3 = s.op("ve", _tt(stf.t[0:16, SOFF + 256:SOFF + 384],
                                    stf.t[0:16, SOFF:SOFF + 128],
                                    stf.t[0:16, SOFF:SOFF + 128],
                                    ALU.mult), deps=[c1],
                          label="gm2")
                stf.add(c3)
                c4 = s.op("ve", _tt(stf.t[0:16, SOFF + 256:SOFF + 384],
                                    stf.t[0:16, SOFF + 128:SOFF + 256],
                                    stf.t[0:16, SOFF + 256:SOFF + 384],
                                    ALU.subtract),
                          deps=[c2, c3], label="gvar")
                stf.add(c4)
                c5 = s.op("se", _act(stf.t[0:16, SOFF + 256:SOFF + 384],
                                     stf.t[0:16, SOFF + 256:SOFF + 384],
                                     AF.Sqrt, bias=cm.epscol[0:16, 0:1]),
                          deps=[c4, cm.tepc], label="gsd")
                stf.add(c5)
                c6 = s.op("ve", _recip(stf.t[0:16, SOFF + 256:SOFF + 384],
                                       stf.t[0:16, SOFF + 256:SOFF + 384]),
                          deps=[c5], label="grs")
                stf.add(c6)
                c7 = s.op("ve", _copy(st16.t[0:16, G16:G16 + 128],
                                      stf.t[0:16, SOFF + 256:SOFF + 384]),
                          deps=[c6] + st16.w(), label="gc1")
                st16.add(c7)
                c8 = s.op("ve", _tt(stf.t[0:16, SOFF:SOFF + 128],
                                    stf.t[0:16, SOFF:SOFF + 128],
                                    stf.t[0:16, SOFF + 256:SOFF + 384],
                                    ALU.mult), deps=[c6],
                          label="gmr")
                stf.add(c8)
                c9 = s.op("ve", _copy(st16.t[0:16, G16 + 128:G16 + 256],
                                      stf.t[0:16, SOFF:SOFF + 128]),
                          deps=[c8], label="gc2")
                st16.add(c9)
                stf.add(c9)
                ggw = gng.w()
                ga4s = []
                for mh2 in range(2):
                    bA = cm.bank()
                    psA = bA.t
                    bAw = bA.w()
                    bB = cm.bank()
                    psB = bB.t
                    bBw = bB.w()
                    ta = tb_ = None
                    for i in range(4):
                        p = mh2 * 4 + i
                        ta = s.op("pe", _mm(psA[:, i * 128:(i + 1) * 128],
                                            bct[:, p * 128:(p + 1) * 128],
                                            st16.t[0:16, G16:G16 + 128],
                                            start=(i == 0), stop=(i == 3)),
                                  deps=([c7, tbc] + bAw) if i == 0 else [],
                                  label="gbA")
                        tb_ = s.op("pe", _mm(psB[:, i * 128:(i + 1) * 128],
                                             bct[:, p * 128:(p + 1) * 128],
                                             st16.t[0:16,
                                                    G16 + 128:G16 + 256],
                                             start=(i == 0), stop=(i == 3)),
                                   deps=([c9, tbc] + bBw) if i == 0 else [],
                                   label="gbB")
                    st16.add(ta, tb_)
                    a1 = s.op("ve", _tt(gnt.t[:, 0:512],
                                        wkv.t[:, mh2 * 512:
                                              mh2 * 512 + 512],
                                        psA[:, 0:512], ALU.mult),
                              deps=[ta, wkv.deps[-1]] + gnt.w(), label="ga1")
                    gnt.add(a1)
                    bA.add(a1)
                    wkv.add(a1)
                    a2 = s.op("ve", _tt(gnt.t[:, 0:512], gnt.t[:, 0:512],
                                        psB[:, 0:512], ALU.subtract),
                              deps=[a1, tb_], label="ga2")
                    gnt.add(a2)
                    bB.add(a2)
                    a3 = s.op("ve", _ts(gnt.t[:, 0:512], gnt.t[:, 0:512],
                                        vec[:, 16:17], vec[:, 17:18],
                                        ALU.mult, ALU.add),
                              deps=[a2, tvec], label="ga3")
                    gnt.add(a3)
                    for i in range(4):
                        p = mh2 * 4 + i
                        a4 = s.op("ve", _tt(gng.t[:, p * 128:(p + 1) * 128],
                                            gnt.t[:, i * 128:(i + 1) * 128],
                                            gfm[p].t[:, rc:rc + 128],
                                            ALU.mult),
                                  deps=[a3, gevs[p]] +
                                       (ggw if p == 0 else []),
                                  label="ga4")
                        gng.add(a4)
                        gnt.add(a4)
                        gfm[p].add(a4)
                        ga4s.append(a4)
                ob = oti
                obw = ob.w()
                o_toks = []
                for nn_ in range(2):
                    bank = cm.bank()
                    ps = bank.t
                    deps = bank.w() + [gng.deps[-1], two]
                    tmm = None
                    for p in range(8):
                        tmm = s.op("pe", _mm(ps[:, 0:512],
                                             gng.t[:, p * 128:(p + 1) * 128],
                                             wC.t[:, p * D + nn_ * 512:
                                                  p * D + nn_ * 512 + 512],
                                             start=(p == 0), stop=(p == 7)),
                                   deps=deps if p == 0 else [], label="wom")
                    wC.add(tmm)
                    gng.add(tmm)
                    tev = s.op("se", _copy(ob.t[:, nn_ * 512:(nn_ + 1) * 512],
                                           ps[:, 0:512]),
                               deps=[tmm] + (obw if nn_ == 0 else []),
                               label="woev")
                    bank.add(tev)
                    ob.add(tev)
                    o_toks.append(tev)
                tst, _ = epilogue_store(kb, cm, ob, o_toks, c, dst_dram, gc)
                store_toks.append(tst)
    return store_toks



def build(blocks):
    nc = bass.Bass("TRN2", target_bir_lowering=False)
    kb = KB(nc)
    s = kb.s

    x_h = kb.dram_in("x", [TCORE, D])
    y_h = kb.dram_out("y", [TCORE, D])
    kb.dram_in("meta_in", [128, 32])
    if "l1f" in blocks:
        kb.dram_in("l1f_wrT", [2 * D, 512])
        kb.dram_in("l1f_wkT", [2 * D, 512])
        kb.dram_in("l1f_wvT", [2 * D, 512])
        kb.dram_in("l1f_mu", [128, 16])
    if "l0a" in blocks:
        for nm in ("wr", "wk", "wv", "wg", "wo"):
            kb.dram_in(f"l0a_{nm}T", [2 * D, 512])
        kb.dram_in("l0a_ldT", [D, 64])
        kb.dram_in("l0a_luT", [64, D])
        kb.dram_in("l0a_vec", [128, 32])
    if "l0f" in blocks:
        kb.dram_in("l0f_w1T", [64 * 128, 512])
        kb.dram_in("l0f_w2T", [64 * 128, 512])
        kb.dram_in("l0f_cw", [128, 128])
    if "l1a" in blocks:
        for nm in ("wq", "wk", "wv", "wo"):
            kb.dram_in(f"l1a_{nm}T", [2 * D, 512])
        kb.dram_in("l1a_bqk", [128, 16])
        kb.dram_in("l1a_bvo", [128, 2 * D])

    bufs = [x_h]
    for i in range(len(blocks) - 1):
        bufs.append(kb.dram_tmp(f"xb{i + 1}", [TCORE, D]))
    bufs.append(y_h)

    cm = Common(kb)
    cm.setup()

    from contextlib import ExitStack
    all_store = []
    prev_st = None
    l0f_fin = None
    for bi, bl in enumerate(blocks):
        src, dst = bufs[bi], bufs[bi + 1]
        kb.scope = ExitStack()
        kb.small_sem = f"dms{bi}"
        s.batch_sems.add(kb.small_sem)
        if bl == "l1f":
            prep_l1f(kb, cm)
            st = emit_l1f(kb, cm, src, dst, src_deps=prev_st)
        elif bl == "l1a":
            prep_l1a(kb, cm)
            st = emit_l1a(kb, cm, src, dst, src_deps=prev_st)
        elif bl == "l0f":
            ged = gather_edge_rows(kb, cm, src, idx=1, src_deps=prev_st)
            prep_l0f(kb, cm, pass1_finish=l0f_fin)
            st = emit_l0f(kb, cm, src, dst, src_deps=prev_st, gather=ged)
        elif bl == "l0a":
            prep_l0a(kb, cm)
            pump = pdrain = None
            if bi + 1 < len(blocks) and blocks[bi + 1] == "l0f":
                pump, pdrain, l0f_fin = make_l0f_pass1(kb, cm)
            st = emit_l0a(kb, cm, src, dst, src_deps=prev_st, pump=pump,
                          pump_drain=pdrain)
        else:
            raise NotImplementedError(bl)
        s.barrier()
        kb.scope.close()
        kb.scope = None
        all_store += st
        prev_st = st

    s.op("sp", _nop(), deps=all_store, label="fin")

    with nc.Block() as block:
        s.emit(block)
    return nc



def _blk(wT, cc=512):
    wT = np.ascontiguousarray(wT)
    R, C = wT.shape
    cc = min(C, cc)
    return np.ascontiguousarray(
        wT.reshape(R // 128, 128, C // cc, cc).transpose(0, 2, 1, 3)
        .reshape(-1, cc))


def _t(a):
    return np.ascontiguousarray(np.asarray(a, dtype=np.float32).T)


def _fmcols(v, ncol):
    v = np.asarray(v, dtype=np.float32).reshape(ncol, 128).T
    return np.ascontiguousarray(v)


def kernel(L0_attn_x, L0_attn_w_r, L0_attn_w_k, L0_attn_w_v, L0_attn_w_o,
           L0_attn_w_g, L0_attn_w_base, L0_attn_w_lora_down, L0_attn_w_lora_up,
           L0_attn_u, L0_attn_ln_gamma, L0_attn_ln_beta, L0_ffn_w1,
           L0_ffn_conv_w, L0_ffn_conv_b, L0_ffn_w2, L1_attn_Wq, L1_attn_bq,
           L1_attn_Wk, L1_attn_bk, L1_attn_Wv, L1_attn_bv, L1_attn_Wo,
           L1_attn_bo, L1_ffn_W_r, L1_ffn_W_k, L1_ffn_W_v, L1_ffn_mu_r,
           L1_ffn_mu_k):
    blocks = list(BLOCKS)
    x = np.ascontiguousarray(np.asarray(L0_attn_x, dtype=np.float32)
                             .reshape(2 * 8192, D))
    shards = [x[c * TCORE:(c + 1) * TCORE] for c in range(NCORES)]

    nc = build(blocks)

    shared = {}
    if "l0a" in blocks:
        shared["l0a_wrT"] = _blk(_t(L0_attn_w_r))
        shared["l0a_wkT"] = _blk(_t(L0_attn_w_k))
        shared["l0a_wvT"] = _blk(_t(L0_attn_w_v))
        shared["l0a_wgT"] = _blk(_t(L0_attn_w_g))
        shared["l0a_woT"] = _blk(_t(L0_attn_w_o))
        shared["l0a_ldT"] = _blk(_t(L0_attn_w_lora_down), cc=64)
        shared["l0a_luT"] = _t(L0_attn_w_lora_up)
        av = np.zeros((128, 32), dtype=np.float32)
        av[:, 0:8] = _fmcols(L0_attn_w_base, 8)
        u = np.asarray(L0_attn_u, np.float32)
        up = np.zeros((128, 8), dtype=np.float32)
        for h in range(16):
            up[(h % 2) * 64:(h % 2) * 64 + 64, h // 2] = u[h]
        av[:, 8:16] = up
        av[:, 16] = np.tile(np.asarray(L0_attn_ln_gamma, np.float32), 2)
        av[:, 17] = np.tile(np.asarray(L0_attn_ln_beta, np.float32), 2)
        shared["l0a_vec"] = av
    if "l0f" in blocks:
        shared["l0f_w1T"] = _blk(_t(L0_ffn_w1))
        shared["l0f_w2T"] = _blk(_t(L0_ffn_w2))
        cw = np.zeros((128, 128), dtype=np.float32)
        cwa = np.asarray(L0_ffn_conv_w, np.float32)
        for t in range(3):
            cw[:, t * 32:(t + 1) * 32] = _fmcols(cwa[:, 0, t], 32)
        cw[:, 96:128] = _fmcols(L0_ffn_conv_b, 32)
        shared["l0f_cw"] = cw
    if "l1a" in blocks:
        shared["l1a_wqT"] = _blk(_t(L1_attn_Wq))
        shared["l1a_wkT"] = _blk(_t(L1_attn_Wk))
        shared["l1a_wvT"] = _blk(_t(L1_attn_Wv))
        shared["l1a_woT"] = _blk(_t(L1_attn_Wo))
        bqk = np.zeros((128, 16), dtype=np.float32)
        bqk[:, 0:8] = _fmcols(L1_attn_bq, 8)
        bqk[:, 8:16] = _fmcols(L1_attn_bk, 8)
        shared["l1a_bqk"] = bqk
        bvo = np.zeros((128, 2 * D), dtype=np.float32)
        bvo[:, 0:D] = np.tile(np.asarray(L1_attn_bv, np.float32)[None, :],
                              (128, 1))
        bvo[:, D:] = np.tile(np.asarray(L1_attn_bo, np.float32)[None, :],
                             (128, 1))
        shared["l1a_bvo"] = bvo
    if "l1f" in blocks:
        shared["l1f_wrT"] = _blk(_t(L1_ffn_W_r))
        shared["l1f_wkT"] = _blk(_t(L1_ffn_W_k))
        shared["l1f_wvT"] = _blk(_t(L1_ffn_W_v))
        mu = np.zeros((128, 16), dtype=np.float32)
        mu[:, 0:8] = _fmcols(L1_ffn_mu_r, 8)
        mu[:, 8:16] = _fmcols(L1_ffn_mu_k, 8)
        shared["l1f_mu"] = mu

    in_maps = []
    for c in range(NCORES):
        meta = np.zeros((128, 32), dtype=np.float32)
        grp = c // 4
        for cc in range(NCORES):
            if cc // 4 == grp and cc < c:
                meta[:, cc] = 1.0
        if c % 4 != 0:
            meta[c - 1, 8] = 1.0
            meta[2 * (c - 1) + 1, 9] = 1.0
        if c % 4 != 3:
            meta[2 * (c + 1), 10] = 1.0
        for cc in range(NCORES):
            if cc < c:
                meta[:, 16 + cc] = 1.0
        m = {"x": shards[c], "meta_in": meta}
        m.update(shared)
        in_maps.append(m)

    res = run_bass_kernel_spmd(nc, in_maps, core_ids=list(range(NCORES)))
    out = np.concatenate([res.results[c]["y"] for c in range(NCORES)], axis=0)
    return out.reshape(2, 8192, D)
